# version 19
# baseline (speedup 1.0000x reference)
import contextlib
import math

import numpy as np

import concourse.bass as bass
import concourse.mybir as mybir
from concourse.bass_utils import run_bass_kernel_spmd
import concourse.dve_ops as dve_ops
from concourse.dve_op import DveOp
from concourse.dve_spec import Spec, Src0, Src1, C0, C1, C2, One, eq, lower as dve_lower

F32 = mybir.dt.float32
BF16 = mybir.dt.bfloat16
I32 = mybir.dt.int32
AF = mybir.ActivationFunctionType
ALU = mybir.AluOpType

D = 2048
NH = 16
HD = 128
SEQ = 2048
NB = 4
INTER = 5504
NIC = 16
TOWN = 1024
NSLOT = 8
SCALE = 1.0 / math.sqrt(HD)
EPS = 1.0e-5

A_BLOCKS = [0, 3, 4, 7, 8, 11, 12, 15]
B_BLOCKS = [1, 2, 5, 6, 9, 10, 13, 14]
NT = [1, 2, 3, 4, 5, 6, 7, 8]
SLOT_ORDER = [7, 0, 6, 1, 5, 2, 4, 3]
NSTEP = NH * NSLOT
DEBUG_DUMP = False


def _gate_assign():
    real, g = [], 0
    for r in range(8):
        n = 6 if r < 3 else 5
        real.append(list(range(g, g + n)))
        g += n
    assert g == 43
    padded = [bl + [bl[-1]] * (6 - len(bl)) for bl in real]
    inv = {oc: (r, j) for r, bl in enumerate(real) for j, oc in enumerate(bl)}
    return padded, inv


GATE_PAD, GATE_INV = _gate_assign()
DOWN_INV = {oc: (oc // 2, oc % 2) for oc in range(16)}

GBLK = 128 * 2048
DBLK = 128 * 5504
W_IN_ELEMS = 12 * GBLK + 2 * DBLK
OFF_UP = 6 * GBLK
OFF_DN = 12 * GBLK


def _mk_op(name, body, reference):
    for op in dve_ops.OPS:
        if op.name == name:
            return op
    shas = {}
    for ver in ("v3", "v4"):
        shas[ver] = dve_lower(Spec(body=body), ver=ver, perf_en=False, name=name).sha(ver)
    op = DveOp(name, Spec(body=body), subdim=False, uops_sha=shas, reference=reference)
    dve_ops.OPS.append(op)
    dve_ops._SUB_OPCODE_FOR_NAME[name] = dve_ops._CUSTOM_DVE_ROW_BASE + len(dve_ops.OPS) - 1
    dve_ops.CUSTOM_DVE_SPECS[name] = op.spec
    dve_ops.CUSTOM_DVE_REFS[name] = op.ref
    return op


LUT_INIT2 = _mk_op(
    "LUT_INIT2",
    C0 * eq(Src0, C2) + C1 * eq(Src0, C2 + One),
    lambda in0, in1, s0, s1, imm2: s0 * (in0 == imm2) + s1 * (in0 == imm2 + 1.0),
)
LUT_ACC2 = _mk_op(
    "LUT_ACC2",
    Src1 + C0 * eq(Src0, C2) + C1 * eq(Src0, C2 + One),
    lambda in0, in1, s0, s1, imm2: in1 + s0 * (in0 == imm2) + s1 * (in0 == imm2 + 1.0),
)


_SER_STATE = [0]


def build_program():
    nc = bass.Bass("TRN2", target_bir_lowering=False, detect_race_conditions=False)

    d_xown = nc.dram_tensor("xown", [NIC, 128, TOWN], F32, kind="ExternalInput").ap()
    d_xfull = nc.dram_tensor("xfull", [NIC, 128, SEQ], F32, kind="ExternalInput").ap()
    d_ct = nc.dram_tensor("ctr", [D, NB], F32, kind="ExternalInput").ap()
    d_wmod = nc.dram_tensor("wmods", [D, 1536], F32, kind="ExternalInput").ap()
    d_lut = nc.dram_tensor("lutv", [16], F32, kind="ExternalInput").ap()
    d_w4 = {m: nc.dram_tensor(f"w4{m}", [16, 128, D], F32, kind="ExternalInput").ap()
            for m in "qkvo"}
    d_b2 = {m: nc.dram_tensor(f"b2{m}", [128, 16], F32, kind="ExternalInput").ap()
            for m in "qkvo"}
    d_gidx = nc.dram_tensor("gidx", [6, 128, 2048], I32, kind="ExternalInput").ap()
    d_uidx = nc.dram_tensor("uidx", [6, 128, 2048], I32, kind="ExternalInput").ap()
    d_didx = nc.dram_tensor("didx", [2, 128, 5504], I32, kind="ExternalInput").ap()
    d_sc = nc.dram_tensor("scv", [8], F32, kind="ExternalInput").ap()
    d_eb = nc.dram_tensor("ebv", [NB], F32, kind="ExternalInput").ap()
    d_y = nc.dram_tensor("y", [NIC, 128, TOWN], F32, kind="ExternalOutput").ap()

    d_modin = nc.dram_tensor("modin", [128, 48], F32).ap()
    d_modout = nc.dram_tensor("modout", [8, 128, 48], F32).ap()
    d_win = nc.dram_tensor("win", [W_IN_ELEMS], BF16).ap()
    d_wout = nc.dram_tensor("wout", [8 * W_IN_ELEMS], BF16, addr_space="Shared").ap()
    d_kv = nc.dram_tensor("kvs", [2, NH, 128, SEQ], BF16).ap()
    d_q = nc.dram_tensor("qsc", [NH, 128, TOWN], BF16).ap()
    if DEBUG_DUMP:
        d_dbg_modv = nc.dram_tensor("dbg_modv", [128, 96], F32, kind="ExternalOutput").ap()
        d_dbg_vec = nc.dram_tensor("dbg_vec", [128, 64], F32, kind="ExternalOutput").ap()
        d_dbg_xres = nc.dram_tensor("dbg_xres", [NIC, 128, TOWN], F32, kind="ExternalOutput").ap()
        d_dbg_g = nc.dram_tensor("dbg_g", [43, 128, TOWN], BF16, kind="ExternalOutput").ap()
        d_dbg_w = nc.dram_tensor("dbg_w", [2, 128, 4096], BF16, kind="ExternalOutput").ap()
        d_dbg_h2a = nc.dram_tensor("dbg_h2a", [128, 16384], BF16, kind="ExternalOutput").ap()
        d_dbg_h2b = nc.dram_tensor("dbg_h2b", [128, 16384], BF16, kind="ExternalOutput").ap()
        d_dbg_sgf = nc.dram_tensor("dbg_sgf", [128, TOWN], F32, kind="ExternalOutput").ap()

    def wout_slice(base_off, blk_elems):
        return d_wout[base_off:base_off + blk_elems].rearrange("(p c) -> p c", p=128)

    def wout_gate(oc):
        r, j = GATE_INV[oc]
        return wout_slice(r * W_IN_ELEMS + j * GBLK, GBLK)

    def wout_up(oc):
        r, j = GATE_INV[oc]
        return wout_slice(r * W_IN_ELEMS + OFF_UP + j * GBLK, GBLK)

    def wout_down(oc):
        r, j = DOWN_INV[oc]
        return wout_slice(r * W_IN_ELEMS + OFF_DN + j * DBLK, DBLK)

    ES = contextlib.ExitStack()
    with ES:
        sem = {}

        def S(name):
            if name not in sem:
                sem[name] = ES.enter_context(nc.semaphore(name))
            return sem[name]

        x_res = ES.enter_context(nc.sbuf_tensor("x_res", [128, NIC * TOWN], F32))
        identb = ES.enter_context(nc.sbuf_tensor("identb", [128, 128], BF16))
        identf = ES.enter_context(nc.sbuf_tensor("identf", [128, 128], F32))
        lt = ES.enter_context(nc.sbuf_tensor("lt", [128, 16], F32))
        scb = ES.enter_context(nc.sbuf_tensor("scb", [128, 8], F32))
        ebb = ES.enter_context(nc.sbuf_tensor("ebb", [128, NB], F32))
        bias_t = {m: ES.enter_context(nc.sbuf_tensor(f"bias_{m}", [128, 16], F32))
                  for m in "qkvo"}
        modv = ES.enter_context(nc.sbuf_tensor("modv", [128, 96], F32))
        g1c = ES.enter_context(nc.sbuf_tensor("g1c", [128, 16], F32))
        g2c = ES.enter_context(nc.sbuf_tensor("g2c", [128, 16], F32))
        gac = ES.enter_context(nc.sbuf_tensor("gac", [128, 16], F32))
        gbc = ES.enter_context(nc.sbuf_tensor("gbc", [128, 16], F32))
        onesb = ES.enter_context(nc.sbuf_tensor("onesb", [128, 1], BF16))
        onesf1 = ES.enter_context(nc.sbuf_tensor("onesf1", [1, 128], F32))
        onesf1b = ES.enter_context(nc.sbuf_tensor("onesf1b", [1, 128], BF16))
        epsr = ES.enter_context(nc.sbuf_tensor("epsr", [1, 1], F32))
        scr = ES.enter_context(nc.sbuf_tensor("scr", [128, 16], F32))

        SA = modv[:, 0:16]
        CA = modv[:, 16:32]
        GA = modv[:, 32:48]
        SB_ = modv[:, 48:64]
        CB = modv[:, 64:80]
        GB = modv[:, 80:96]

        conv_tiles = []
        for k in range(6):
            for p2_ in range(2):
                conv_tiles.append((d_gidx[k][:, 1024 * p2_:1024 * (p2_ + 1)], 1024,
                                   k * GBLK, 2048, 1024 * p2_))
        for k in range(6):
            for p2_ in range(2):
                conv_tiles.append((d_uidx[k][:, 1024 * p2_:1024 * (p2_ + 1)], 1024,
                                   OFF_UP + k * GBLK, 2048, 1024 * p2_))
        for k in range(2):
            off = 0
            while off < 5504:
                w = min(1024, 5504 - off)
                conv_tiles.append((d_didx[k][:, off:off + w], w,
                                   OFF_DN + k * DBLK, 5504, off))
                off += w
        NCONV = len(conv_tiles)

        def emit_ln_stats(tag, T, src_spec, LN, blk, stats_waits=(), cast_waits=(),
                          dma_engine=None):
            kind = src_spec[0]
            NJ = T // 512
            PSTAT = contextlib.ExitStack()
            ps_sx = PSTAT.enter_context(nc.psum_tensor(f"{tag}sx", [1, T], F32))
            ps_sxx = PSTAT.enter_context(nc.psum_tensor(f"{tag}sxx", [1, T], F32))
            sxxs = LN.enter_context(nc.sbuf_tensor(f"{tag}sxxs", [1, T], F32))
            mir = LN.enter_context(nc.sbuf_tensor(f"{tag}mir", [1, T], BF16))
            xf = None
            if kind == "dram":
                xf = [LN.enter_context(nc.sbuf_tensor(f"{tag}xf{i}", [128, T], F32)) for i in range(2)]
            xb = [PSTAT.enter_context(nc.sbuf_tensor(f"{tag}xb{i}", [128, T], BF16)) for i in range(2)]
            xq = PSTAT.enter_context(nc.sbuf_tensor(f"{tag}xq", [128, T], BF16))

            def lsrc(ic):
                if kind == "dram":
                    return xf[ic % 2][:, :]
                _, t_sb, stride = src_spec
                return t_sb[:, ic * stride:ic * stride + T]

            if kind == "dram":
                def _dma_pass1(eng):
                    dap = src_spec[1]
                    for t in range(NIC):
                        if t >= 2:
                            eng.wait_ge(S(f"{tag}_vc"), t - 1)
                            eng.wait_ge(S(f"{tag}_ac"), t - 1)
                        eng.dma_start(out=xf[t % 2][:, :],
                                      in_=dap[t]).then_inc(S(f"{tag}_qx"), 16)
                def _dma_pass2(eng):
                    for t in range(NIC):
                        tg = NIC + t
                        if t < 2:
                            eng.wait_ge(S(f"{tag}_vc"), NIC)
                            eng.wait_ge(S(f"{tag}_ac"), NIC)
                        else:
                            eng.wait_ge(S(f"{tag}_ap"), t - 1)
                        eng.dma_start(out=xf[tg % 2][:, :],
                                      in_=src_spec[1][t]).then_inc(S(f"{tag}_qx"), 16)
            else:
                _dma_pass1 = _dma_pass2 = None

            @blk.vector
            def _(vector):
                for sname, v in cast_waits:
                    vector.wait_ge(S(sname), v)
                for ic in range(NIC):
                    if kind == "dram":
                        vector.wait_ge(S(f"{tag}_qx"), 16 * (ic + 1))
                    if ic >= 2:
                        vector.wait_ge(S(f"{tag}_pe"), ic - 1)
                    vector.tensor_copy(out=xb[ic % 2][:, :],
                                       in_=lsrc(ic)).then_inc(S(f"{tag}_vc"), 1)

            @blk.scalar
            def _(scalar):
                for sname, v in cast_waits:
                    scalar.wait_ge(S(sname), v)
                for ic in range(NIC):
                    if kind == "dram":
                        scalar.wait_ge(S(f"{tag}_qx"), 16 * (ic + 1))
                    if ic >= 1:
                        scalar.wait_ge(S(f"{tag}_pe"), ic)
                    scalar.activation(out=xq[:, :], in_=lsrc(ic),
                                      func=AF.Square).then_inc(S(f"{tag}_ac"), 1)

            @blk.tensor
            def _(tensor):
                for sname, v in stats_waits:
                    tensor.wait_ge(S(sname), v)
                for ic in range(NIC):
                    tensor.wait_ge(S(f"{tag}_vc"), ic + 1)
                    tensor.wait_ge(S(f"{tag}_ac"), ic + 1)
                    ins = None
                    for j in range(NJ):
                        tensor.matmul(ps_sx[:, j * 512:(j + 1) * 512], onesb[:, :],
                                      xb[ic % 2][:, j * 512:(j + 1) * 512],
                                      start=(ic == 0), stop=(ic == NIC - 1))
                        ins = tensor.matmul(ps_sxx[:, j * 512:(j + 1) * 512], onesb[:, :],
                                            xq[:, j * 512:(j + 1) * 512],
                                            start=(ic == 0), stop=(ic == NIC - 1))
                    ins.then_inc(S(f"{tag}_pe"), 1)

            @blk.vector
            def _(vector):
                vector.wait_ge(S(f"{tag}_pe"), NIC)
                vector.tensor_scalar(out=ps_sx[:, :], in0=ps_sx[:, :], scalar1=1.0 / D,
                                     scalar2=None, op0=ALU.mult)
                vector.tensor_scalar(out=ps_sxx[:, :], in0=ps_sxx[:, :], scalar1=1.0 / D,
                                     scalar2=None, op0=ALU.mult).then_inc(S(f"{tag}_st"), 1)

            @blk.scalar
            def _(scalar):
                scalar.wait_ge(S(f"{tag}_st"), 1)
                scalar.activation(out=mir[:, :], in_=ps_sx[:, :],
                                  func=AF.Square).then_inc(S(f"{tag}_st"), 1)

            @blk.vector
            def _(vector):
                vector.wait_ge(S(f"{tag}_st"), 2)
                vector.tensor_sub(out=ps_sxx[:, :], in0=ps_sxx[:, :],
                                  in1=mir[:, :]).then_inc(S(f"{tag}_st"), 1)

            @blk.scalar
            def _(scalar):
                scalar.wait_ge(S(f"{tag}_st"), 3)
                scalar.activation(out=sxxs[:, :], in_=ps_sxx[:, :], func=AF.Sqrt,
                                  bias=epsr[:, 0:1]).then_inc(S(f"{tag}_st"), 1)

            @blk.vector
            def _(vector):
                vector.wait_ge(S(f"{tag}_st"), 4)
                vector.reciprocal(out=sxxs[:, :], in_=sxxs[:, :])
                vector.tensor_mul(out=mir[:, :], in0=ps_sx[:, :],
                                  in1=sxxs[:, :]).then_inc(S(f"{tag}_st"), 1)

            PSTAT.close()
            return {"sxxs": sxxs, "mir": mir, "lsrc": lsrc, "kind": kind,
                    "dma1": _dma_pass1, "dma2": _dma_pass2}

        def emit_ln_apply(tag, T, st, h_out, h_stride, gcol, scol, LN, blk, apply_waits=()):
            NJ = T // 512
            APL = contextlib.ExitStack()
            pb_istd = APL.enter_context(nc.psum_tensor(f"{tag}bi", [128, T], F32))
            pb_mi = APL.enter_context(nc.psum_tensor(f"{tag}bm", [128, T], F32))
            tt = APL.enter_context(nc.sbuf_tensor(f"{tag}tt", [128, T], F32))
            sxxs, mir, lsrc, kind = st["sxxs"], st["mir"], st["lsrc"], st["kind"]

            @blk.tensor
            def _(tensor):
                tensor.wait_ge(S(f"{tag}_st"), 5)
                ins = None
                for j in range(NJ):
                    tensor.matmul(pb_istd[:, j * 512:(j + 1) * 512], onesf1[:, :],
                                  sxxs[:, j * 512:(j + 1) * 512], start=True, stop=True)
                    ins = tensor.matmul(pb_mi[:, j * 512:(j + 1) * 512], onesf1b[:, :],
                                        mir[:, j * 512:(j + 1) * 512], start=True, stop=True)
                ins.then_inc(S(f"{tag}_bc"), 1)

            @blk.vector
            def _(vector):
                for sname, v in apply_waits:
                    vector.wait_ge(S(sname), v)
                vector.wait_ge(S(f"{tag}_bc"), 1)
                for ic in range(NIC):
                    if kind == "dram":
                        vector.wait_ge(S(f"{tag}_qx"), 16 * (NIC + ic + 1))
                    vector.tensor_mul(out=tt[:, :], in0=lsrc(ic), in1=pb_istd[:, :])
                    vector.tensor_sub(out=tt[:, :], in0=tt[:, :], in1=pb_mi[:, :])
                    vector.tensor_scalar(
                        out=h_out[:, ic * h_stride:ic * h_stride + T],
                        in0=tt[:, :], scalar1=gcol[:, ic:ic + 1],
                        scalar2=scol[:, ic:ic + 1],
                        op0=ALU.mult, op1=ALU.add).then_inc(S(f"{tag}_ap"), 1)
            return APL

        P12 = contextlib.ExitStack()
        h_kv = P12.enter_context(nc.sbuf_tensor("h_kv", [128, NIC * SEQ], BF16))
        h1own = P12.enter_context(nc.sbuf_tensor("h1own", [128, NIC * TOWN], BF16))

        P0 = contextlib.ExitStack()
        with P0:
            Mall = P0.enter_context(nc.sbuf_tensor("Mall", [128, 384], F32))
            P0M = contextlib.ExitStack()
            ct = P0M.enter_context(nc.sbuf_tensor("ct", [128, 64], F32))
            cs = P0M.enter_context(nc.sbuf_tensor("cs", [128, 64], F32))
            wmf = [P0M.enter_context(nc.sbuf_tensor(f"wmf{i}", [128, 1536], F32)) for i in range(2)]
            modp = P0M.enter_context(nc.sbuf_tensor("modp", [4, 1536], F32))
            PMOD = contextlib.ExitStack()
            pmod = PMOD.enter_context(nc.psum_tensor("pmod", [4, 1536], F32))
            pmt = PMOD.enter_context(nc.psum_tensor("pmt", [128, 48], F32))

            with nc.Block() as blk:
                @blk.sync
                def _(sync):
                    for ic in range(NIC):
                        sync.dma_start(out=x_res[:, ic * TOWN:(ic + 1) * TOWN],
                                       in_=d_xown[ic]).then_inc(S("q_xres"), 16)
                    for ic in range(NIC):
                        sync.dma_start(out=ct[:, ic * 4:(ic + 1) * 4],
                                       in_=d_ct[ic * 128:(ic + 1) * 128, :]).then_inc(S("q_ct"), 16)
                    for m in "qkvo":
                        sync.dma_start(out=bias_t[m][:, :], in_=d_b2[m][:, :]).then_inc(S("q_bias"), 16)
                    for ic in range(NIC):
                        if ic >= 2:
                            sync.wait_ge(S("e_modmm"), ic - 1)
                        sync.dma_start(out=wmf[ic % 2][:, :],
                                       in_=d_wmod[ic * 128:(ic + 1) * 128, :]).then_inc(S("q_wmf"), 16)
                    sync.wait_ge(S("ch0"), 4)
                    sync.dma_start(out=d_modin[:, :], in_=Mall[:, 0:48]).then_inc(S("q_modin"), 16)

                @blk.vector
                def _(vector):
                    _sern = [0]

                    def ser(ins):
                        _sern[0] += 1
                        ins.then_inc(S("e_ser"), 1)
                        vector.wait_ge(S("e_ser"), _sern[0])

                    vector.wait_ge(S("e_iota"), 1)
                    ser(vector.tensor_scalar(out=identf[:, :], in0=identf[:, :], scalar1=0.0,
                                             scalar2=None, op0=ALU.is_equal))
                    vector.tensor_copy(out=identb[:, :], in_=identf[:, :])
                    vector.memset(onesb[:, :], 1.0)
                    vector.memset(epsr[:, :], EPS)
                    vector.memset(onesf1b[:, :], 1.0)
                    vector.memset(onesf1[:, :], 1.0).then_inc(S("e_ident"), 1)
                    vector.wait_ge(S("q_bias"), 64)
                    ser(vector.tensor_scalar(out=bias_t["q"][:, :], in0=bias_t["q"][:, :],
                                             scalar1=SCALE, scalar2=None, op0=ALU.mult))
                    vector.engine_nop().then_inc(S("e_serdone"), 1)
                    _SER_STATE[0] = _sern[0]

                @blk.scalar
                def _(scalar):
                    scalar.wait_ge(S("q_ct"), 256)
                    scalar.activation(out=cs[:, :], in_=ct[:, :], func=AF.Silu).then_inc(S("a_cs"), 1)
                    scalar.wait_ge(S("e_modmm"), NIC)
                    scalar.copy(out=modp[:, :], in_=pmod[:, :]).then_inc(S("ch0"), 1)
                    scalar.wait_ge(S("ch0"), 3)
                    scalar.copy(out=Mall[:, 0:48], in_=pmt[:, :]).then_inc(S("ch0"), 1)

                @blk.tensor
                def _(tensor):
                    tensor.wait_ge(S("a_cs"), 1)
                    for ic in range(NIC):
                        tensor.wait_ge(S("q_wmf"), 16 * (ic + 1))
                        ins = None
                        for j in range(3):
                            ins = tensor.matmul(pmod[:, j * 512:(j + 1) * 512],
                                                cs[:, ic * 4:(ic + 1) * 4],
                                                wmf[ic % 2][:, j * 512:(j + 1) * 512],
                                                start=(ic == 0), stop=(ic == NIC - 1))
                        ins.then_inc(S("e_modmm"), 1)
                    tensor.wait_ge(S("ch0"), 1)
                    tensor.wait_ge(S("e_ident"), 1)
                    ins = None
                    for g in range(12):
                        ins = tensor.transpose(pmt[:, g * 4:(g + 1) * 4],
                                               modp[:, g * 128:(g + 1) * 128],
                                               identf[0:4, 0:4])
                    ins.then_inc(S("ch0"), 2)

                PMOD.close()
                P0M.close()
                LN1 = contextlib.ExitStack()
                stg_lg = emit_ln_stats("lg", SEQ, ("dram", d_xfull), LN1, blk,
                                       stats_waits=(("ch0", 4),),
                                       cast_waits=(("ch0", 4),))

                @blk.gpsimd
                def _(gpsimd):
                    gpsimd.dma_start(out=lt[:, :],
                                     in_=d_lut[None, :].broadcast_to((128, 16))).then_inc(S("q_pre"), 16)
                    gpsimd.dma_start(out=scb[:, :],
                                     in_=d_sc[None, :].broadcast_to((128, 8))).then_inc(S("q_pre"), 16)
                    gpsimd.dma_start(out=ebb[:, :],
                                     in_=d_eb[None, :].broadcast_to((128, NB))).then_inc(S("q_pre"), 16)
                    gpsimd.iota(identf[:, :], pattern=[[1, 128]], base=0, channel_multiplier=-1,
                                allow_small_or_imprecise_dtypes=True).then_inc(S("e_iota"), 1)
                    gpsimd.wait_ge(S("ch0"), 4)
                    stg_lg["dma1"](gpsimd)
                    gpsimd.wait_ge(S("q_modin"), 16)
                    gpsimd.collective_compute(
                        "AllGather", ALU.bypass, replica_groups=[list(range(8))],
                        ins=[d_modin.opt()], outs=[d_modout.opt()]).then_inc(S("c_mod"), 1)
                    gpsimd.wait_ge(S("c_mod"), 1)
                    gpsimd.dma_start(out=Mall[:, :],
                                     in_=d_modout.transpose([1, 0, 2])).then_inc(S("q_modall"), 16)
                    stg_lg["dma2"](gpsimd)

                @blk.vector
                def _(vector):
                    _sern = [_SER_STATE[0]]

                    def ser(ins):
                        _sern[0] += 1
                        ins.then_inc(S("e_ser"), 1)
                        vector.wait_ge(S("e_ser"), _sern[0])

                    vector.wait_ge(S("q_pre"), 48)
                    vector.wait_ge(S("q_modall"), 16)
                    ser(vector.tensor_scalar(out=modv[:, :], in0=Mall[:, 0::4],
                                             scalar1=ebb[:, 0:1], scalar2=None, op0=ALU.mult))
                    for b4 in range(1, NB):
                        ser(vector.scalar_tensor_tensor(out=modv[:, :], in0=Mall[:, b4::4],
                                                        scalar=ebb[:, b4:b4 + 1], in1=modv[:, :],
                                                        op0=ALU.mult, op1=ALU.add))
                    ser(vector.tensor_scalar_add(out=g1c[:, :], in0=CA, scalar1=1.0))
                    ser(vector.tensor_scalar_add(out=g2c[:, :], in0=CB, scalar1=1.0))
                    for vsrc, dst in ((GA, gac), (GB, gbc)):
                        ser(vector.tensor_scalar(out=scr[:, :], in0=vsrc, scalar1=0.0,
                                                 scalar2=2.0, op0=ALU.is_ge, op1=ALU.mult))
                        ser(vector.tensor_scalar_add(out=scr[:, :], in0=scr[:, :], scalar1=-1.0))
                        ser(vector.tensor_mul(out=dst[:, :], in0=vsrc, in1=scr[:, :]))
                        ser(vector.tensor_scalar_max(out=dst[:, :], in0=dst[:, :], scalar1=0.25))
                        ser(vector.tensor_mul(out=dst[:, :], in0=dst[:, :], in1=scr[:, :]))
                    vector.engine_nop().then_inc(S("e_mod"), 1)

                emit_ln_apply("lg", SEQ, stg_lg, h_kv, SEQ, g1c, modv[:, 0:16], LN1, blk,
                              apply_waits=(("e_mod", 1),)).close()
                stg_lo = emit_ln_stats("lo", TOWN, ("sbuf", x_res, TOWN), LN1, blk,
                                       stats_waits=(("lg_ap", NIC),),
                                       cast_waits=(("q_xres", 256), ("lg_ap", NIC)))
                emit_ln_apply("lo", TOWN, stg_lo, h1own, TOWN, g1c, modv[:, 0:16], LN1, blk,
                              apply_waits=(("e_mod", 1),)).close()
            LN1.close()

        P2 = contextlib.ExitStack()
        with P2:
            wf = [P2.enter_context(nc.sbuf_tensor(f"wf{i}", [128, 2048], F32)) for i in range(2)]
            wb = [P2.enter_context(nc.sbuf_tensor(f"wbq{i}", [128, 2048], BF16)) for i in range(2)]
            stg = [P2.enter_context(nc.sbuf_tensor(f"stg{i}", [128, 2048], BF16)) for i in range(2)]
            cvi = [P2.enter_context(nc.sbuf_tensor(f"cvi{i}", [128, 1024], I32)) for i in range(2)]
            cvo = [P2.enter_context(nc.sbuf_tensor(f"cvo{i}", [128, 1024], BF16)) for i in range(2)]
            pqk = [P2.enter_context(nc.psum_tensor(f"pqk{i}", [128, 2048], F32)) for i in range(2)]

            jobs = [("k", oc) for oc in range(NH)] + \
                   [("v", oc) for oc in range(NH)] + \
                   [("q", oc) for oc in range(NH)]
            NJOB = len(jobs)
            conv_at = {}
            for k in range(NCONV):
                conv_at.setdefault(k * NJOB // NCONV, []).append(k)

            with nc.Block() as blk:
                @blk.sync
                def _(sync):
                    for t, (m, oc) in enumerate(jobs):
                        if t >= 2:
                            sync.wait_ge(S("p2_cast"), t - 1)
                        sync.dma_start(out=wf[t % 2][:, :], in_=d_w4[m][oc]).then_inc(S("p2_qw"), 16)

                @blk.gpsimd
                def _(gpsimd):
                    def _cstage(k):
                        gpsimd.wait_ge(S("e_conv"), k + 1)
                        _, w, off, bw, coff = conv_tiles[k]
                        blkview = d_win[off:off + 128 * bw].rearrange("(p c) -> p c", p=128)
                        gpsimd.dma_start(out=blkview[:, coff:coff + w],
                                         in_=cvo[k % 2][:, 0:w]).then_inc(S("q_cvout"), 16)

                    for t, (m, oc) in enumerate(jobs):
                        for k in conv_at.get(t, []):
                            if k >= 2:
                                gpsimd.wait_ge(S("e_conv"), k - 1)
                            w = conv_tiles[k][1]
                            gpsimd.dma_start(out=cvi[k % 2][:, 0:w],
                                             in_=conv_tiles[k][0]).then_inc(S("q_cvin"), 16)
                            if k >= 2:
                                _cstage(k - 2)
                        gpsimd.wait_ge(S("p2_ev"), t + 1)
                        if m == "k":
                            dst, T = d_kv[0, oc], SEQ
                        elif m == "v":
                            dst, T = d_kv[1, oc], SEQ
                        else:
                            dst, T = d_q[oc], TOWN
                        gpsimd.dma_start(out=dst, in_=stg[t % 2][:, 0:T]).then_inc(S("p2_st"), 16)
                    _cstage(NCONV - 2)
                    _cstage(NCONV - 1)
                    gpsimd.wait_ge(S("p2_st"), 16 * len(jobs))
                    gpsimd.wait_ge(S("q_cvout"), 16 * NCONV)

                def _p2_evac(scalar, t):
                    m, oc = jobs[t]
                    scalar.wait_ge(S("p2_pe"), t + 1)
                    if t >= 2:
                        scalar.wait_ge(S("p2_st"), 16 * (t - 1))
                    T = TOWN if m == "q" else SEQ
                    sc_in = SCALE if m == "q" else 1.0
                    scalar.activation(out=stg[t % 2][:, 0:T], in_=pqk[t % 2][:, 0:T],
                                      func=AF.Identity, scale=sc_in,
                                      bias=bias_t[m][:, oc:oc + 1]).then_inc(S("p2_ev"), 1)

                @blk.scalar
                def _(scalar):
                    for t, (m, oc) in enumerate(jobs):
                        scalar.wait_ge(S("p2_qw"), 16 * (t + 1))
                        if t >= 2:
                            scalar.wait_ge(S("p2_pe"), t - 1)
                        scalar.copy(out=wb[t % 2][:, :],
                                    in_=wf[t % 2][:, :]).then_inc(S("p2_cast"), 1)
                        if t >= 2:
                            _p2_evac(scalar, t - 2)
                    _p2_evac(scalar, len(jobs) - 2)
                    _p2_evac(scalar, len(jobs) - 1)

                @blk.vector
                def _(vector):
                    vector.wait_ge(S("q_pre"), 48)
                    for t, (m, oc) in enumerate(jobs):
                        for k in conv_at.get(t, []):
                            w = conv_tiles[k][1]
                            vector.wait_ge(S("q_cvin"), 16 * (k + 1))
                            if k >= 2:
                                vector.wait_ge(S("q_cvout"), 16 * (k - 1))
                            o = cvo[k % 2][:, 0:w]
                            i = cvi[k % 2][:, 0:w]
                            vector._custom_dve(LUT_INIT2, out=o, in0=i,
                                               s0=lt[:, 0:1], s1=lt[:, 1:2], imm2=0.0)
                            ins = None
                            for kk in range(1, 8):
                                ins = vector._custom_dve(
                                    LUT_ACC2, out=o, in0=i, in1=o,
                                    s0=lt[:, 2 * kk:2 * kk + 1],
                                    s1=lt[:, 2 * kk + 1:2 * kk + 2],
                                    imm2=float(2 * kk))
                            ins.then_inc(S("e_conv"), 1)

                @blk.tensor
                def _(tensor):
                    for t, (m, oc) in enumerate(jobs):
                        tensor.wait_ge(S("p2_cast"), t + 1)
                        if t >= 2:
                            tensor.wait_ge(S("p2_ev"), t - 1)
                        if m == "q":
                            hsrc, hs, T = h1own, TOWN, TOWN
                        else:
                            hsrc, hs, T = h_kv, SEQ, SEQ
                        ins = None
                        for ic in range(NIC):
                            for j in range(T // 512):
                                ins = tensor.matmul(
                                    pqk[t % 2][:, j * 512:(j + 1) * 512],
                                    wb[t % 2][:, ic * 128:(ic + 1) * 128],
                                    hsrc[:, ic * hs + j * 512:ic * hs + (j + 1) * 512],
                                    start=(ic == 0), stop=(ic == NIC - 1))
                        ins.then_inc(S("p2_pe"), 1)

        P12.close()

        steps = []
        for h in range(NH):
            for k, s in enumerate(SLOT_ORDER):
                steps.append((h, s))
        NTl = [NT[s] for (_, s) in steps]
        NCH = [2 * n for n in NTl]
        NGR = [(c + 7) // 8 for c in NCH]

        LAG = 3
        ptp_order = []
        for h in range(NH):
            ptp_order.append(("v", h, 0))
            ptp_order.append(("v", h, 1))
            for k in range(NSLOT):
                t = h * NSLOT + k
                if t >= LAG:
                    for g in range(NGR[t - LAG]):
                        ptp_order.append(("t", t - LAG, g))
        for t in range(NSTEP - LAG, NSTEP):
            for g in range(NGR[t]):
                ptp_order.append(("t", t, g))
        ptp_pos = {key: i for i, key in enumerate(ptp_order)}
        trav_done = {}
        vtr_done = {}
        for i, key in enumerate(ptp_order):
            if key[0] == "t":
                trav_done[key[1]] = i + 1
            else:
                vtr_done[key[1]] = i + 1

        P3 = contextlib.ExitStack()
        with P3:
            oT = P3.enter_context(nc.sbuf_tensor("oT", [128, NIC * TOWN], BF16))
            AT = contextlib.ExitStack()
            with AT:
                jg = AT.enter_context(nc.sbuf_tensor("jg", [128, SEQ], F32))
                o_sb = AT.enter_context(nc.sbuf_tensor("o_sb", [128, NSLOT * D], BF16))
                kh = [AT.enter_context(nc.sbuf_tensor(f"kh{i}", [128, SEQ], BF16)) for i in range(2)]
                vh = [AT.enter_context(nc.sbuf_tensor(f"vh{i}", [128, SEQ], BF16)) for i in range(2)]
                vt = [AT.enter_context(nc.sbuf_tensor(f"vt{i}", [128, SEQ], BF16)) for i in range(2)]
                qh = [AT.enter_context(nc.sbuf_tensor(f"qh{i}", [128, TOWN], BF16)) for i in range(2)]
                ex = [AT.enter_context(nc.sbuf_tensor(f"ex{i}", [128, 2048], BF16)) for i in range(3)]
                pbf = [AT.enter_context(nc.sbuf_tensor(f"pbf{i}", [128, 2048], BF16)) for i in range(3)]
                pts = [AT.enter_context(nc.sbuf_tensor(f"pts{i}", [128, 2048], BF16)) for i in range(3)]
                mx = AT.enter_context(nc.sbuf_tensor("mx", [128, 4], F32))
                ssum = AT.enter_context(nc.sbuf_tensor("ssum", [128, 4], F32))
                rec = AT.enter_context(nc.sbuf_tensor("rec", [128, 8], F32))
                obuf = AT.enter_context(nc.sbuf_tensor("obuf", [128, 512], F32))
                SCL = AT.enter_context(nc.psum_tensor("SCL", [128, 2048], F32))
                SCS = AT.enter_context(nc.psum_tensor("SCS", [128, 1024], F32))
                ptp = AT.enter_context(nc.psum_tensor("ptp", [128, 1024], BF16))
                ov = AT.enter_context(nc.psum_tensor("ov", [128, 128], F32))

                _fl = {"d": 0, "a": 0}

                def flush_tick(eng, key, sername, ins, sem_name, k=1):
                    _fl[key] += 1
                    ins.then_inc(S(sername), 1)
                    eng.wait_ge(S(sername), _fl[key])
                    eng.nop(nofuse=True).then_inc(S(sem_name), k)

                def lane(t):
                    return SCL if (t % 2 == 0) else SCS

                def emit_trav(tensor, t):
                    h, s = steps[t]
                    tensor.wait_ge(S("p3_st"), t + 1)
                    for g in range(NGR[t]):
                        pos = ptp_pos[("t", t, g)]
                        tensor.wait_ge(S("p3_pte"), pos)
                        ins = None
                        for kk in range(min(8, NCH[t] - g * 8)):
                            kc = g * 8 + kk
                            ins = tensor.transpose(ptp[:, kk * 128:(kk + 1) * 128],
                                                   pbf[t % 3][:, kc * 128:(kc + 1) * 128],
                                                   identb[:, :])
                        ins.then_inc(S("p3_ptw"), 1)
                    tensor.wait_ge(S("p3_pte"), trav_done[t])
                    if t >= 1:
                        tensor.wait_ge(S("a_ovc"), t)
                    ovl = ov[:, :]
                    ins = None
                    for kc in range(NCH[t]):
                        ins = tensor.matmul(ovl,
                                            pts[t % 3][:, kc * 128:(kc + 1) * 128],
                                            vt[h % 2][:, kc * 128:(kc + 1) * 128],
                                            start=(kc == 0), stop=(kc == NCH[t] - 1))
                    ins.then_inc(S("p3_av"), 1)

                def emit_stt(vector, t):
                    h, s = steps[t]
                    W = NT[s] * 256
                    if t == 0:
                        vector.wait_ge(S("e_jg"), 1)
                    vector.wait_ge(S("p3_ex"), t + 1)
                    if t >= LAG:
                        vector.wait_ge(S("p3_ptw"), trav_done[t - LAG])
                    vector.scalar_tensor_tensor(
                        out=pbf[t % 3][:, 0:W], in0=jg[:, 0:W],
                        scalar=scb[:, s:s + 1], in1=ex[t % 3][:, 0:W],
                        op0=ALU.is_le, op1=ALU.mult,
                        accum_out=ssum[:, t % 4:t % 4 + 1]).then_inc(S("p3_st"), 1)

                def emit_rec(vector, t):
                    vector.wait_ge(S("p3_st"), t + 1)
                    vector.reciprocal(out=rec[:, t % 8:t % 8 + 1],
                                      in_=ssum[:, t % 4:t % 4 + 1]).then_inc(S("p3_rc"), 1)

                def emit_oev(vector, t):
                    h, s = steps[t]
                    vector.wait_ge(S("a_ovc"), t + 1)
                    vector.wait_ge(S("p3_rc"), t + 1)
                    vector.tensor_scalar(
                        out=o_sb[:, s * D + h * 128:s * D + (h + 1) * 128],
                        in0=obuf[:, (t % 4) * 128:(t % 4) * 128 + 128],
                        scalar1=rec[:, t % 8:t % 8 + 1],
                        scalar2=None, op0=ALU.mult).then_inc(S("p3_oe"), 1)

                def emit_pte(scalar, t):
                    for g in range(NGR[t]):
                        pos = ptp_pos[("t", t, g)]
                        scalar.wait_ge(S("p3_ptw"), pos + 1)
                        if t >= LAG:
                            scalar.wait_ge(S("p3_av"), t - LAG + 1)
                        w = min(1024, NCH[t] * 128 - g * 1024)
                        flush_tick(scalar, "a", "aser",
                                   scalar.copy(out=pts[t % 3][:, g * 1024:g * 1024 + w],
                                               in_=ptp[:, 0:w]), "p3_pte")

                with nc.Block() as blk:
                    @blk.gpsimd
                    def _(gpsimd):
                        gpsimd.iota(jg[:, :], pattern=[[1, SEQ]], base=0, channel_multiplier=-1,
                                    allow_small_or_imprecise_dtypes=True).then_inc(S("e_jg"), 1)
                        for h in range(NH):
                            if h >= 2:
                                gpsimd.wait_ge(S("p3_av"), (h - 1) * NSLOT)
                            gpsimd.dma_start(out=kh[h % 2][:, :], in_=d_kv[0, h]).then_inc(S("p3_kv"), 16)
                            gpsimd.dma_start(out=vh[h % 2][:, :], in_=d_kv[1, h]).then_inc(S("p3_kv"), 16)
                            gpsimd.dma_start(out=qh[h % 2][:, :], in_=d_q[h]).then_inc(S("p3_kv"), 16)
                            if h == 6:
                                gpsimd.wait_ge(S("q_cvout"), 16 * NCONV)
                                gpsimd.collective_compute(
                                    "AllGather", ALU.bypass, replica_groups=[list(range(8))],
                                    ins=[d_win.opt()], outs=[d_wout.opt()]).then_inc(S("c_w"), 1)

                    @blk.tensor
                    def _(tensor):
                        for h in range(NH):
                            tensor.wait_ge(S("p3_kv"), 48 * (h + 1))
                            for g in range(2):
                                pos = ptp_pos[("v", h, g)]
                                if pos >= 1:
                                    tensor.wait_ge(S("p3_pte"), pos)
                                ins = None
                                for kk in range(8):
                                    kc = g * 8 + kk
                                    ins = tensor.transpose(ptp[:, kk * 128:(kk + 1) * 128],
                                                           vh[h % 2][:, kc * 128:(kc + 1) * 128],
                                                           identb[:, :])
                                ins.then_inc(S("p3_ptw"), 1)
                            for k in range(NSLOT):
                                t = h * NSLOT + k
                                s = steps[t][1]
                                SC = lane(t)
                                if t >= 2:
                                    tensor.wait_ge(S("p3_ex"), t - 1)
                                ins = None
                                for kt in range(NT[s]):
                                    ins = tensor.matmul(SC[:, kt * 256:(kt + 1) * 256],
                                                        qh[h % 2][:, s * 128:(s + 1) * 128],
                                                        kh[h % 2][:, kt * 256:(kt + 1) * 256],
                                                        start=True, stop=True)
                                ins.then_inc(S("p3_sc"), 1)
                                if t >= LAG:
                                    emit_trav(tensor, t - LAG)
                        for tl in range(NSTEP - LAG, NSTEP):
                            emit_trav(tensor, tl)

                    @blk.vector
                    def _(vector):
                        for t in range(NSTEP):
                            h, s = steps[t]
                            SC = lane(t)
                            W = NT[s] * 256
                            vector.wait_ge(S("p3_sc"), t + 1)
                            if t >= 4:
                                vector.wait_ge(S("p3_ex"), t - 3)
                            flush_tick(vector, "d", "dser",
                                       vector.reduce_max(mx[:, t % 4:t % 4 + 1], SC[:, 0:W],
                                                         axis=mybir.AxisListType.X,
                                                         negate=True), "p3_mb")
                            if t >= 3:
                                emit_oev(vector, t - 3)
                            if t >= 1:
                                emit_stt(vector, t - 1)
                                emit_rec(vector, t - 1)
                        emit_stt(vector, NSTEP - 1)
                        emit_rec(vector, NSTEP - 1)
                        for tl in range(NSTEP - 3, NSTEP):
                            emit_oev(vector, tl)

                    @blk.scalar
                    def _(scalar):
                        for h in range(NH):
                            if h >= 2:
                                scalar.wait_ge(S("p3_av"), (h - 1) * NSLOT)
                            for g in range(2):
                                pos = ptp_pos[("v", h, g)]
                                scalar.wait_ge(S("p3_ptw"), pos + 1)
                                flush_tick(scalar, "a", "aser",
                                           scalar.copy(out=vt[h % 2][:, g * 1024:(g + 1) * 1024],
                                                       in_=ptp[:, :]), "p3_pte")
                            for k in range(NSLOT):
                                t = h * NSLOT + k
                                s = steps[t][1]
                                W = NT[s] * 256
                                SC = lane(t)
                                if t >= LAG:
                                    emit_pte(scalar, t - LAG)
                                if t >= 3:
                                    tov = t - 3
                                    scalar.wait_ge(S("p3_av"), tov + 1)
                                    if tov >= 4:
                                        scalar.wait_ge(S("p3_oe"), tov - 3)
                                    flush_tick(scalar, "a", "aser",
                                               scalar.copy(out=obuf[:, (tov % 4) * 128:(tov % 4) * 128 + 128],
                                                           in_=ov[:, :]), "a_ovc")
                                scalar.wait_ge(S("p3_mb"), t + 1)
                                if t >= 3:
                                    scalar.wait_ge(S("p3_st"), t - 2)
                                flush_tick(scalar, "a", "aser",
                                           scalar.activation(out=ex[t % 3][:, 0:W], in_=SC[:, 0:W],
                                                             func=AF.Exp,
                                                             bias=mx[:, t % 4:t % 4 + 1],
                                                             scale=1.0), "p3_ex")
                        for tl in range(NSTEP - LAG, NSTEP):
                            emit_pte(scalar, tl)
                            tov = tl
                            scalar.wait_ge(S("p3_av"), tov + 1)
                            flush_tick(scalar, "a", "aser",
                                       scalar.copy(out=obuf[:, (tov % 4) * 128:(tov % 4) * 128 + 128],
                                                   in_=ov[:, :]), "a_ovc")

            OP = contextlib.ExitStack()
            with OP:
                wof = OP.enter_context(nc.sbuf_tensor("wof", [128, 2048], F32))
                wob = [OP.enter_context(nc.sbuf_tensor(f"wob{i}", [128, 2048], BF16)) for i in range(2)]
                atmp = [OP.enter_context(nc.sbuf_tensor(f"atmp{i}", [128, TOWN], F32)) for i in range(2)]
                pot = OP.enter_context(nc.psum_tensor("pot", [128, 1024], BF16))
                po = [OP.enter_context(nc.psum_tensor(f"po{i}", [128, TOWN], F32)) for i in range(2)]

                with nc.Block() as blk:
                    @blk.sync
                    def _(sync):
                        for oc in range(NIC):
                            if oc >= 1:
                                sync.wait_ge(S("p3o_cast"), oc)
                            sync.dma_start(out=wof[:, :], in_=d_w4["o"][oc]).then_inc(S("p3o_qw"), 16)

                    @blk.gpsimd
                    def _(gpsimd):
                        for oc in range(NIC):
                            gpsimd.wait_ge(S("p3o_qw"), 16 * (oc + 1))
                            if oc >= 2:
                                gpsimd.wait_ge(S("p3o_pe"), oc - 1)
                            gpsimd.tensor_copy(out=wob[oc % 2][:, :],
                                               in_=wof[:, :]).then_inc(S("p3o_cast"), 1)

                    @blk.tensor
                    def _(tensor):
                        for ic in range(NIC):
                            if ic >= 1:
                                tensor.wait_ge(S("p3o_te"), ic)
                            ins = None
                            for s in range(NSLOT):
                                ins = tensor.transpose(pot[:, s * 128:(s + 1) * 128],
                                                       o_sb[:, s * D + ic * 128:s * D + (ic + 1) * 128],
                                                       identb[:, :])
                            ins.then_inc(S("p3o_tr"), 1)

                    @blk.scalar
                    def _(scalar):
                        for ic in range(NIC):
                            scalar.wait_ge(S("p3o_tr"), ic + 1)
                            scalar.copy(out=oT[:, ic * TOWN:(ic + 1) * TOWN],
                                        in_=pot[:, :]).then_inc(S("p3o_te"), 1)

                    @blk.tensor
                    def _(tensor):
                        tensor.wait_ge(S("p3o_te"), NIC)
                        for oc in range(NIC):
                            tensor.wait_ge(S("p3o_cast"), oc + 1)
                            if oc >= 2:
                                tensor.wait_ge(S("p3o_ev"), oc - 1)
                            ins = None
                            for ic in range(NIC):
                                for j in range(TOWN // 512):
                                    ins = tensor.matmul(
                                        po[oc % 2][:, j * 512:(j + 1) * 512],
                                        wob[oc % 2][:, ic * 128:(ic + 1) * 128],
                                        oT[:, ic * TOWN + j * 512:ic * TOWN + (j + 1) * 512],
                                        start=(ic == 0), stop=(ic == NIC - 1))
                            ins.then_inc(S("p3o_pe"), 1)

                    @blk.scalar
                    def _(scalar):
                        for oc in range(NIC):
                            scalar.wait_ge(S("p3o_pe"), oc + 1)
                            if oc >= 2:
                                scalar.wait_ge(S("p3o_res"), oc - 1)
                            scalar.activation(out=atmp[oc % 2][:, :], in_=po[oc % 2][:, :],
                                              func=AF.Identity,
                                              bias=bias_t["o"][:, oc:oc + 1]).then_inc(S("p3o_ev"), 1)

                    @blk.vector
                    def _(vector):
                        for oc in range(NIC):
                            vector.wait_ge(S("p3o_ev"), oc + 1)
                            vector.scalar_tensor_tensor(
                                out=x_res[:, oc * TOWN:(oc + 1) * TOWN],
                                in0=atmp[oc % 2][:, :], scalar=gac[:, oc:oc + 1],
                                in1=x_res[:, oc * TOWN:(oc + 1) * TOWN],
                                op0=ALU.mult, op1=ALU.add).then_inc(S("p3o_res"), 1)

        FG = contextlib.ExitStack()
        with FG:
            g_sb = FG.enter_context(nc.sbuf_tensor("g_sb", [128, 43 * TOWN], BF16))
            FH = contextlib.ExitStack()
            with FH:
                h2 = FH.enter_context(nc.sbuf_tensor("h2", [128, NIC * TOWN], BF16))
                LN2 = contextlib.ExitStack()
                with LN2:
                    with nc.Block() as blk2:
                        stg_l2 = emit_ln_stats("l2", TOWN, ("sbuf", x_res, TOWN), LN2, blk2)
                        emit_ln_apply("l2", TOWN, stg_l2, h2, TOWN, g2c,
                                      modv[:, 48:64], LN2, blk2).close()

                GU = contextlib.ExitStack()
                with GU:
                    wgb = [GU.enter_context(nc.sbuf_tensor(f"wgb{i}", [128, 2048], BF16)) for i in range(2)]
                    wub = [GU.enter_context(nc.sbuf_tensor(f"wub{i}", [128, 2048], BF16)) for i in range(2)]
                    sgf = GU.enter_context(nc.sbuf_tensor("sgf", [128, TOWN], F32))
                    pg = [GU.enter_context(nc.psum_tensor(f"pg{i}", [128, TOWN], F32)) for i in range(2)]
                    pu = [GU.enter_context(nc.psum_tensor(f"pu{i}", [128, TOWN], F32)) for i in range(2)]

                    with nc.Block() as blk:
                        @blk.sync
                        def _(sync):
                            sync.wait_ge(S("c_w"), 1)
                            for oc in range(43):
                                if oc >= 2:
                                    sync.wait_ge(S("p5_pe"), oc - 1)
                                if DEBUG_DUMP and oc == 2:
                                    sync.wait_ge(S("q_dbw"), 64)
                                sync.dma_start(out=wgb[oc % 2][:, :],
                                               in_=wout_gate(oc)).then_inc(S("p5_qw"), 16)
                                sync.dma_start(out=wub[oc % 2][:, :],
                                               in_=wout_up(oc)).then_inc(S("p5_qw"), 16)

                        if DEBUG_DUMP:
                            @blk.gpsimd
                            def _(gpsimd):
                                gpsimd.wait_ge(S("p5_qw"), 32)
                                gpsimd.dma_start(out=d_dbg_w[0][:, 0:2048],
                                                 in_=wgb[0][:, :]).then_inc(S("q_dbw"), 16)
                                gpsimd.dma_start(out=d_dbg_w[0][:, 2048:4096],
                                                 in_=wub[0][:, :]).then_inc(S("q_dbw"), 16)
                                gpsimd.dma_start(out=d_dbg_w[1][:, 0:2048],
                                                 in_=wout_gate(0)).then_inc(S("q_dbw"), 16)
                                gpsimd.dma_start(out=d_dbg_w[1][:, 2048:4096],
                                                 in_=wout_up(0)).then_inc(S("q_dbw"), 16)
                                gpsimd.dma_start(out=d_dbg_h2a[:, :],
                                                 in_=h2[:, :]).then_inc(S("q_dbh"), 16)
                                gpsimd.wait_ge(S("p5_si"), 1)
                                gpsimd.dma_start(out=d_dbg_sgf[:, :],
                                                 in_=sgf[:, :]).then_inc(S("q_dbs"), 16)
                                gpsimd.wait_ge(S("p5_pe"), 6)
                                gpsimd.dma_start(out=d_dbg_h2b[:, :],
                                                 in_=h2[:, :]).then_inc(S("q_dbh"), 16)
                                gpsimd.wait_ge(S("q_dbw"), 64)

                        @blk.tensor
                        def _(tensor):
                            tensor.wait_ge(S("p5_qw"), 32)
                            for ic in range(NIC):
                                for j in range(TOWN // 512):
                                    sl = slice(j * 512, (j + 1) * 512)
                                    hsl = h2[:, ic * TOWN + j * 512:ic * TOWN + (j + 1) * 512]
                                    tensor.matmul(pg[0][:, sl],
                                                  wgb[0][:, ic * 128:(ic + 1) * 128], hsl,
                                                  start=(ic == 0), stop=(ic == NIC - 1))
                                    tensor.matmul(pu[0][:, sl],
                                                  wub[0][:, ic * 128:(ic + 1) * 128], hsl,
                                                  start=(ic == 0), stop=(ic == NIC - 1))
                            for oc in range(43):
                                tensor.wait_ge(S("p5_qw"), 32 * (oc + 1))
                                if oc >= 2:
                                    tensor.wait_ge(S("p5_gm"), oc - 1)
                                ins = None
                                for ic in range(NIC):
                                    for j in range(TOWN // 512):
                                        sl = slice(j * 512, (j + 1) * 512)
                                        hsl = h2[:, ic * TOWN + j * 512:ic * TOWN + (j + 1) * 512]
                                        tensor.matmul(pg[oc % 2][:, sl],
                                                      wgb[oc % 2][:, ic * 128:(ic + 1) * 128], hsl,
                                                      start=(ic == 0), stop=(ic == NIC - 1))
                                        ins = tensor.matmul(pu[oc % 2][:, sl],
                                                            wub[oc % 2][:, ic * 128:(ic + 1) * 128], hsl,
                                                            start=(ic == 0), stop=(ic == NIC - 1))
                                ins.then_inc(S("p5_pe"), 1)

                        @blk.scalar
                        def _(scalar):
                            for oc in range(43):
                                scalar.wait_ge(S("p5_pe"), oc + 1)
                                if DEBUG_DUMP and oc == 1:
                                    scalar.wait_ge(S("q_dbs"), 16)
                                if oc >= 2:
                                    scalar.wait_ge(S("p5_gm"), oc - 1)
                                if oc >= 1:
                                    scalar.wait_ge(S("p5_gm"), oc)
                                scalar.activation(out=sgf[:, :], in_=pg[oc % 2][:, :],
                                                  func=AF.Silu).then_inc(S("p5_si"), 1)

                        @blk.vector
                        def _(vector):
                            for oc in range(43):
                                vector.wait_ge(S("p5_si"), oc + 1)
                                vector.tensor_mul(out=g_sb[:, oc * TOWN:(oc + 1) * TOWN],
                                                  in0=sgf[:, :],
                                                  in1=pu[oc % 2][:, :]).then_inc(S("p5_gm"), 1)

            DN = contextlib.ExitStack()
            with DN:
                wdb = [DN.enter_context(nc.sbuf_tensor(f"wdb{i}", [128, 5504], BF16)) for i in range(2)]
                ysb = [DN.enter_context(nc.sbuf_tensor(f"ysb{i}", [128, TOWN], F32)) for i in range(2)]
                pd = [DN.enter_context(nc.psum_tensor(f"pd{i}", [128, TOWN], F32)) for i in range(2)]

                with nc.Block() as blk:
                    @blk.sync
                    def _(sync):
                        for oc in range(NIC):
                            if oc >= 2:
                                sync.wait_ge(S("p6_pe"), oc - 1)
                            sync.dma_start(out=wdb[oc % 2][:, :],
                                           in_=wout_down(oc)).then_inc(S("p6_qw"), 16)

                    @blk.tensor
                    def _(tensor):
                        for oc in range(NIC):
                            tensor.wait_ge(S("p6_qw"), 16 * (oc + 1))
                            if oc >= 2:
                                tensor.wait_ge(S("p6_ye"), oc - 1)
                            ins = None
                            for ic in range(43):
                                for j in range(TOWN // 512):
                                    ins = tensor.matmul(
                                        pd[oc % 2][:, j * 512:(j + 1) * 512],
                                        wdb[oc % 2][:, ic * 128:(ic + 1) * 128],
                                        g_sb[:, ic * TOWN + j * 512:ic * TOWN + (j + 1) * 512],
                                        start=(ic == 0), stop=(ic == 42))
                            ins.then_inc(S("p6_pe"), 1)

                    @blk.vector
                    def _(vector):
                        for oc in range(NIC):
                            vector.wait_ge(S("p6_pe"), oc + 1)
                            if oc >= 2:
                                vector.wait_ge(S("p6_qy"), 16 * (oc - 1))
                            vector.scalar_tensor_tensor(
                                out=ysb[oc % 2][:, :], in0=pd[oc % 2][:, :],
                                scalar=gbc[:, oc:oc + 1],
                                in1=x_res[:, oc * TOWN:(oc + 1) * TOWN],
                                op0=ALU.mult, op1=ALU.add).then_inc(S("p6_ye"), 1)

                    @blk.gpsimd
                    def _(gpsimd):
                        for oc in range(NIC):
                            gpsimd.wait_ge(S("p6_ye"), oc + 1)
                            gpsimd.dma_start(out=d_y[oc],
                                             in_=ysb[oc % 2][:, :]).then_inc(S("p6_qy"), 16)
                        gpsimd.wait_ge(S("p6_qy"), 16 * NIC)
                        if DEBUG_DUMP:
                            gpsimd.dma_start(out=d_dbg_modv[:, :], in_=modv[:, :]).then_inc(S("q_dbg"), 16)
                            gpsimd.dma_start(out=d_dbg_vec[:, 0:16], in_=g1c[:, :]).then_inc(S("q_dbg"), 16)
                            gpsimd.dma_start(out=d_dbg_vec[:, 16:32], in_=g2c[:, :]).then_inc(S("q_dbg"), 16)
                            gpsimd.dma_start(out=d_dbg_vec[:, 32:48], in_=gac[:, :]).then_inc(S("q_dbg"), 16)
                            gpsimd.dma_start(out=d_dbg_vec[:, 48:64], in_=gbc[:, :]).then_inc(S("q_dbg"), 16)
                            for ic in range(NIC):
                                gpsimd.dma_start(out=d_dbg_xres[ic],
                                                 in_=x_res[:, ic * TOWN:(ic + 1) * TOWN]).then_inc(S("q_dbg"), 16)
                            for ocg in range(43):
                                gpsimd.dma_start(out=d_dbg_g[ocg],
                                                 in_=g_sb[:, ocg * TOWN:(ocg + 1) * TOWN]).then_inc(S("q_dbg"), 16)
                            gpsimd.wait_ge(S("q_dbg"), 16 * (5 + NIC + 43))

    for i in nc.all_instructions():
        if isinstance(i, mybir.InstCustomDveAnt) and len(i.instr) == 0:
            mybir.codegen_inst_isa_one(i, nc, nc.isa)
    return nc


def _prep_shared(inputs):
    f32 = np.float32
    sh = {}
    for m, wkey in (("q", "L0_attn_Wq"), ("k", "L0_attn_Wk"),
                    ("v", "L0_attn_Wv"), ("o", "L0_attn_Wo")):
        W = np.asarray(inputs[wkey], dtype=f32)
        sh[f"w4{m}"] = np.ascontiguousarray(
            W.reshape(16, 128, 16, 128).transpose(0, 3, 2, 1).reshape(16, 128, 2048))
    for m, bkey in (("q", "L0_attn_bq"), ("k", "L0_attn_bk"),
                    ("v", "L0_attn_bv"), ("o", "L0_attn_bo")):
        bvec = np.asarray(inputs[bkey], dtype=f32)
        sh[f"b2{m}"] = np.ascontiguousarray(bvec.reshape(16, 128).T)
    sh["ctr"] = np.ascontiguousarray(np.asarray(inputs["L0_c"], dtype=f32).T)
    sh["lutv"] = np.ascontiguousarray(np.asarray(inputs["L0_ffn_lut"], dtype=f32))
    gidx = np.asarray(inputs["L0_ffn_gate_idx"], dtype=np.int32)
    uidx = np.asarray(inputs["L0_ffn_up_idx"], dtype=np.int32)
    didx = np.asarray(inputs["L0_ffn_down_idx"], dtype=np.int32)
    gpad = np.concatenate([gidx, gidx[-128:]], axis=0)
    upad = np.concatenate([uidx, uidx[-128:]], axis=0)
    sh["_gA"] = np.ascontiguousarray(
        gpad.reshape(44, 128, 16, 128).transpose(0, 3, 2, 1).reshape(44, 128, 2048))
    sh["_uA"] = np.ascontiguousarray(
        upad.reshape(44, 128, 16, 128).transpose(0, 3, 2, 1).reshape(44, 128, 2048))
    sh["_dA"] = np.ascontiguousarray(
        didx.reshape(16, 128, 43, 128).transpose(0, 3, 2, 1).reshape(16, 128, 5504))
    sh["wmod_f"] = np.asarray(inputs["L0_wmod"], dtype=f32)
    sh["x_f"] = np.asarray(inputs["L0_attn_h"], dtype=f32)
    return sh


def _in_map_for_core(c, sh):
    b, a = divmod(c, 2)
    blocks = A_BLOCKS if a == 0 else B_BLOCKS
    xT = np.ascontiguousarray(sh["x_f"][b].T)
    xown = np.concatenate([xT[:, blk * 128:(blk + 1) * 128] for blk in blocks], axis=1)
    gsel = [GATE_PAD[c][j] for j in range(6)]
    m = {
        "xown": np.ascontiguousarray(xown.reshape(16, 128, TOWN)),
        "xfull": np.ascontiguousarray(xT.reshape(16, 128, SEQ)),
        "ctr": sh["ctr"],
        "wmods": np.ascontiguousarray(sh["wmod_f"][:, c * 1536:(c + 1) * 1536]),
        "lutv": sh["lutv"],
        "gidx": np.ascontiguousarray(sh["_gA"][gsel]),
        "uidx": np.ascontiguousarray(sh["_uA"][gsel]),
        "didx": np.ascontiguousarray(sh["_dA"][[2 * c, 2 * c + 1]]),
        "scv": np.asarray([blk * 128 for blk in blocks], dtype=np.float32),
        "ebv": np.eye(NB, dtype=np.float32)[b],
    }
    for mm in "qkvo":
        m[f"w4{mm}"] = sh[f"w4{mm}"]
        m[f"b2{mm}"] = sh[f"b2{mm}"]
    return m


_NC_CACHE = {}


def kernel(**inputs):
    sh = _prep_shared(inputs)
    in_maps = [_in_map_for_core(c, sh) for c in range(8)]
    if "nc" not in _NC_CACHE:
        _NC_CACHE["nc"] = build_program()
    nc = _NC_CACHE["nc"]
    res = run_bass_kernel_spmd(nc, in_maps, core_ids=list(range(8)))
    out = np.empty((NB, SEQ, D), dtype=np.float32)
    for c in range(8):
        b, a = divmod(c, 2)
        blocks = A_BLOCKS if a == 0 else B_BLOCKS
        yc = np.asarray(res.results[c]["y"]).reshape(D, TOWN)
        for s, blk in enumerate(blocks):
            out[b, blk * 128:(blk + 1) * 128, :] = yc[:, s * 128:(s + 1) * 128].T
    return out


# version 20
# speedup vs baseline: 1.0133x; 1.0133x over previous
import contextlib
import math

import numpy as np

import concourse.bass as bass
import concourse.mybir as mybir
from concourse.bass_utils import run_bass_kernel_spmd
import concourse.dve_ops as dve_ops
from concourse.dve_op import DveOp
from concourse.dve_spec import Spec, Src0, Src1, C0, C1, C2, One, eq, lower as dve_lower

F32 = mybir.dt.float32
BF16 = mybir.dt.bfloat16
I32 = mybir.dt.int32
AF = mybir.ActivationFunctionType
ALU = mybir.AluOpType

D = 2048
NH = 16
HD = 128
SEQ = 2048
NB = 4
INTER = 5504
NIC = 16
TOWN = 1024
NSLOT = 8
SCALE = 1.0 / math.sqrt(HD)
EPS = 1.0e-5

A_BLOCKS = [0, 3, 4, 7, 8, 11, 12, 15]
B_BLOCKS = [1, 2, 5, 6, 9, 10, 13, 14]
NT = [1, 2, 3, 4, 5, 6, 7, 8]
SLOT_ORDER = [7, 0, 6, 1, 5, 2, 4, 3]
NSTEP = NH * NSLOT
DEBUG_DUMP = False


def _gate_assign():
    real, g = [], 0
    for r in range(8):
        n = 6 if r < 3 else 5
        real.append(list(range(g, g + n)))
        g += n
    assert g == 43
    padded = [bl + [bl[-1]] * (6 - len(bl)) for bl in real]
    inv = {oc: (r, j) for r, bl in enumerate(real) for j, oc in enumerate(bl)}
    return padded, inv


GATE_PAD, GATE_INV = _gate_assign()
DOWN_INV = {oc: (oc // 2, oc % 2) for oc in range(16)}

GBLK = 128 * 2048
DBLK = 128 * 5504
W_IN_ELEMS = 12 * GBLK + 2 * DBLK
OFF_UP = 6 * GBLK
OFF_DN = 12 * GBLK


def _mk_op(name, body, reference):
    for op in dve_ops.OPS:
        if op.name == name:
            return op
    shas = {}
    for ver in ("v3", "v4"):
        shas[ver] = dve_lower(Spec(body=body), ver=ver, perf_en=False, name=name).sha(ver)
    op = DveOp(name, Spec(body=body), subdim=False, uops_sha=shas, reference=reference)
    dve_ops.OPS.append(op)
    dve_ops._SUB_OPCODE_FOR_NAME[name] = dve_ops._CUSTOM_DVE_ROW_BASE + len(dve_ops.OPS) - 1
    dve_ops.CUSTOM_DVE_SPECS[name] = op.spec
    dve_ops.CUSTOM_DVE_REFS[name] = op.ref
    return op


LUT_INIT2 = _mk_op(
    "LUT_INIT2",
    C0 * eq(Src0, C2) + C1 * eq(Src0, C2 + One),
    lambda in0, in1, s0, s1, imm2: s0 * (in0 == imm2) + s1 * (in0 == imm2 + 1.0),
)
LUT_ACC2 = _mk_op(
    "LUT_ACC2",
    Src1 + C0 * eq(Src0, C2) + C1 * eq(Src0, C2 + One),
    lambda in0, in1, s0, s1, imm2: in1 + s0 * (in0 == imm2) + s1 * (in0 == imm2 + 1.0),
)


_SER_STATE = [0]


def build_program():
    nc = bass.Bass("TRN2", target_bir_lowering=False, detect_race_conditions=False)

    d_xown = nc.dram_tensor("xown", [NIC, 128, TOWN], F32, kind="ExternalInput").ap()
    d_xfull = nc.dram_tensor("xfull", [NIC, 128, SEQ], F32, kind="ExternalInput").ap()
    d_ct = nc.dram_tensor("ctr", [D, NB], F32, kind="ExternalInput").ap()
    d_wmod = nc.dram_tensor("wmods", [D, 1536], F32, kind="ExternalInput").ap()
    d_lut = nc.dram_tensor("lutv", [16], F32, kind="ExternalInput").ap()
    d_w4 = {m: nc.dram_tensor(f"w4{m}", [16, 128, D], F32, kind="ExternalInput").ap()
            for m in "qkvo"}
    d_b2 = {m: nc.dram_tensor(f"b2{m}", [128, 16], F32, kind="ExternalInput").ap()
            for m in "qkvo"}
    d_gidx = nc.dram_tensor("gidx", [6, 128, 2048], I32, kind="ExternalInput").ap()
    d_uidx = nc.dram_tensor("uidx", [6, 128, 2048], I32, kind="ExternalInput").ap()
    d_didx = nc.dram_tensor("didx", [2, 128, 5504], I32, kind="ExternalInput").ap()
    d_sc = nc.dram_tensor("scv", [8], F32, kind="ExternalInput").ap()
    d_eb = nc.dram_tensor("ebv", [NB], F32, kind="ExternalInput").ap()
    d_y = nc.dram_tensor("y", [NIC, 128, TOWN], F32, kind="ExternalOutput").ap()

    d_modin = nc.dram_tensor("modin", [128, 48], F32).ap()
    d_modout = nc.dram_tensor("modout", [8, 128, 48], F32).ap()
    d_win = nc.dram_tensor("win", [W_IN_ELEMS], BF16).ap()
    d_wout = nc.dram_tensor("wout", [8 * W_IN_ELEMS], BF16, addr_space="Shared").ap()
    d_kv = nc.dram_tensor("kvs", [2, NH, 128, SEQ], BF16).ap()
    d_q = nc.dram_tensor("qsc", [NH, 128, TOWN], BF16).ap()
    if DEBUG_DUMP:
        d_dbg_modv = nc.dram_tensor("dbg_modv", [128, 96], F32, kind="ExternalOutput").ap()
        d_dbg_vec = nc.dram_tensor("dbg_vec", [128, 64], F32, kind="ExternalOutput").ap()
        d_dbg_xres = nc.dram_tensor("dbg_xres", [NIC, 128, TOWN], F32, kind="ExternalOutput").ap()
        d_dbg_g = nc.dram_tensor("dbg_g", [43, 128, TOWN], BF16, kind="ExternalOutput").ap()
        d_dbg_w = nc.dram_tensor("dbg_w", [2, 128, 4096], BF16, kind="ExternalOutput").ap()
        d_dbg_h2a = nc.dram_tensor("dbg_h2a", [128, 16384], BF16, kind="ExternalOutput").ap()
        d_dbg_h2b = nc.dram_tensor("dbg_h2b", [128, 16384], BF16, kind="ExternalOutput").ap()
        d_dbg_sgf = nc.dram_tensor("dbg_sgf", [128, TOWN], F32, kind="ExternalOutput").ap()
        d_dbg_at = nc.dram_tensor("dbg_at", [128, 144], F32, kind="ExternalOutput").ap()
        d_dbg_ab = nc.dram_tensor("dbg_ab", [128, 1664], BF16, kind="ExternalOutput").ap()

    def wout_slice(base_off, blk_elems):
        return d_wout[base_off:base_off + blk_elems].rearrange("(p c) -> p c", p=128)

    def wout_gate(oc):
        r, j = GATE_INV[oc]
        return wout_slice(r * W_IN_ELEMS + j * GBLK, GBLK)

    def wout_up(oc):
        r, j = GATE_INV[oc]
        return wout_slice(r * W_IN_ELEMS + OFF_UP + j * GBLK, GBLK)

    def wout_down(oc):
        r, j = DOWN_INV[oc]
        return wout_slice(r * W_IN_ELEMS + OFF_DN + j * DBLK, DBLK)

    ES = contextlib.ExitStack()
    with ES:
        sem = {}

        def S(name):
            if name not in sem:
                sem[name] = ES.enter_context(nc.semaphore(name))
            return sem[name]

        x_res = ES.enter_context(nc.sbuf_tensor("x_res", [128, NIC * TOWN], F32))
        identb = ES.enter_context(nc.sbuf_tensor("identb", [128, 128], BF16))
        identf = ES.enter_context(nc.sbuf_tensor("identf", [128, 128], F32))
        lt = ES.enter_context(nc.sbuf_tensor("lt", [128, 16], F32))
        scb = ES.enter_context(nc.sbuf_tensor("scb", [128, 8], F32))
        ebb = ES.enter_context(nc.sbuf_tensor("ebb", [128, NB], F32))
        bias_t = {m: ES.enter_context(nc.sbuf_tensor(f"bias_{m}", [128, 16], F32))
                  for m in "qkvo"}
        modv = ES.enter_context(nc.sbuf_tensor("modv", [128, 96], F32))
        g1c = ES.enter_context(nc.sbuf_tensor("g1c", [128, 16], F32))
        g2c = ES.enter_context(nc.sbuf_tensor("g2c", [128, 16], F32))
        gac = ES.enter_context(nc.sbuf_tensor("gac", [128, 16], F32))
        gbc = ES.enter_context(nc.sbuf_tensor("gbc", [128, 16], F32))
        onesb = ES.enter_context(nc.sbuf_tensor("onesb", [128, 1], BF16))
        onesf1 = ES.enter_context(nc.sbuf_tensor("onesf1", [1, 128], F32))
        onesf1b = ES.enter_context(nc.sbuf_tensor("onesf1b", [1, 128], BF16))
        epsr = ES.enter_context(nc.sbuf_tensor("epsr", [1, 1], F32))
        scr = ES.enter_context(nc.sbuf_tensor("scr", [128, 16], F32))

        SA = modv[:, 0:16]
        CA = modv[:, 16:32]
        GA = modv[:, 32:48]
        SB_ = modv[:, 48:64]
        CB = modv[:, 64:80]
        GB = modv[:, 80:96]

        conv_tiles = []
        for k in range(6):
            for p2_ in range(2):
                conv_tiles.append((d_gidx[k][:, 1024 * p2_:1024 * (p2_ + 1)], 1024,
                                   k * GBLK, 2048, 1024 * p2_))
        for k in range(6):
            for p2_ in range(2):
                conv_tiles.append((d_uidx[k][:, 1024 * p2_:1024 * (p2_ + 1)], 1024,
                                   OFF_UP + k * GBLK, 2048, 1024 * p2_))
        for k in range(2):
            off = 0
            while off < 5504:
                w = min(1024, 5504 - off)
                conv_tiles.append((d_didx[k][:, off:off + w], w,
                                   OFF_DN + k * DBLK, 5504, off))
                off += w
        NCONV = len(conv_tiles)

        def emit_ln_stats(tag, T, src_spec, LN, blk, stats_waits=(), cast_waits=(),
                          dma_engine=None):
            kind = src_spec[0]
            NJ = T // 512
            PSTAT = contextlib.ExitStack()
            ps_sx = PSTAT.enter_context(nc.psum_tensor(f"{tag}sx", [1, T], F32))
            ps_sxx = PSTAT.enter_context(nc.psum_tensor(f"{tag}sxx", [1, T], F32))
            sxxs = LN.enter_context(nc.sbuf_tensor(f"{tag}sxxs", [1, T], F32))
            mir = LN.enter_context(nc.sbuf_tensor(f"{tag}mir", [1, T], BF16))
            xf = None
            if kind == "dram":
                xf = [LN.enter_context(nc.sbuf_tensor(f"{tag}xf{i}", [128, T], F32)) for i in range(2)]
            xb = [PSTAT.enter_context(nc.sbuf_tensor(f"{tag}xb{i}", [128, T], BF16)) for i in range(2)]
            xq = PSTAT.enter_context(nc.sbuf_tensor(f"{tag}xq", [128, T], BF16))

            def lsrc(ic):
                if kind == "dram":
                    return xf[ic % 2][:, :]
                _, t_sb, stride = src_spec
                return t_sb[:, ic * stride:ic * stride + T]

            if kind == "dram":
                def _dma_pass1(eng):
                    dap = src_spec[1]
                    for t in range(NIC):
                        if t >= 2:
                            eng.wait_ge(S(f"{tag}_vc"), t - 1)
                            eng.wait_ge(S(f"{tag}_ac"), t - 1)
                        eng.dma_start(out=xf[t % 2][:, :],
                                      in_=dap[t]).then_inc(S(f"{tag}_qx"), 16)
                def _dma_pass2(eng):
                    for t in range(NIC):
                        tg = NIC + t
                        if t < 2:
                            eng.wait_ge(S(f"{tag}_vc"), NIC)
                            eng.wait_ge(S(f"{tag}_ac"), NIC)
                        else:
                            eng.wait_ge(S(f"{tag}_ap"), t - 1)
                        eng.dma_start(out=xf[tg % 2][:, :],
                                      in_=src_spec[1][t]).then_inc(S(f"{tag}_qx"), 16)
            else:
                _dma_pass1 = _dma_pass2 = None

            @blk.vector
            def _(vector):
                for sname, v in cast_waits:
                    vector.wait_ge(S(sname), v)
                for ic in range(NIC):
                    if kind == "dram":
                        vector.wait_ge(S(f"{tag}_qx"), 16 * (ic + 1))
                    if ic >= 2:
                        vector.wait_ge(S(f"{tag}_pe"), ic - 1)
                    vector.tensor_copy(out=xb[ic % 2][:, :],
                                       in_=lsrc(ic)).then_inc(S(f"{tag}_vc"), 1)

            @blk.scalar
            def _(scalar):
                for sname, v in cast_waits:
                    scalar.wait_ge(S(sname), v)
                for ic in range(NIC):
                    if kind == "dram":
                        scalar.wait_ge(S(f"{tag}_qx"), 16 * (ic + 1))
                    if ic >= 1:
                        scalar.wait_ge(S(f"{tag}_pe"), ic)
                    scalar.activation(out=xq[:, :], in_=lsrc(ic),
                                      func=AF.Square).then_inc(S(f"{tag}_ac"), 1)

            @blk.tensor
            def _(tensor):
                for sname, v in stats_waits:
                    tensor.wait_ge(S(sname), v)
                for ic in range(NIC):
                    tensor.wait_ge(S(f"{tag}_vc"), ic + 1)
                    tensor.wait_ge(S(f"{tag}_ac"), ic + 1)
                    ins = None
                    for j in range(NJ):
                        tensor.matmul(ps_sx[:, j * 512:(j + 1) * 512], onesb[:, :],
                                      xb[ic % 2][:, j * 512:(j + 1) * 512],
                                      start=(ic == 0), stop=(ic == NIC - 1))
                        ins = tensor.matmul(ps_sxx[:, j * 512:(j + 1) * 512], onesb[:, :],
                                            xq[:, j * 512:(j + 1) * 512],
                                            start=(ic == 0), stop=(ic == NIC - 1))
                    ins.then_inc(S(f"{tag}_pe"), 1)

            @blk.vector
            def _(vector):
                vector.wait_ge(S(f"{tag}_pe"), NIC)
                vector.tensor_scalar(out=ps_sx[:, :], in0=ps_sx[:, :], scalar1=1.0 / D,
                                     scalar2=None, op0=ALU.mult)
                vector.tensor_scalar(out=ps_sxx[:, :], in0=ps_sxx[:, :], scalar1=1.0 / D,
                                     scalar2=None, op0=ALU.mult).then_inc(S(f"{tag}_st"), 1)

            @blk.scalar
            def _(scalar):
                scalar.wait_ge(S(f"{tag}_st"), 1)
                scalar.activation(out=mir[:, :], in_=ps_sx[:, :],
                                  func=AF.Square).then_inc(S(f"{tag}_st"), 1)

            @blk.vector
            def _(vector):
                vector.wait_ge(S(f"{tag}_st"), 2)
                vector.tensor_sub(out=ps_sxx[:, :], in0=ps_sxx[:, :],
                                  in1=mir[:, :]).then_inc(S(f"{tag}_st"), 1)

            @blk.scalar
            def _(scalar):
                scalar.wait_ge(S(f"{tag}_st"), 3)
                scalar.activation(out=sxxs[:, :], in_=ps_sxx[:, :], func=AF.Sqrt,
                                  bias=epsr[:, 0:1]).then_inc(S(f"{tag}_st"), 1)

            @blk.vector
            def _(vector):
                vector.wait_ge(S(f"{tag}_st"), 4)
                vector.reciprocal(out=sxxs[:, :], in_=sxxs[:, :])
                vector.tensor_mul(out=mir[:, :], in0=ps_sx[:, :],
                                  in1=sxxs[:, :]).then_inc(S(f"{tag}_st"), 1)

            PSTAT.close()
            return {"sxxs": sxxs, "mir": mir, "lsrc": lsrc, "kind": kind,
                    "dma1": _dma_pass1, "dma2": _dma_pass2}

        def emit_ln_apply(tag, T, st, h_out, h_stride, gcol, scol, LN, blk, apply_waits=()):
            NJ = T // 512
            APL = contextlib.ExitStack()
            pb_istd = APL.enter_context(nc.psum_tensor(f"{tag}bi", [128, T], F32))
            pb_mi = APL.enter_context(nc.psum_tensor(f"{tag}bm", [128, T], F32))
            tt = APL.enter_context(nc.sbuf_tensor(f"{tag}tt", [128, T], F32))
            sxxs, mir, lsrc, kind = st["sxxs"], st["mir"], st["lsrc"], st["kind"]

            @blk.tensor
            def _(tensor):
                tensor.wait_ge(S(f"{tag}_st"), 5)
                ins = None
                for j in range(NJ):
                    tensor.matmul(pb_istd[:, j * 512:(j + 1) * 512], onesf1[:, :],
                                  sxxs[:, j * 512:(j + 1) * 512], start=True, stop=True)
                    ins = tensor.matmul(pb_mi[:, j * 512:(j + 1) * 512], onesf1b[:, :],
                                        mir[:, j * 512:(j + 1) * 512], start=True, stop=True)
                ins.then_inc(S(f"{tag}_bc"), 1)

            @blk.vector
            def _(vector):
                for sname, v in apply_waits:
                    vector.wait_ge(S(sname), v)
                vector.wait_ge(S(f"{tag}_bc"), 1)
                for ic in range(NIC):
                    if kind == "dram":
                        vector.wait_ge(S(f"{tag}_qx"), 16 * (NIC + ic + 1))
                    vector.tensor_mul(out=tt[:, :], in0=lsrc(ic), in1=pb_istd[:, :])
                    vector.tensor_sub(out=tt[:, :], in0=tt[:, :], in1=pb_mi[:, :])
                    vector.tensor_scalar(
                        out=h_out[:, ic * h_stride:ic * h_stride + T],
                        in0=tt[:, :], scalar1=gcol[:, ic:ic + 1],
                        scalar2=scol[:, ic:ic + 1],
                        op0=ALU.mult, op1=ALU.add).then_inc(S(f"{tag}_ap"), 1)
            return APL

        P12 = contextlib.ExitStack()
        h_kv = P12.enter_context(nc.sbuf_tensor("h_kv", [128, NIC * SEQ], BF16))
        h1own = P12.enter_context(nc.sbuf_tensor("h1own", [128, NIC * TOWN], BF16))

        P0 = contextlib.ExitStack()
        with P0:
            Mall = P0.enter_context(nc.sbuf_tensor("Mall", [128, 384], F32))
            P0M = contextlib.ExitStack()
            ct = P0M.enter_context(nc.sbuf_tensor("ct", [128, 64], F32))
            cs = P0M.enter_context(nc.sbuf_tensor("cs", [128, 64], F32))
            wmf = [P0M.enter_context(nc.sbuf_tensor(f"wmf{i}", [128, 1536], F32)) for i in range(2)]
            modp = P0M.enter_context(nc.sbuf_tensor("modp", [4, 1536], F32))
            PMOD = contextlib.ExitStack()
            pmod = PMOD.enter_context(nc.psum_tensor("pmod", [4, 1536], F32))
            pmt = PMOD.enter_context(nc.psum_tensor("pmt", [128, 48], F32))

            with nc.Block() as blk:
                @blk.sync
                def _(sync):
                    for ic in range(NIC):
                        sync.dma_start(out=x_res[:, ic * TOWN:(ic + 1) * TOWN],
                                       in_=d_xown[ic]).then_inc(S("q_xres"), 16)
                    for ic in range(NIC):
                        sync.dma_start(out=ct[:, ic * 4:(ic + 1) * 4],
                                       in_=d_ct[ic * 128:(ic + 1) * 128, :]).then_inc(S("q_ct"), 16)
                    for m in "qkvo":
                        sync.dma_start(out=bias_t[m][:, :], in_=d_b2[m][:, :]).then_inc(S("q_bias"), 16)
                    for ic in range(NIC):
                        if ic >= 2:
                            sync.wait_ge(S("e_modmm"), ic - 1)
                        sync.dma_start(out=wmf[ic % 2][:, :],
                                       in_=d_wmod[ic * 128:(ic + 1) * 128, :]).then_inc(S("q_wmf"), 16)
                    sync.wait_ge(S("ch0"), 4)
                    sync.dma_start(out=d_modin[:, :], in_=Mall[:, 0:48]).then_inc(S("q_modin"), 16)

                @blk.vector
                def _(vector):
                    _sern = [0]

                    def ser(ins):
                        _sern[0] += 1
                        ins.then_inc(S("e_ser"), 1)
                        vector.wait_ge(S("e_ser"), _sern[0])

                    vector.wait_ge(S("e_iota"), 1)
                    ser(vector.tensor_scalar(out=identf[:, :], in0=identf[:, :], scalar1=0.0,
                                             scalar2=None, op0=ALU.is_equal))
                    vector.tensor_copy(out=identb[:, :], in_=identf[:, :])
                    vector.memset(onesb[:, :], 1.0)
                    vector.memset(epsr[:, :], EPS)
                    vector.memset(onesf1b[:, :], 1.0)
                    vector.memset(onesf1[:, :], 1.0).then_inc(S("e_ident"), 1)
                    vector.wait_ge(S("q_bias"), 64)
                    ser(vector.tensor_scalar(out=bias_t["q"][:, :], in0=bias_t["q"][:, :],
                                             scalar1=SCALE, scalar2=None, op0=ALU.mult))
                    vector.engine_nop().then_inc(S("e_serdone"), 1)
                    _SER_STATE[0] = _sern[0]

                @blk.scalar
                def _(scalar):
                    scalar.wait_ge(S("q_ct"), 256)
                    scalar.activation(out=cs[:, :], in_=ct[:, :], func=AF.Silu).then_inc(S("a_cs"), 1)
                    scalar.wait_ge(S("e_modmm"), NIC)
                    scalar.copy(out=modp[:, :], in_=pmod[:, :]).then_inc(S("ch0"), 1)
                    scalar.wait_ge(S("ch0"), 3)
                    scalar.copy(out=Mall[:, 0:48], in_=pmt[:, :]).then_inc(S("ch0"), 1)

                @blk.tensor
                def _(tensor):
                    tensor.wait_ge(S("a_cs"), 1)
                    for ic in range(NIC):
                        tensor.wait_ge(S("q_wmf"), 16 * (ic + 1))
                        ins = None
                        for j in range(3):
                            ins = tensor.matmul(pmod[:, j * 512:(j + 1) * 512],
                                                cs[:, ic * 4:(ic + 1) * 4],
                                                wmf[ic % 2][:, j * 512:(j + 1) * 512],
                                                start=(ic == 0), stop=(ic == NIC - 1))
                        ins.then_inc(S("e_modmm"), 1)
                    tensor.wait_ge(S("ch0"), 1)
                    tensor.wait_ge(S("e_ident"), 1)
                    ins = None
                    for g in range(12):
                        ins = tensor.transpose(pmt[:, g * 4:(g + 1) * 4],
                                               modp[:, g * 128:(g + 1) * 128],
                                               identf[0:4, 0:4])
                    ins.then_inc(S("ch0"), 2)

                PMOD.close()
                P0M.close()
                LN1 = contextlib.ExitStack()
                stg_lg = emit_ln_stats("lg", SEQ, ("dram", d_xfull), LN1, blk,
                                       stats_waits=(("ch0", 4),),
                                       cast_waits=(("ch0", 4),))

                @blk.gpsimd
                def _(gpsimd):
                    gpsimd.dma_start(out=lt[:, :],
                                     in_=d_lut[None, :].broadcast_to((128, 16))).then_inc(S("q_pre"), 16)
                    gpsimd.dma_start(out=scb[:, :],
                                     in_=d_sc[None, :].broadcast_to((128, 8))).then_inc(S("q_pre"), 16)
                    gpsimd.dma_start(out=ebb[:, :],
                                     in_=d_eb[None, :].broadcast_to((128, NB))).then_inc(S("q_pre"), 16)
                    gpsimd.iota(identf[:, :], pattern=[[1, 128]], base=0, channel_multiplier=-1,
                                allow_small_or_imprecise_dtypes=True).then_inc(S("e_iota"), 1)
                    gpsimd.wait_ge(S("ch0"), 4)
                    stg_lg["dma1"](gpsimd)
                    gpsimd.wait_ge(S("q_modin"), 16)
                    gpsimd.collective_compute(
                        "AllGather", ALU.bypass, replica_groups=[list(range(8))],
                        ins=[d_modin.opt()], outs=[d_modout.opt()]).then_inc(S("c_mod"), 1)
                    gpsimd.wait_ge(S("c_mod"), 1)
                    gpsimd.dma_start(out=Mall[:, :],
                                     in_=d_modout.transpose([1, 0, 2])).then_inc(S("q_modall"), 16)
                    stg_lg["dma2"](gpsimd)

                @blk.vector
                def _(vector):
                    _sern = [_SER_STATE[0]]

                    def ser(ins):
                        _sern[0] += 1
                        ins.then_inc(S("e_ser"), 1)
                        vector.wait_ge(S("e_ser"), _sern[0])

                    vector.wait_ge(S("q_pre"), 48)
                    vector.wait_ge(S("q_modall"), 16)
                    ser(vector.tensor_scalar(out=modv[:, :], in0=Mall[:, 0::4],
                                             scalar1=ebb[:, 0:1], scalar2=None, op0=ALU.mult))
                    for b4 in range(1, NB):
                        ser(vector.scalar_tensor_tensor(out=modv[:, :], in0=Mall[:, b4::4],
                                                        scalar=ebb[:, b4:b4 + 1], in1=modv[:, :],
                                                        op0=ALU.mult, op1=ALU.add))
                    ser(vector.tensor_scalar_add(out=g1c[:, :], in0=CA, scalar1=1.0))
                    ser(vector.tensor_scalar_add(out=g2c[:, :], in0=CB, scalar1=1.0))
                    for vsrc, dst in ((GA, gac), (GB, gbc)):
                        ser(vector.tensor_scalar(out=scr[:, :], in0=vsrc, scalar1=0.0,
                                                 scalar2=2.0, op0=ALU.is_ge, op1=ALU.mult))
                        ser(vector.tensor_scalar_add(out=scr[:, :], in0=scr[:, :], scalar1=-1.0))
                        ser(vector.tensor_mul(out=dst[:, :], in0=vsrc, in1=scr[:, :]))
                        ser(vector.tensor_scalar_max(out=dst[:, :], in0=dst[:, :], scalar1=0.25))
                        ser(vector.tensor_mul(out=dst[:, :], in0=dst[:, :], in1=scr[:, :]))
                    vector.engine_nop().then_inc(S("e_mod"), 1)

                emit_ln_apply("lg", SEQ, stg_lg, h_kv, SEQ, g1c, modv[:, 0:16], LN1, blk,
                              apply_waits=(("e_mod", 1),)).close()
                stg_lo = emit_ln_stats("lo", TOWN, ("sbuf", x_res, TOWN), LN1, blk,
                                       stats_waits=(("lg_ap", NIC),),
                                       cast_waits=(("q_xres", 256), ("lg_ap", NIC)))
                emit_ln_apply("lo", TOWN, stg_lo, h1own, TOWN, g1c, modv[:, 0:16], LN1, blk,
                              apply_waits=(("e_mod", 1),)).close()
            LN1.close()

        P2 = contextlib.ExitStack()
        with P2:
            wf = [P2.enter_context(nc.sbuf_tensor(f"wf{i}", [128, 2048], F32)) for i in range(2)]
            wb = [P2.enter_context(nc.sbuf_tensor(f"wbq{i}", [128, 2048], BF16)) for i in range(2)]
            stg = [P2.enter_context(nc.sbuf_tensor(f"stg{i}", [128, 2048], BF16)) for i in range(2)]
            cvi = [P2.enter_context(nc.sbuf_tensor(f"cvi{i}", [128, 1024], I32)) for i in range(2)]
            cvo = [P2.enter_context(nc.sbuf_tensor(f"cvo{i}", [128, 1024], BF16)) for i in range(2)]
            pqk = [P2.enter_context(nc.psum_tensor(f"pqk{i}", [128, 2048], F32)) for i in range(2)]

            jobs = [("k", oc) for oc in range(NH)] + \
                   [("v", oc) for oc in range(NH)] + \
                   [("q", oc) for oc in range(NH)]
            NJOB = len(jobs)
            conv_at = {}
            for k in range(NCONV):
                conv_at.setdefault(k * NJOB // NCONV, []).append(k)

            with nc.Block() as blk:
                @blk.sync
                def _(sync):
                    for t, (m, oc) in enumerate(jobs):
                        if t >= 2:
                            sync.wait_ge(S("p2_cast"), t - 1)
                        sync.dma_start(out=wf[t % 2][:, :], in_=d_w4[m][oc]).then_inc(S("p2_qw"), 16)

                @blk.gpsimd
                def _(gpsimd):
                    def _cstage(k):
                        gpsimd.wait_ge(S("e_conv"), k + 1)
                        _, w, off, bw, coff = conv_tiles[k]
                        blkview = d_win[off:off + 128 * bw].rearrange("(p c) -> p c", p=128)
                        gpsimd.dma_start(out=blkview[:, coff:coff + w],
                                         in_=cvo[k % 2][:, 0:w]).then_inc(S("q_cvout"), 16)

                    for t, (m, oc) in enumerate(jobs):
                        for k in conv_at.get(t, []):
                            if k >= 2:
                                gpsimd.wait_ge(S("e_conv"), k - 1)
                            w = conv_tiles[k][1]
                            gpsimd.dma_start(out=cvi[k % 2][:, 0:w],
                                             in_=conv_tiles[k][0]).then_inc(S("q_cvin"), 16)
                            if k >= 2:
                                _cstage(k - 2)
                        gpsimd.wait_ge(S("p2_ev"), t + 1)
                        if m == "k":
                            dst, T = d_kv[0, oc], SEQ
                        elif m == "v":
                            dst, T = d_kv[1, oc], SEQ
                        else:
                            dst, T = d_q[oc], TOWN
                        gpsimd.dma_start(out=dst, in_=stg[t % 2][:, 0:T]).then_inc(S("p2_st"), 16)
                    _cstage(NCONV - 2)
                    _cstage(NCONV - 1)
                    gpsimd.wait_ge(S("p2_st"), 16 * len(jobs))
                    gpsimd.wait_ge(S("q_cvout"), 16 * NCONV)

                def _p2_evac(scalar, t):
                    m, oc = jobs[t]
                    scalar.wait_ge(S("p2_pe"), t + 1)
                    if t >= 2:
                        scalar.wait_ge(S("p2_st"), 16 * (t - 1))
                    T = TOWN if m == "q" else SEQ
                    sc_in = SCALE if m == "q" else 1.0
                    scalar.activation(out=stg[t % 2][:, 0:T], in_=pqk[t % 2][:, 0:T],
                                      func=AF.Identity, scale=sc_in,
                                      bias=bias_t[m][:, oc:oc + 1]).then_inc(S("p2_ev"), 1)

                @blk.scalar
                def _(scalar):
                    for t, (m, oc) in enumerate(jobs):
                        scalar.wait_ge(S("p2_qw"), 16 * (t + 1))
                        if t >= 2:
                            scalar.wait_ge(S("p2_pe"), t - 1)
                        scalar.copy(out=wb[t % 2][:, :],
                                    in_=wf[t % 2][:, :]).then_inc(S("p2_cast"), 1)
                        if t >= 2:
                            _p2_evac(scalar, t - 2)
                    _p2_evac(scalar, len(jobs) - 2)
                    _p2_evac(scalar, len(jobs) - 1)

                @blk.vector
                def _(vector):
                    vector.wait_ge(S("q_pre"), 48)
                    for t, (m, oc) in enumerate(jobs):
                        for k in conv_at.get(t, []):
                            w = conv_tiles[k][1]
                            vector.wait_ge(S("q_cvin"), 16 * (k + 1))
                            if k >= 2:
                                vector.wait_ge(S("q_cvout"), 16 * (k - 1))
                            o = cvo[k % 2][:, 0:w]
                            i = cvi[k % 2][:, 0:w]
                            vector._custom_dve(LUT_INIT2, out=o, in0=i,
                                               s0=lt[:, 0:1], s1=lt[:, 1:2], imm2=0.0)
                            ins = None
                            for kk in range(1, 8):
                                ins = vector._custom_dve(
                                    LUT_ACC2, out=o, in0=i, in1=o,
                                    s0=lt[:, 2 * kk:2 * kk + 1],
                                    s1=lt[:, 2 * kk + 1:2 * kk + 2],
                                    imm2=float(2 * kk))
                            ins.then_inc(S("e_conv"), 1)

                @blk.tensor
                def _(tensor):
                    for t, (m, oc) in enumerate(jobs):
                        tensor.wait_ge(S("p2_cast"), t + 1)
                        if t >= 2:
                            tensor.wait_ge(S("p2_ev"), t - 1)
                        if m == "q":
                            hsrc, hs, T = h1own, TOWN, TOWN
                        else:
                            hsrc, hs, T = h_kv, SEQ, SEQ
                        ins = None
                        for ic in range(NIC):
                            for j in range(T // 512):
                                ins = tensor.matmul(
                                    pqk[t % 2][:, j * 512:(j + 1) * 512],
                                    wb[t % 2][:, ic * 128:(ic + 1) * 128],
                                    hsrc[:, ic * hs + j * 512:ic * hs + (j + 1) * 512],
                                    start=(ic == 0), stop=(ic == NIC - 1))
                        ins.then_inc(S("p2_pe"), 1)

        P12.close()

        steps = []
        for h in range(NH):
            for k, s in enumerate(SLOT_ORDER):
                steps.append((h, s))
        NTl = [NT[s] for (_, s) in steps]
        NCH = [2 * n for n in NTl]
        NGR = [(c + 7) // 8 for c in NCH]

        LAG = 3
        ptp_order = []
        for h in range(NH):
            ptp_order.append(("v", h, 0))
            ptp_order.append(("v", h, 1))
            for k in range(NSLOT):
                t = h * NSLOT + k
                if t >= LAG:
                    for g in range(NGR[t - LAG]):
                        ptp_order.append(("t", t - LAG, g))
        for t in range(NSTEP - LAG, NSTEP):
            for g in range(NGR[t]):
                ptp_order.append(("t", t, g))
        ptp_pos = {key: i for i, key in enumerate(ptp_order)}
        trav_done = {}
        vtr_done = {}
        for i, key in enumerate(ptp_order):
            if key[0] == "t":
                trav_done[key[1]] = i + 1
            else:
                vtr_done[key[1]] = i + 1

        P3 = contextlib.ExitStack()
        with P3:
            oT = P3.enter_context(nc.sbuf_tensor("oT", [128, NIC * TOWN], BF16))
            o_sb = P3.enter_context(nc.sbuf_tensor("o_sb", [128, NSLOT * D], BF16))
            AT = contextlib.ExitStack()
            with AT:
                jg = AT.enter_context(nc.sbuf_tensor("jg", [128, SEQ], F32))
                kh = [AT.enter_context(nc.sbuf_tensor(f"kh{i}", [128, SEQ], BF16)) for i in range(2)]
                vh = [AT.enter_context(nc.sbuf_tensor(f"vh{i}", [128, SEQ], BF16)) for i in range(2)]
                vt = [AT.enter_context(nc.sbuf_tensor(f"vt{i}", [128, SEQ], BF16)) for i in range(2)]
                qh = [AT.enter_context(nc.sbuf_tensor(f"qh{i}", [128, TOWN], BF16)) for i in range(2)]
                ex = [AT.enter_context(nc.sbuf_tensor(f"ex{i}", [128, 2048], BF16)) for i in range(3)]
                pbf = [AT.enter_context(nc.sbuf_tensor(f"pbf{i}", [128, 2048], BF16)) for i in range(3)]
                pts = [AT.enter_context(nc.sbuf_tensor(f"pts{i}", [128, 2048], BF16)) for i in range(3)]
                mx = AT.enter_context(nc.sbuf_tensor("mx", [128, 4], F32))
                ssum = AT.enter_context(nc.sbuf_tensor("ssum", [128, 4], F32))
                rec = AT.enter_context(nc.sbuf_tensor("rec", [128, 8], F32))
                obuf = AT.enter_context(nc.sbuf_tensor("obuf", [128, 512], F32))
                SCL = AT.enter_context(nc.psum_tensor("SCL", [128, 2048], F32))
                SCS = AT.enter_context(nc.psum_tensor("SCS", [128, 1024], F32))
                ptp = AT.enter_context(nc.psum_tensor("ptp", [128, 1024], BF16))
                ov = AT.enter_context(nc.psum_tensor("ov", [128, 128], F32))

                _fl = {"d": 0, "a": 0}

                def flush_tick(eng, key, sername, ins, sem_name, k=1):
                    _fl[key] += 1
                    ins.then_inc(S(sername), 1)
                    eng.wait_ge(S(sername), _fl[key])
                    eng.nop(nofuse=True).then_inc(S(sem_name), k)

                def lane(t):
                    return SCL if (t % 2 == 0) else SCS

                def emit_trav(tensor, t):
                    h, s = steps[t]
                    tensor.wait_ge(S("p3_st"), t + 1)
                    for g in range(NGR[t]):
                        pos = ptp_pos[("t", t, g)]
                        tensor.wait_ge(S("p3_pte"), pos)
                        ins = None
                        for kk in range(min(8, NCH[t] - g * 8)):
                            kc = g * 8 + kk
                            ins = tensor.transpose(ptp[:, kk * 128:(kk + 1) * 128],
                                                   pbf[t % 3][:, kc * 128:(kc + 1) * 128],
                                                   identb[:, :])
                        ins.then_inc(S("p3_ptw"), 1)
                    tensor.wait_ge(S("p3_pte"), trav_done[t])
                    if t >= 1:
                        tensor.wait_ge(S("a_ovc"), t)
                    ovl = ov[:, :]
                    ins = None
                    for kc in range(NCH[t]):
                        ins = tensor.matmul(ovl,
                                            pts[t % 3][:, kc * 128:(kc + 1) * 128],
                                            vt[h % 2][:, kc * 128:(kc + 1) * 128],
                                            start=(kc == 0), stop=(kc == NCH[t] - 1))
                    ins.then_inc(S("p3_av"), 1)

                def emit_stt(vector, t):
                    h, s = steps[t]
                    W = NT[s] * 256
                    if t == 0:
                        vector.wait_ge(S("e_jg"), 1)
                    vector.wait_ge(S("p3_ex"), t + 1)
                    if t >= LAG:
                        vector.wait_ge(S("p3_ptw"), trav_done[t - LAG])
                    vector.scalar_tensor_tensor(
                        out=pbf[t % 3][:, 0:W], in0=jg[:, 0:W],
                        scalar=scb[:, s:s + 1], in1=ex[t % 3][:, 0:W],
                        op0=ALU.is_le, op1=ALU.mult,
                        accum_out=ssum[:, t % 4:t % 4 + 1]).then_inc(S("p3_st"), 1)

                def emit_rec(vector, t):
                    vector.wait_ge(S("p3_st"), t + 1)
                    vector.reciprocal(out=rec[:, t % 8:t % 8 + 1],
                                      in_=ssum[:, t % 4:t % 4 + 1]).then_inc(S("p3_rc"), 1)

                def emit_oev(vector, t):
                    h, s = steps[t]
                    vector.wait_ge(S("a_ovc"), t + 1)
                    vector.wait_ge(S("p3_rc"), t + 1)
                    vector.tensor_scalar(
                        out=o_sb[:, s * D + h * 128:s * D + (h + 1) * 128],
                        in0=obuf[:, (t % 4) * 128:(t % 4) * 128 + 128],
                        scalar1=rec[:, t % 8:t % 8 + 1],
                        scalar2=None, op0=ALU.mult).then_inc(S("p3_oe"), 1)

                def emit_pte(scalar, t):
                    for g in range(NGR[t]):
                        pos = ptp_pos[("t", t, g)]
                        scalar.wait_ge(S("p3_ptw"), pos + 1)
                        if t >= LAG:
                            scalar.wait_ge(S("p3_av"), t - LAG + 1)
                        w = min(1024, NCH[t] * 128 - g * 1024)
                        flush_tick(scalar, "a", "aser",
                                   scalar.copy(out=pts[t % 3][:, g * 1024:g * 1024 + w],
                                               in_=ptp[:, 0:w]), "p3_pte")

                with nc.Block() as blk:
                    @blk.gpsimd
                    def _(gpsimd):
                        gpsimd.iota(jg[:, :], pattern=[[1, SEQ]], base=0, channel_multiplier=-1,
                                    allow_small_or_imprecise_dtypes=True).then_inc(S("e_jg"), 1)
                        for h in range(NH):
                            if h >= 2:
                                gpsimd.wait_ge(S("p3_av"), (h - 1) * NSLOT)
                            gpsimd.dma_start(out=kh[h % 2][:, :], in_=d_kv[0, h]).then_inc(S("p3_kv"), 16)
                            gpsimd.dma_start(out=vh[h % 2][:, :], in_=d_kv[1, h]).then_inc(S("p3_kv"), 16)
                            gpsimd.dma_start(out=qh[h % 2][:, :], in_=d_q[h]).then_inc(S("p3_kv"), 16)
                            if DEBUG_DUMP and h == 1:
                                gpsimd.wait_ge(S("p3_rc"), 2)
                                gpsimd.dma_start(out=d_dbg_at[:, 0:4], in_=ssum[:, :]).then_inc(S("q_dba"), 16)
                                gpsimd.dma_start(out=d_dbg_at[:, 4:12], in_=rec[:, :]).then_inc(S("q_dba"), 16)
                                gpsimd.dma_start(out=d_dbg_at[:, 12:16], in_=mx[:, :]).then_inc(S("q_dba"), 16)
                                gpsimd.dma_start(out=d_dbg_ab[:, 0:256],
                                                 in_=ex[1][:, 0:256]).then_inc(S("q_dba"), 16)
                                gpsimd.dma_start(out=d_dbg_ab[:, 256:512],
                                                 in_=pbf[1][:, 0:256]).then_inc(S("q_dba"), 16)
                                gpsimd.dma_start(out=d_dbg_ab[:, 512:1536],
                                                 in_=qh[0][:, :]).then_inc(S("q_dba"), 16)
                                gpsimd.wait_ge(S("p3_oe"), 2)
                                gpsimd.dma_start(out=d_dbg_at[:, 16:144],
                                                 in_=obuf[:, 128:256]).then_inc(S("q_dba"), 16)
                                gpsimd.dma_start(out=d_dbg_ab[:, 1536:1664],
                                                 in_=o_sb[:, 0:128]).then_inc(S("q_dba"), 16)
                                gpsimd.wait_ge(S("q_dba"), 128)
                            if h == 6:
                                gpsimd.wait_ge(S("q_cvout"), 16 * NCONV)
                                gpsimd.collective_compute(
                                    "AllGather", ALU.bypass, replica_groups=[list(range(8))],
                                    ins=[d_win.opt()], outs=[d_wout.opt()]).then_inc(S("c_w"), 1)

                    @blk.tensor
                    def _(tensor):
                        for h in range(NH):
                            tensor.wait_ge(S("p3_kv"), 48 * (h + 1))
                            for g in range(2):
                                pos = ptp_pos[("v", h, g)]
                                if pos >= 1:
                                    tensor.wait_ge(S("p3_pte"), pos)
                                ins = None
                                for kk in range(8):
                                    kc = g * 8 + kk
                                    ins = tensor.transpose(ptp[:, kk * 128:(kk + 1) * 128],
                                                           vh[h % 2][:, kc * 128:(kc + 1) * 128],
                                                           identb[:, :])
                                ins.then_inc(S("p3_ptw"), 1)
                            for k in range(NSLOT):
                                t = h * NSLOT + k
                                s = steps[t][1]
                                SC = lane(t)
                                if t >= 2:
                                    tensor.wait_ge(S("p3_ex"), t - 1)
                                ins = None
                                for kt in range(NT[s]):
                                    ins = tensor.matmul(SC[:, kt * 256:(kt + 1) * 256],
                                                        qh[h % 2][:, s * 128:(s + 1) * 128],
                                                        kh[h % 2][:, kt * 256:(kt + 1) * 256],
                                                        start=True, stop=True)
                                ins.then_inc(S("p3_sc"), 1)
                                if t >= LAG:
                                    emit_trav(tensor, t - LAG)
                        for tl in range(NSTEP - LAG, NSTEP):
                            emit_trav(tensor, tl)

                    @blk.vector
                    def _(vector):
                        for t in range(NSTEP):
                            h, s = steps[t]
                            SC = lane(t)
                            W = NT[s] * 256
                            vector.wait_ge(S("p3_sc"), t + 1)
                            if t >= 4:
                                vector.wait_ge(S("p3_ex"), t - 3)
                            flush_tick(vector, "d", "dser",
                                       vector.reduce_max(mx[:, t % 4:t % 4 + 1], SC[:, 0:W],
                                                         axis=mybir.AxisListType.X,
                                                         negate=True), "p3_mb")
                            if t >= 3:
                                emit_oev(vector, t - 3)
                            if t >= 1:
                                emit_stt(vector, t - 1)
                                emit_rec(vector, t - 1)
                        emit_stt(vector, NSTEP - 1)
                        emit_rec(vector, NSTEP - 1)
                        for tl in range(NSTEP - 3, NSTEP):
                            emit_oev(vector, tl)

                    @blk.scalar
                    def _(scalar):
                        for h in range(NH):
                            if h >= 2:
                                scalar.wait_ge(S("p3_av"), (h - 1) * NSLOT)
                            for g in range(2):
                                pos = ptp_pos[("v", h, g)]
                                scalar.wait_ge(S("p3_ptw"), pos + 1)
                                flush_tick(scalar, "a", "aser",
                                           scalar.copy(out=vt[h % 2][:, g * 1024:(g + 1) * 1024],
                                                       in_=ptp[:, :]), "p3_pte")
                            for k in range(NSLOT):
                                t = h * NSLOT + k
                                s = steps[t][1]
                                W = NT[s] * 256
                                SC = lane(t)
                                if t >= LAG:
                                    emit_pte(scalar, t - LAG)
                                if t >= 3:
                                    tov = t - 3
                                    scalar.wait_ge(S("p3_av"), tov + 1)
                                    if tov >= 4:
                                        scalar.wait_ge(S("p3_oe"), tov - 3)
                                    flush_tick(scalar, "a", "aser",
                                               scalar.copy(out=obuf[:, (tov % 4) * 128:(tov % 4) * 128 + 128],
                                                           in_=ov[:, :]), "a_ovc")
                                scalar.wait_ge(S("p3_mb"), t + 1)
                                if t >= 3:
                                    scalar.wait_ge(S("p3_st"), t - 2)
                                flush_tick(scalar, "a", "aser",
                                           scalar.activation(out=ex[t % 3][:, 0:W], in_=SC[:, 0:W],
                                                             func=AF.Exp,
                                                             bias=mx[:, t % 4:t % 4 + 1],
                                                             scale=1.0), "p3_ex")
                        for tl in range(NSTEP - LAG, NSTEP):
                            emit_pte(scalar, tl)
                            tov = tl
                            scalar.wait_ge(S("p3_av"), tov + 1)
                            flush_tick(scalar, "a", "aser",
                                       scalar.copy(out=obuf[:, (tov % 4) * 128:(tov % 4) * 128 + 128],
                                                   in_=ov[:, :]), "a_ovc")

            OP = contextlib.ExitStack()
            with OP:
                wof = OP.enter_context(nc.sbuf_tensor("wof", [128, 2048], F32))
                wob = [OP.enter_context(nc.sbuf_tensor(f"wob{i}", [128, 2048], BF16)) for i in range(2)]
                atmp = [OP.enter_context(nc.sbuf_tensor(f"atmp{i}", [128, TOWN], F32)) for i in range(2)]
                pot = OP.enter_context(nc.psum_tensor("pot", [128, 1024], BF16))
                po = [OP.enter_context(nc.psum_tensor(f"po{i}", [128, TOWN], F32)) for i in range(2)]

                with nc.Block() as blk:
                    @blk.sync
                    def _(sync):
                        for oc in range(NIC):
                            if oc >= 1:
                                sync.wait_ge(S("p3o_cast"), oc)
                            sync.dma_start(out=wof[:, :], in_=d_w4["o"][oc]).then_inc(S("p3o_qw"), 16)

                    @blk.gpsimd
                    def _(gpsimd):
                        for oc in range(NIC):
                            gpsimd.wait_ge(S("p3o_qw"), 16 * (oc + 1))
                            if oc >= 2:
                                gpsimd.wait_ge(S("p3o_pe"), oc - 1)
                            gpsimd.tensor_copy(out=wob[oc % 2][:, :],
                                               in_=wof[:, :]).then_inc(S("p3o_cast"), 1)

                    @blk.tensor
                    def _(tensor):
                        for ic in range(NIC):
                            if ic >= 1:
                                tensor.wait_ge(S("p3o_te"), ic)
                            ins = None
                            for s in range(NSLOT):
                                ins = tensor.transpose(pot[:, s * 128:(s + 1) * 128],
                                                       o_sb[:, s * D + ic * 128:s * D + (ic + 1) * 128],
                                                       identb[:, :])
                            ins.then_inc(S("p3o_tr"), 1)

                    @blk.scalar
                    def _(scalar):
                        for ic in range(NIC):
                            scalar.wait_ge(S("p3o_tr"), ic + 1)
                            scalar.copy(out=oT[:, ic * TOWN:(ic + 1) * TOWN],
                                        in_=pot[:, :]).then_inc(S("p3o_te"), 1)

                    @blk.tensor
                    def _(tensor):
                        tensor.wait_ge(S("p3o_te"), NIC)
                        for oc in range(NIC):
                            tensor.wait_ge(S("p3o_cast"), oc + 1)
                            if oc >= 2:
                                tensor.wait_ge(S("p3o_ev"), oc - 1)
                            ins = None
                            for ic in range(NIC):
                                for j in range(TOWN // 512):
                                    ins = tensor.matmul(
                                        po[oc % 2][:, j * 512:(j + 1) * 512],
                                        wob[oc % 2][:, ic * 128:(ic + 1) * 128],
                                        oT[:, ic * TOWN + j * 512:ic * TOWN + (j + 1) * 512],
                                        start=(ic == 0), stop=(ic == NIC - 1))
                            ins.then_inc(S("p3o_pe"), 1)

                    @blk.scalar
                    def _(scalar):
                        for oc in range(NIC):
                            scalar.wait_ge(S("p3o_pe"), oc + 1)
                            if oc >= 2:
                                scalar.wait_ge(S("p3o_res"), oc - 1)
                            scalar.activation(out=atmp[oc % 2][:, :], in_=po[oc % 2][:, :],
                                              func=AF.Identity,
                                              bias=bias_t["o"][:, oc:oc + 1]).then_inc(S("p3o_ev"), 1)

                    @blk.vector
                    def _(vector):
                        for oc in range(NIC):
                            vector.wait_ge(S("p3o_ev"), oc + 1)
                            vector.scalar_tensor_tensor(
                                out=x_res[:, oc * TOWN:(oc + 1) * TOWN],
                                in0=atmp[oc % 2][:, :], scalar=gac[:, oc:oc + 1],
                                in1=x_res[:, oc * TOWN:(oc + 1) * TOWN],
                                op0=ALU.mult, op1=ALU.add).then_inc(S("p3o_res"), 1)

        FG = contextlib.ExitStack()
        with FG:
            g_sb = FG.enter_context(nc.sbuf_tensor("g_sb", [128, 43 * TOWN], BF16))
            FH = contextlib.ExitStack()
            with FH:
                h2 = FH.enter_context(nc.sbuf_tensor("h2", [128, NIC * TOWN], BF16))
                LN2 = contextlib.ExitStack()
                with LN2:
                    with nc.Block() as blk2:
                        stg_l2 = emit_ln_stats("l2", TOWN, ("sbuf", x_res, TOWN), LN2, blk2)
                        emit_ln_apply("l2", TOWN, stg_l2, h2, TOWN, g2c,
                                      modv[:, 48:64], LN2, blk2).close()

                GU = contextlib.ExitStack()
                with GU:
                    wgb = [GU.enter_context(nc.sbuf_tensor(f"wgb{i}", [128, 2048], BF16)) for i in range(2)]
                    wub = [GU.enter_context(nc.sbuf_tensor(f"wub{i}", [128, 2048], BF16)) for i in range(2)]
                    sgf = GU.enter_context(nc.sbuf_tensor("sgf", [128, TOWN], F32))
                    pg = [GU.enter_context(nc.psum_tensor(f"pg{i}", [128, TOWN], F32)) for i in range(2)]
                    pu = [GU.enter_context(nc.psum_tensor(f"pu{i}", [128, TOWN], F32)) for i in range(2)]

                    with nc.Block() as blk:
                        @blk.sync
                        def _(sync):
                            sync.wait_ge(S("c_w"), 1)
                            for oc in range(43):
                                if oc >= 2:
                                    sync.wait_ge(S("p5_pe"), oc - 1)
                                if DEBUG_DUMP and oc == 2:
                                    sync.wait_ge(S("q_dbw"), 64)
                                sync.dma_start(out=wgb[oc % 2][:, :],
                                               in_=wout_gate(oc)).then_inc(S("p5_qw"), 16)
                                sync.dma_start(out=wub[oc % 2][:, :],
                                               in_=wout_up(oc)).then_inc(S("p5_qw"), 16)

                        if DEBUG_DUMP:
                            @blk.gpsimd
                            def _(gpsimd):
                                gpsimd.wait_ge(S("p5_qw"), 32)
                                gpsimd.dma_start(out=d_dbg_w[0][:, 0:2048],
                                                 in_=wgb[0][:, :]).then_inc(S("q_dbw"), 16)
                                gpsimd.dma_start(out=d_dbg_w[0][:, 2048:4096],
                                                 in_=wub[0][:, :]).then_inc(S("q_dbw"), 16)
                                gpsimd.dma_start(out=d_dbg_w[1][:, 0:2048],
                                                 in_=wout_gate(0)).then_inc(S("q_dbw"), 16)
                                gpsimd.dma_start(out=d_dbg_w[1][:, 2048:4096],
                                                 in_=wout_up(0)).then_inc(S("q_dbw"), 16)
                                gpsimd.dma_start(out=d_dbg_h2a[:, :],
                                                 in_=h2[:, :]).then_inc(S("q_dbh"), 16)
                                gpsimd.wait_ge(S("p5_si"), 1)
                                gpsimd.dma_start(out=d_dbg_sgf[:, :],
                                                 in_=sgf[:, :]).then_inc(S("q_dbs"), 16)
                                gpsimd.wait_ge(S("p5_pe"), 6)
                                gpsimd.dma_start(out=d_dbg_h2b[:, :],
                                                 in_=h2[:, :]).then_inc(S("q_dbh"), 16)
                                gpsimd.wait_ge(S("q_dbw"), 64)

                        @blk.tensor
                        def _(tensor):
                            tensor.wait_ge(S("p5_qw"), 32)
                            for ic in range(NIC):
                                for j in range(TOWN // 512):
                                    sl = slice(j * 512, (j + 1) * 512)
                                    hsl = h2[:, ic * TOWN + j * 512:ic * TOWN + (j + 1) * 512]
                                    tensor.matmul(pg[0][:, sl],
                                                  wgb[0][:, ic * 128:(ic + 1) * 128], hsl,
                                                  start=(ic == 0), stop=(ic == NIC - 1))
                                    tensor.matmul(pu[0][:, sl],
                                                  wub[0][:, ic * 128:(ic + 1) * 128], hsl,
                                                  start=(ic == 0), stop=(ic == NIC - 1))
                            for oc in range(43):
                                tensor.wait_ge(S("p5_qw"), 32 * (oc + 1))
                                if oc >= 2:
                                    tensor.wait_ge(S("p5_gm"), oc - 1)
                                ins = None
                                for ic in range(NIC):
                                    for j in range(TOWN // 512):
                                        sl = slice(j * 512, (j + 1) * 512)
                                        hsl = h2[:, ic * TOWN + j * 512:ic * TOWN + (j + 1) * 512]
                                        tensor.matmul(pg[oc % 2][:, sl],
                                                      wgb[oc % 2][:, ic * 128:(ic + 1) * 128], hsl,
                                                      start=(ic == 0), stop=(ic == NIC - 1))
                                        ins = tensor.matmul(pu[oc % 2][:, sl],
                                                            wub[oc % 2][:, ic * 128:(ic + 1) * 128], hsl,
                                                            start=(ic == 0), stop=(ic == NIC - 1))
                                ins.then_inc(S("p5_pe"), 1)

                        @blk.scalar
                        def _(scalar):
                            for oc in range(43):
                                scalar.wait_ge(S("p5_pe"), oc + 1)
                                if DEBUG_DUMP and oc == 1:
                                    scalar.wait_ge(S("q_dbs"), 16)
                                if oc >= 2:
                                    scalar.wait_ge(S("p5_gm"), oc - 1)
                                if oc >= 1:
                                    scalar.wait_ge(S("p5_gm"), oc)
                                scalar.activation(out=sgf[:, :], in_=pg[oc % 2][:, :],
                                                  func=AF.Silu).then_inc(S("p5_si"), 1)

                        @blk.vector
                        def _(vector):
                            for oc in range(43):
                                vector.wait_ge(S("p5_si"), oc + 1)
                                vector.tensor_mul(out=g_sb[:, oc * TOWN:(oc + 1) * TOWN],
                                                  in0=sgf[:, :],
                                                  in1=pu[oc % 2][:, :]).then_inc(S("p5_gm"), 1)

            DN = contextlib.ExitStack()
            with DN:
                wdb = [DN.enter_context(nc.sbuf_tensor(f"wdb{i}", [128, 5504], BF16)) for i in range(2)]
                ysb = [DN.enter_context(nc.sbuf_tensor(f"ysb{i}", [128, TOWN], F32)) for i in range(2)]
                pd = [DN.enter_context(nc.psum_tensor(f"pd{i}", [128, TOWN], F32)) for i in range(2)]

                with nc.Block() as blk:
                    @blk.sync
                    def _(sync):
                        for oc in range(NIC):
                            if oc >= 2:
                                sync.wait_ge(S("p6_pe"), oc - 1)
                            sync.dma_start(out=wdb[oc % 2][:, :],
                                           in_=wout_down(oc)).then_inc(S("p6_qw"), 16)

                    @blk.tensor
                    def _(tensor):
                        for oc in range(NIC):
                            tensor.wait_ge(S("p6_qw"), 16 * (oc + 1))
                            if oc >= 2:
                                tensor.wait_ge(S("p6_ye"), oc - 1)
                            ins = None
                            for ic in range(43):
                                for j in range(TOWN // 512):
                                    ins = tensor.matmul(
                                        pd[oc % 2][:, j * 512:(j + 1) * 512],
                                        wdb[oc % 2][:, ic * 128:(ic + 1) * 128],
                                        g_sb[:, ic * TOWN + j * 512:ic * TOWN + (j + 1) * 512],
                                        start=(ic == 0), stop=(ic == 42))
                            ins.then_inc(S("p6_pe"), 1)

                    @blk.vector
                    def _(vector):
                        for oc in range(NIC):
                            vector.wait_ge(S("p6_pe"), oc + 1)
                            if oc >= 2:
                                vector.wait_ge(S("p6_qy"), 16 * (oc - 1))
                            vector.scalar_tensor_tensor(
                                out=ysb[oc % 2][:, :], in0=pd[oc % 2][:, :],
                                scalar=gbc[:, oc:oc + 1],
                                in1=x_res[:, oc * TOWN:(oc + 1) * TOWN],
                                op0=ALU.mult, op1=ALU.add).then_inc(S("p6_ye"), 1)

                    @blk.gpsimd
                    def _(gpsimd):
                        for oc in range(NIC):
                            gpsimd.wait_ge(S("p6_ye"), oc + 1)
                            gpsimd.dma_start(out=d_y[oc],
                                             in_=ysb[oc % 2][:, :]).then_inc(S("p6_qy"), 16)
                        gpsimd.wait_ge(S("p6_qy"), 16 * NIC)
                        if DEBUG_DUMP:
                            gpsimd.dma_start(out=d_dbg_modv[:, :], in_=modv[:, :]).then_inc(S("q_dbg"), 16)
                            gpsimd.dma_start(out=d_dbg_vec[:, 0:16], in_=g1c[:, :]).then_inc(S("q_dbg"), 16)
                            gpsimd.dma_start(out=d_dbg_vec[:, 16:32], in_=g2c[:, :]).then_inc(S("q_dbg"), 16)
                            gpsimd.dma_start(out=d_dbg_vec[:, 32:48], in_=gac[:, :]).then_inc(S("q_dbg"), 16)
                            gpsimd.dma_start(out=d_dbg_vec[:, 48:64], in_=gbc[:, :]).then_inc(S("q_dbg"), 16)
                            for ic in range(NIC):
                                gpsimd.dma_start(out=d_dbg_xres[ic],
                                                 in_=x_res[:, ic * TOWN:(ic + 1) * TOWN]).then_inc(S("q_dbg"), 16)
                            for ocg in range(43):
                                gpsimd.dma_start(out=d_dbg_g[ocg],
                                                 in_=g_sb[:, ocg * TOWN:(ocg + 1) * TOWN]).then_inc(S("q_dbg"), 16)
                            gpsimd.wait_ge(S("q_dbg"), 16 * (5 + NIC + 43))

    for i in nc.all_instructions():
        if isinstance(i, mybir.InstCustomDveAnt) and len(i.instr) == 0:
            mybir.codegen_inst_isa_one(i, nc, nc.isa)
    return nc


def _prep_shared(inputs):
    f32 = np.float32
    sh = {}
    for m, wkey in (("q", "L0_attn_Wq"), ("k", "L0_attn_Wk"),
                    ("v", "L0_attn_Wv"), ("o", "L0_attn_Wo")):
        W = np.asarray(inputs[wkey], dtype=f32)
        sh[f"w4{m}"] = np.ascontiguousarray(
            W.reshape(16, 128, 16, 128).transpose(0, 3, 2, 1).reshape(16, 128, 2048))
    for m, bkey in (("q", "L0_attn_bq"), ("k", "L0_attn_bk"),
                    ("v", "L0_attn_bv"), ("o", "L0_attn_bo")):
        bvec = np.asarray(inputs[bkey], dtype=f32)
        sh[f"b2{m}"] = np.ascontiguousarray(bvec.reshape(16, 128).T)
    sh["ctr"] = np.ascontiguousarray(np.asarray(inputs["L0_c"], dtype=f32).T)
    sh["lutv"] = np.ascontiguousarray(np.asarray(inputs["L0_ffn_lut"], dtype=f32))
    gidx = np.asarray(inputs["L0_ffn_gate_idx"], dtype=np.int32)
    uidx = np.asarray(inputs["L0_ffn_up_idx"], dtype=np.int32)
    didx = np.asarray(inputs["L0_ffn_down_idx"], dtype=np.int32)
    gpad = np.concatenate([gidx, gidx[-128:]], axis=0)
    upad = np.concatenate([uidx, uidx[-128:]], axis=0)
    sh["_gA"] = np.ascontiguousarray(
        gpad.reshape(44, 128, 16, 128).transpose(0, 3, 2, 1).reshape(44, 128, 2048))
    sh["_uA"] = np.ascontiguousarray(
        upad.reshape(44, 128, 16, 128).transpose(0, 3, 2, 1).reshape(44, 128, 2048))
    sh["_dA"] = np.ascontiguousarray(
        didx.reshape(16, 128, 43, 128).transpose(0, 3, 2, 1).reshape(16, 128, 5504))
    sh["wmod_f"] = np.asarray(inputs["L0_wmod"], dtype=f32)
    sh["x_f"] = np.asarray(inputs["L0_attn_h"], dtype=f32)
    return sh


def _in_map_for_core(c, sh):
    b, a = divmod(c, 2)
    blocks = A_BLOCKS if a == 0 else B_BLOCKS
    xT = np.ascontiguousarray(sh["x_f"][b].T)
    xown = np.concatenate([xT[:, blk * 128:(blk + 1) * 128] for blk in blocks], axis=1)
    gsel = [GATE_PAD[c][j] for j in range(6)]
    m = {
        "xown": np.ascontiguousarray(xown.reshape(16, 128, TOWN)),
        "xfull": np.ascontiguousarray(xT.reshape(16, 128, SEQ)),
        "ctr": sh["ctr"],
        "wmods": np.ascontiguousarray(sh["wmod_f"][:, c * 1536:(c + 1) * 1536]),
        "lutv": sh["lutv"],
        "gidx": np.ascontiguousarray(sh["_gA"][gsel]),
        "uidx": np.ascontiguousarray(sh["_uA"][gsel]),
        "didx": np.ascontiguousarray(sh["_dA"][[2 * c, 2 * c + 1]]),
        "scv": np.asarray([blk * 128 for blk in blocks], dtype=np.float32),
        "ebv": np.eye(NB, dtype=np.float32)[b],
    }
    for mm in "qkvo":
        m[f"w4{mm}"] = sh[f"w4{mm}"]
        m[f"b2{mm}"] = sh[f"b2{mm}"]
    return m


_NC_CACHE = {}


def kernel(**inputs):
    sh = _prep_shared(inputs)
    in_maps = [_in_map_for_core(c, sh) for c in range(8)]
    if "nc" not in _NC_CACHE:
        _NC_CACHE["nc"] = build_program()
    nc = _NC_CACHE["nc"]
    res = run_bass_kernel_spmd(nc, in_maps, core_ids=list(range(8)))
    out = np.empty((NB, SEQ, D), dtype=np.float32)
    for c in range(8):
        b, a = divmod(c, 2)
        blocks = A_BLOCKS if a == 0 else B_BLOCKS
        yc = np.asarray(res.results[c]["y"]).reshape(D, TOWN)
        for s, blk in enumerate(blocks):
            out[b, blk * 128:(blk + 1) * 128, :] = yc[:, s * 128:(s + 1) * 128].T
    return out


# version 21
# speedup vs baseline: 1.0356x; 1.0221x over previous
import contextlib
import math

import numpy as np

import concourse.bass as bass
import concourse.mybir as mybir
from concourse.bass_utils import run_bass_kernel_spmd
import concourse.dve_ops as dve_ops
from concourse.dve_op import DveOp
from concourse.dve_spec import Spec, Src0, Src1, C0, C1, C2, One, eq, lower as dve_lower

F32 = mybir.dt.float32
BF16 = mybir.dt.bfloat16
I32 = mybir.dt.int32
AF = mybir.ActivationFunctionType
ALU = mybir.AluOpType

D = 2048
NH = 16
HD = 128
SEQ = 2048
NB = 4
INTER = 5504
NIC = 16
TOWN = 1024
NSLOT = 8
SCALE = 1.0 / math.sqrt(HD)
EPS = 1.0e-5

A_BLOCKS = [0, 3, 4, 7, 8, 11, 12, 15]
B_BLOCKS = [1, 2, 5, 6, 9, 10, 13, 14]
NT = [1, 2, 3, 4, 5, 6, 7, 8]
SLOT_ORDER = [7, 0, 6, 1, 5, 2, 4, 3]
NSTEP = NH * NSLOT
DEBUG_DUMP = False


def _gate_assign():
    real, g = [], 0
    for r in range(8):
        n = 6 if r < 3 else 5
        real.append(list(range(g, g + n)))
        g += n
    assert g == 43
    padded = [bl + [bl[-1]] * (6 - len(bl)) for bl in real]
    inv = {oc: (r, j) for r, bl in enumerate(real) for j, oc in enumerate(bl)}
    return padded, inv


GATE_PAD, GATE_INV = _gate_assign()
DOWN_INV = {oc: (oc // 2, oc % 2) for oc in range(16)}

GBLK = 128 * 2048
DBLK = 128 * 5504
W_IN_ELEMS = 12 * GBLK + 2 * DBLK
OFF_UP = 6 * GBLK
OFF_DN = 12 * GBLK


def _mk_op(name, body, reference):
    for op in dve_ops.OPS:
        if op.name == name:
            return op
    shas = {}
    for ver in ("v3", "v4"):
        shas[ver] = dve_lower(Spec(body=body), ver=ver, perf_en=False, name=name).sha(ver)
    op = DveOp(name, Spec(body=body), subdim=False, uops_sha=shas, reference=reference)
    dve_ops.OPS.append(op)
    dve_ops._SUB_OPCODE_FOR_NAME[name] = dve_ops._CUSTOM_DVE_ROW_BASE + len(dve_ops.OPS) - 1
    dve_ops.CUSTOM_DVE_SPECS[name] = op.spec
    dve_ops.CUSTOM_DVE_REFS[name] = op.ref
    return op


LUT_INIT2 = _mk_op(
    "LUT_INIT2",
    C0 * eq(Src0, C2) + C1 * eq(Src0, C2 + One),
    lambda in0, in1, s0, s1, imm2: s0 * (in0 == imm2) + s1 * (in0 == imm2 + 1.0),
)
LUT_ACC2 = _mk_op(
    "LUT_ACC2",
    Src1 + C0 * eq(Src0, C2) + C1 * eq(Src0, C2 + One),
    lambda in0, in1, s0, s1, imm2: in1 + s0 * (in0 == imm2) + s1 * (in0 == imm2 + 1.0),
)


_SER_STATE = [0]


def build_program():
    nc = bass.Bass("TRN2", target_bir_lowering=False, detect_race_conditions=False)

    d_xown = nc.dram_tensor("xown", [NIC, 128, TOWN], F32, kind="ExternalInput").ap()
    d_xfull = nc.dram_tensor("xfull", [NIC, 128, SEQ], F32, kind="ExternalInput").ap()
    d_ct = nc.dram_tensor("ctr", [D, NB], F32, kind="ExternalInput").ap()
    d_wmod = nc.dram_tensor("wmods", [D, 1536], F32, kind="ExternalInput").ap()
    d_lut = nc.dram_tensor("lutv", [16], F32, kind="ExternalInput").ap()
    d_w4 = {m: nc.dram_tensor(f"w4{m}", [16, 128, D], F32, kind="ExternalInput").ap()
            for m in "qkvo"}
    d_b2 = {m: nc.dram_tensor(f"b2{m}", [128, 16], F32, kind="ExternalInput").ap()
            for m in "qkvo"}
    d_gidx = nc.dram_tensor("gidx", [6, 128, 2048], I32, kind="ExternalInput").ap()
    d_uidx = nc.dram_tensor("uidx", [6, 128, 2048], I32, kind="ExternalInput").ap()
    d_didx = nc.dram_tensor("didx", [2, 128, 5504], I32, kind="ExternalInput").ap()
    d_sc = nc.dram_tensor("scv", [8], F32, kind="ExternalInput").ap()
    d_eb = nc.dram_tensor("ebv", [NB], F32, kind="ExternalInput").ap()
    d_y = nc.dram_tensor("y", [NIC, 128, TOWN], F32, kind="ExternalOutput").ap()

    d_modin = nc.dram_tensor("modin", [128, 48], F32).ap()
    d_modout = nc.dram_tensor("modout", [8, 128, 48], F32).ap()
    d_win = nc.dram_tensor("win", [W_IN_ELEMS], BF16).ap()
    d_wout = nc.dram_tensor("wout", [8 * W_IN_ELEMS], BF16, addr_space="Shared").ap()
    d_kv = nc.dram_tensor("kvs", [2, NH, 128, SEQ], BF16).ap()
    d_q = nc.dram_tensor("qsc", [NH, 128, TOWN], BF16).ap()
    if DEBUG_DUMP:
        d_dbg_modv = nc.dram_tensor("dbg_modv", [128, 96], F32, kind="ExternalOutput").ap()
        d_dbg_vec = nc.dram_tensor("dbg_vec", [128, 64], F32, kind="ExternalOutput").ap()
        d_dbg_xres = nc.dram_tensor("dbg_xres", [NIC, 128, TOWN], F32, kind="ExternalOutput").ap()
        d_dbg_g = nc.dram_tensor("dbg_g", [43, 128, TOWN], BF16, kind="ExternalOutput").ap()
        d_dbg_w = nc.dram_tensor("dbg_w", [2, 128, 4096], BF16, kind="ExternalOutput").ap()
        d_dbg_h2a = nc.dram_tensor("dbg_h2a", [128, 16384], BF16, kind="ExternalOutput").ap()
        d_dbg_h2b = nc.dram_tensor("dbg_h2b", [128, 16384], BF16, kind="ExternalOutput").ap()
        d_dbg_sgf = nc.dram_tensor("dbg_sgf", [128, TOWN], F32, kind="ExternalOutput").ap()
        d_dbg_at = nc.dram_tensor("dbg_at", [128, 144], F32, kind="ExternalOutput").ap()
        d_dbg_ab = nc.dram_tensor("dbg_ab", [128, 1664], BF16, kind="ExternalOutput").ap()

    def wout_slice(base_off, blk_elems):
        return d_wout[base_off:base_off + blk_elems].rearrange("(p c) -> p c", p=128)

    def wout_gate(oc):
        r, j = GATE_INV[oc]
        return wout_slice(r * W_IN_ELEMS + j * GBLK, GBLK)

    def wout_up(oc):
        r, j = GATE_INV[oc]
        return wout_slice(r * W_IN_ELEMS + OFF_UP + j * GBLK, GBLK)

    def wout_down(oc):
        r, j = DOWN_INV[oc]
        return wout_slice(r * W_IN_ELEMS + OFF_DN + j * DBLK, DBLK)

    ES = contextlib.ExitStack()
    with ES:
        sem = {}

        def S(name):
            if name not in sem:
                sem[name] = ES.enter_context(nc.semaphore(name))
            return sem[name]

        x_res = ES.enter_context(nc.sbuf_tensor("x_res", [128, NIC * TOWN], F32))
        identb = ES.enter_context(nc.sbuf_tensor("identb", [128, 128], BF16))
        identf = ES.enter_context(nc.sbuf_tensor("identf", [128, 128], F32))
        lt = ES.enter_context(nc.sbuf_tensor("lt", [128, 16], F32))
        scb = ES.enter_context(nc.sbuf_tensor("scb", [128, 8], F32))
        ebb = ES.enter_context(nc.sbuf_tensor("ebb", [128, NB], F32))
        bias_t = {m: ES.enter_context(nc.sbuf_tensor(f"bias_{m}", [128, 16], F32))
                  for m in "qkvo"}
        modv = ES.enter_context(nc.sbuf_tensor("modv", [128, 96], F32))
        g1c = ES.enter_context(nc.sbuf_tensor("g1c", [128, 16], F32))
        g2c = ES.enter_context(nc.sbuf_tensor("g2c", [128, 16], F32))
        gac = ES.enter_context(nc.sbuf_tensor("gac", [128, 16], F32))
        gbc = ES.enter_context(nc.sbuf_tensor("gbc", [128, 16], F32))
        onesb = ES.enter_context(nc.sbuf_tensor("onesb", [128, 1], BF16))
        onesf1 = ES.enter_context(nc.sbuf_tensor("onesf1", [1, 128], F32))
        onesf1b = ES.enter_context(nc.sbuf_tensor("onesf1b", [1, 128], BF16))
        epsr = ES.enter_context(nc.sbuf_tensor("epsr", [1, 1], F32))
        scr = ES.enter_context(nc.sbuf_tensor("scr", [128, 16], F32))

        SA = modv[:, 0:16]
        CA = modv[:, 16:32]
        GA = modv[:, 32:48]
        SB_ = modv[:, 48:64]
        CB = modv[:, 64:80]
        GB = modv[:, 80:96]

        conv_tiles = []
        for k in range(6):
            for p2_ in range(2):
                conv_tiles.append((d_gidx[k][:, 1024 * p2_:1024 * (p2_ + 1)], 1024,
                                   k * GBLK, 2048, 1024 * p2_))
        for k in range(6):
            for p2_ in range(2):
                conv_tiles.append((d_uidx[k][:, 1024 * p2_:1024 * (p2_ + 1)], 1024,
                                   OFF_UP + k * GBLK, 2048, 1024 * p2_))
        for k in range(2):
            off = 0
            while off < 5504:
                w = min(1024, 5504 - off)
                conv_tiles.append((d_didx[k][:, off:off + w], w,
                                   OFF_DN + k * DBLK, 5504, off))
                off += w
        NCONV = len(conv_tiles)

        def emit_ln_stats(tag, T, src_spec, LN, blk, stats_waits=(), cast_waits=(),
                          dma_engine=None):
            kind = src_spec[0]
            NJ = T // 512
            PSTAT = contextlib.ExitStack()
            ps_sx = PSTAT.enter_context(nc.psum_tensor(f"{tag}sx", [1, T], F32))
            ps_sxx = PSTAT.enter_context(nc.psum_tensor(f"{tag}sxx", [1, T], F32))
            sxxs = LN.enter_context(nc.sbuf_tensor(f"{tag}sxxs", [1, T], F32))
            mir = LN.enter_context(nc.sbuf_tensor(f"{tag}mir", [1, T], BF16))
            xf = None
            if kind == "dram":
                xf = [LN.enter_context(nc.sbuf_tensor(f"{tag}xf{i}", [128, T], F32)) for i in range(2)]
            xb = [PSTAT.enter_context(nc.sbuf_tensor(f"{tag}xb{i}", [128, T], BF16)) for i in range(2)]
            xq = PSTAT.enter_context(nc.sbuf_tensor(f"{tag}xq", [128, T], BF16))

            def lsrc(ic):
                if kind == "dram":
                    return xf[ic % 2][:, :]
                _, t_sb, stride = src_spec
                return t_sb[:, ic * stride:ic * stride + T]

            if kind == "dram":
                def _dma_pass1(eng):
                    dap = src_spec[1]
                    for t in range(NIC):
                        if t >= 2:
                            eng.wait_ge(S(f"{tag}_vc"), t - 1)
                            eng.wait_ge(S(f"{tag}_ac"), t - 1)
                        eng.dma_start(out=xf[t % 2][:, :],
                                      in_=dap[t]).then_inc(S(f"{tag}_qx"), 16)
                def _dma_pass2(eng):
                    for t in range(NIC):
                        tg = NIC + t
                        if t < 2:
                            eng.wait_ge(S(f"{tag}_vc"), NIC)
                            eng.wait_ge(S(f"{tag}_ac"), NIC)
                        else:
                            eng.wait_ge(S(f"{tag}_ap"), t - 1)
                        eng.dma_start(out=xf[tg % 2][:, :],
                                      in_=src_spec[1][t]).then_inc(S(f"{tag}_qx"), 16)
            else:
                _dma_pass1 = _dma_pass2 = None

            @blk.vector
            def _(vector):
                for sname, v in cast_waits:
                    vector.wait_ge(S(sname), v)
                for ic in range(NIC):
                    if kind == "dram":
                        vector.wait_ge(S(f"{tag}_qx"), 16 * (ic + 1))
                    if ic >= 2:
                        vector.wait_ge(S(f"{tag}_pe"), ic - 1)
                    vector.tensor_copy(out=xb[ic % 2][:, :],
                                       in_=lsrc(ic)).then_inc(S(f"{tag}_vc"), 1)

            @blk.scalar
            def _(scalar):
                for sname, v in cast_waits:
                    scalar.wait_ge(S(sname), v)
                for ic in range(NIC):
                    if kind == "dram":
                        scalar.wait_ge(S(f"{tag}_qx"), 16 * (ic + 1))
                    if ic >= 1:
                        scalar.wait_ge(S(f"{tag}_pe"), ic)
                    scalar.activation(out=xq[:, :], in_=lsrc(ic),
                                      func=AF.Square).then_inc(S(f"{tag}_ac"), 1)

            @blk.tensor
            def _(tensor):
                for sname, v in stats_waits:
                    tensor.wait_ge(S(sname), v)
                for ic in range(NIC):
                    tensor.wait_ge(S(f"{tag}_vc"), ic + 1)
                    tensor.wait_ge(S(f"{tag}_ac"), ic + 1)
                    ins = None
                    for j in range(NJ):
                        tensor.matmul(ps_sx[:, j * 512:(j + 1) * 512], onesb[:, :],
                                      xb[ic % 2][:, j * 512:(j + 1) * 512],
                                      start=(ic == 0), stop=(ic == NIC - 1))
                        ins = tensor.matmul(ps_sxx[:, j * 512:(j + 1) * 512], onesb[:, :],
                                            xq[:, j * 512:(j + 1) * 512],
                                            start=(ic == 0), stop=(ic == NIC - 1))
                    ins.then_inc(S(f"{tag}_pe"), 1)

            @blk.vector
            def _(vector):
                vector.wait_ge(S(f"{tag}_pe"), NIC)
                vector.tensor_scalar(out=ps_sx[:, :], in0=ps_sx[:, :], scalar1=1.0 / D,
                                     scalar2=None, op0=ALU.mult)
                vector.tensor_scalar(out=ps_sxx[:, :], in0=ps_sxx[:, :], scalar1=1.0 / D,
                                     scalar2=None, op0=ALU.mult).then_inc(S(f"{tag}_st"), 1)

            @blk.scalar
            def _(scalar):
                scalar.wait_ge(S(f"{tag}_st"), 1)
                scalar.activation(out=mir[:, :], in_=ps_sx[:, :],
                                  func=AF.Square).then_inc(S(f"{tag}_st"), 1)

            @blk.vector
            def _(vector):
                vector.wait_ge(S(f"{tag}_st"), 2)
                vector.tensor_sub(out=ps_sxx[:, :], in0=ps_sxx[:, :],
                                  in1=mir[:, :]).then_inc(S(f"{tag}_st"), 1)

            @blk.scalar
            def _(scalar):
                scalar.wait_ge(S(f"{tag}_st"), 3)
                scalar.activation(out=sxxs[:, :], in_=ps_sxx[:, :], func=AF.Sqrt,
                                  bias=epsr[:, 0:1]).then_inc(S(f"{tag}_st"), 1)

            @blk.vector
            def _(vector):
                vector.wait_ge(S(f"{tag}_st"), 4)
                vector.reciprocal(out=sxxs[:, :], in_=sxxs[:, :])
                vector.tensor_mul(out=mir[:, :], in0=ps_sx[:, :],
                                  in1=sxxs[:, :]).then_inc(S(f"{tag}_st"), 1)

            PSTAT.close()
            return {"sxxs": sxxs, "mir": mir, "lsrc": lsrc, "kind": kind,
                    "dma1": _dma_pass1, "dma2": _dma_pass2}

        def emit_ln_apply(tag, T, st, h_out, h_stride, gcol, scol, LN, blk, apply_waits=()):
            NJ = T // 512
            APL = contextlib.ExitStack()
            pb_istd = APL.enter_context(nc.psum_tensor(f"{tag}bi", [128, T], F32))
            pb_mi = APL.enter_context(nc.psum_tensor(f"{tag}bm", [128, T], F32))
            tt = APL.enter_context(nc.sbuf_tensor(f"{tag}tt", [128, T], F32))
            sxxs, mir, lsrc, kind = st["sxxs"], st["mir"], st["lsrc"], st["kind"]

            @blk.tensor
            def _(tensor):
                tensor.wait_ge(S(f"{tag}_st"), 5)
                ins = None
                for j in range(NJ):
                    tensor.matmul(pb_istd[:, j * 512:(j + 1) * 512], onesf1[:, :],
                                  sxxs[:, j * 512:(j + 1) * 512], start=True, stop=True)
                    ins = tensor.matmul(pb_mi[:, j * 512:(j + 1) * 512], onesf1b[:, :],
                                        mir[:, j * 512:(j + 1) * 512], start=True, stop=True)
                ins.then_inc(S(f"{tag}_bc"), 1)

            @blk.vector
            def _(vector):
                for sname, v in apply_waits:
                    vector.wait_ge(S(sname), v)
                vector.wait_ge(S(f"{tag}_bc"), 1)
                for ic in range(NIC):
                    if kind == "dram":
                        vector.wait_ge(S(f"{tag}_qx"), 16 * (NIC + ic + 1))
                    vector.tensor_mul(out=tt[:, :], in0=lsrc(ic), in1=pb_istd[:, :])
                    vector.tensor_sub(out=tt[:, :], in0=tt[:, :], in1=pb_mi[:, :])
                    vector.tensor_scalar(
                        out=h_out[:, ic * h_stride:ic * h_stride + T],
                        in0=tt[:, :], scalar1=gcol[:, ic:ic + 1],
                        scalar2=scol[:, ic:ic + 1],
                        op0=ALU.mult, op1=ALU.add).then_inc(S(f"{tag}_ap"), 1)
            return APL

        P12 = contextlib.ExitStack()
        h_kv = P12.enter_context(nc.sbuf_tensor("h_kv", [128, NIC * SEQ], BF16))
        h1own = P12.enter_context(nc.sbuf_tensor("h1own", [128, NIC * TOWN], BF16))

        P0 = contextlib.ExitStack()
        with P0:
            Mall = P0.enter_context(nc.sbuf_tensor("Mall", [128, 384], F32))
            P0M = contextlib.ExitStack()
            ct = P0M.enter_context(nc.sbuf_tensor("ct", [128, 64], F32))
            cs = P0M.enter_context(nc.sbuf_tensor("cs", [128, 64], F32))
            wmf = [P0M.enter_context(nc.sbuf_tensor(f"wmf{i}", [128, 1536], F32)) for i in range(2)]
            modp = P0M.enter_context(nc.sbuf_tensor("modp", [4, 1536], F32))
            PMOD = contextlib.ExitStack()
            pmod = PMOD.enter_context(nc.psum_tensor("pmod", [4, 1536], F32))
            pmt = PMOD.enter_context(nc.psum_tensor("pmt", [128, 48], F32))

            with nc.Block() as blk:
                @blk.sync
                def _(sync):
                    for ic in range(NIC):
                        sync.dma_start(out=x_res[:, ic * TOWN:(ic + 1) * TOWN],
                                       in_=d_xown[ic]).then_inc(S("q_xres"), 16)
                    for ic in range(NIC):
                        sync.dma_start(out=ct[:, ic * 4:(ic + 1) * 4],
                                       in_=d_ct[ic * 128:(ic + 1) * 128, :]).then_inc(S("q_ct"), 16)
                    for m in "qkvo":
                        sync.dma_start(out=bias_t[m][:, :], in_=d_b2[m][:, :]).then_inc(S("q_bias"), 16)
                    for ic in range(NIC):
                        if ic >= 2:
                            sync.wait_ge(S("e_modmm"), ic - 1)
                        sync.dma_start(out=wmf[ic % 2][:, :],
                                       in_=d_wmod[ic * 128:(ic + 1) * 128, :]).then_inc(S("q_wmf"), 16)
                    sync.wait_ge(S("ch0"), 4)
                    sync.dma_start(out=d_modin[:, :], in_=Mall[:, 0:48]).then_inc(S("q_modin"), 16)

                @blk.vector
                def _(vector):
                    _sern = [0]

                    def ser(ins):
                        _sern[0] += 1
                        ins.then_inc(S("e_ser"), 1)
                        vector.wait_ge(S("e_ser"), _sern[0])

                    vector.wait_ge(S("e_iota"), 1)
                    ser(vector.tensor_scalar(out=identf[:, :], in0=identf[:, :], scalar1=0.0,
                                             scalar2=None, op0=ALU.is_equal))
                    vector.tensor_copy(out=identb[:, :], in_=identf[:, :])
                    vector.memset(onesb[:, :], 1.0)
                    vector.memset(epsr[:, :], EPS)
                    vector.memset(onesf1b[:, :], 1.0)
                    vector.memset(onesf1[:, :], 1.0).then_inc(S("e_ident"), 1)
                    vector.wait_ge(S("q_bias"), 64)
                    ser(vector.tensor_scalar(out=bias_t["q"][:, :], in0=bias_t["q"][:, :],
                                             scalar1=SCALE, scalar2=None, op0=ALU.mult))
                    vector.engine_nop().then_inc(S("e_serdone"), 1)
                    _SER_STATE[0] = _sern[0]

                @blk.scalar
                def _(scalar):
                    scalar.wait_ge(S("q_ct"), 256)
                    scalar.activation(out=cs[:, :], in_=ct[:, :], func=AF.Silu).then_inc(S("a_cs"), 1)
                    scalar.wait_ge(S("e_modmm"), NIC)
                    scalar.copy(out=modp[:, :], in_=pmod[:, :]).then_inc(S("ch0"), 1)
                    scalar.wait_ge(S("ch0"), 3)
                    scalar.copy(out=Mall[:, 0:48], in_=pmt[:, :]).then_inc(S("ch0"), 1)

                @blk.tensor
                def _(tensor):
                    tensor.wait_ge(S("a_cs"), 1)
                    for ic in range(NIC):
                        tensor.wait_ge(S("q_wmf"), 16 * (ic + 1))
                        ins = None
                        for j in range(3):
                            ins = tensor.matmul(pmod[:, j * 512:(j + 1) * 512],
                                                cs[:, ic * 4:(ic + 1) * 4],
                                                wmf[ic % 2][:, j * 512:(j + 1) * 512],
                                                start=(ic == 0), stop=(ic == NIC - 1))
                        ins.then_inc(S("e_modmm"), 1)
                    tensor.wait_ge(S("ch0"), 1)
                    tensor.wait_ge(S("e_ident"), 1)
                    ins = None
                    for g in range(12):
                        ins = tensor.transpose(pmt[:, g * 4:(g + 1) * 4],
                                               modp[:, g * 128:(g + 1) * 128],
                                               identf[0:4, 0:4])
                    ins.then_inc(S("ch0"), 2)

                PMOD.close()
                P0M.close()
                LN1 = contextlib.ExitStack()
                stg_lg = emit_ln_stats("lg", SEQ, ("dram", d_xfull), LN1, blk,
                                       stats_waits=(("ch0", 4),),
                                       cast_waits=(("ch0", 4),))

                @blk.gpsimd
                def _(gpsimd):
                    gpsimd.dma_start(out=lt[:, :],
                                     in_=d_lut[None, :].broadcast_to((128, 16))).then_inc(S("q_pre"), 16)
                    gpsimd.dma_start(out=scb[:, :],
                                     in_=d_sc[None, :].broadcast_to((128, 8))).then_inc(S("q_pre"), 16)
                    gpsimd.dma_start(out=ebb[:, :],
                                     in_=d_eb[None, :].broadcast_to((128, NB))).then_inc(S("q_pre"), 16)
                    gpsimd.iota(identf[:, :], pattern=[[1, 128]], base=0, channel_multiplier=-1,
                                allow_small_or_imprecise_dtypes=True).then_inc(S("e_iota"), 1)
                    gpsimd.wait_ge(S("ch0"), 4)
                    stg_lg["dma1"](gpsimd)
                    gpsimd.wait_ge(S("q_modin"), 16)
                    gpsimd.collective_compute(
                        "AllGather", ALU.bypass, replica_groups=[list(range(8))],
                        ins=[d_modin.opt()], outs=[d_modout.opt()]).then_inc(S("c_mod"), 1)
                    gpsimd.wait_ge(S("c_mod"), 1)
                    gpsimd.dma_start(out=Mall[:, :],
                                     in_=d_modout.transpose([1, 0, 2])).then_inc(S("q_modall"), 16)
                    stg_lg["dma2"](gpsimd)

                @blk.vector
                def _(vector):
                    _sern = [_SER_STATE[0]]

                    def ser(ins):
                        _sern[0] += 1
                        ins.then_inc(S("e_ser"), 1)
                        vector.wait_ge(S("e_ser"), _sern[0])

                    vector.wait_ge(S("q_pre"), 48)
                    vector.wait_ge(S("q_modall"), 16)
                    ser(vector.tensor_scalar(out=modv[:, :], in0=Mall[:, 0::4],
                                             scalar1=ebb[:, 0:1], scalar2=None, op0=ALU.mult))
                    for b4 in range(1, NB):
                        ser(vector.scalar_tensor_tensor(out=modv[:, :], in0=Mall[:, b4::4],
                                                        scalar=ebb[:, b4:b4 + 1], in1=modv[:, :],
                                                        op0=ALU.mult, op1=ALU.add))
                    ser(vector.tensor_scalar_add(out=g1c[:, :], in0=CA, scalar1=1.0))
                    ser(vector.tensor_scalar_add(out=g2c[:, :], in0=CB, scalar1=1.0))
                    for vsrc, dst in ((GA, gac), (GB, gbc)):
                        ser(vector.tensor_scalar(out=scr[:, :], in0=vsrc, scalar1=0.0,
                                                 scalar2=2.0, op0=ALU.is_ge, op1=ALU.mult))
                        ser(vector.tensor_scalar_add(out=scr[:, :], in0=scr[:, :], scalar1=-1.0))
                        ser(vector.tensor_mul(out=dst[:, :], in0=vsrc, in1=scr[:, :]))
                        ser(vector.tensor_scalar_max(out=dst[:, :], in0=dst[:, :], scalar1=0.25))
                        ser(vector.tensor_mul(out=dst[:, :], in0=dst[:, :], in1=scr[:, :]))
                    vector.engine_nop().then_inc(S("e_mod"), 1)

                emit_ln_apply("lg", SEQ, stg_lg, h_kv, SEQ, g1c, modv[:, 0:16], LN1, blk,
                              apply_waits=(("e_mod", 1),)).close()
                stg_lo = emit_ln_stats("lo", TOWN, ("sbuf", x_res, TOWN), LN1, blk,
                                       stats_waits=(("lg_ap", NIC),),
                                       cast_waits=(("q_xres", 256), ("lg_ap", NIC)))
                emit_ln_apply("lo", TOWN, stg_lo, h1own, TOWN, g1c, modv[:, 0:16], LN1, blk,
                              apply_waits=(("e_mod", 1),)).close()
            LN1.close()

        P2 = contextlib.ExitStack()
        with P2:
            wf = [P2.enter_context(nc.sbuf_tensor(f"wf{i}", [128, 2048], F32)) for i in range(2)]
            wb = [P2.enter_context(nc.sbuf_tensor(f"wbq{i}", [128, 2048], BF16)) for i in range(2)]
            stg = [P2.enter_context(nc.sbuf_tensor(f"stg{i}", [128, 2048], BF16)) for i in range(2)]
            cvi = [P2.enter_context(nc.sbuf_tensor(f"cvi{i}", [128, 1024], I32)) for i in range(2)]
            cvo = [P2.enter_context(nc.sbuf_tensor(f"cvo{i}", [128, 1024], BF16)) for i in range(2)]
            pqk = [P2.enter_context(nc.psum_tensor(f"pqk{i}", [128, 2048], F32)) for i in range(2)]

            jobs = [("k", oc) for oc in range(NH)] + \
                   [("v", oc) for oc in range(NH)] + \
                   [("q", oc) for oc in range(NH)]
            NJOB = len(jobs)
            conv_at = {}
            for k in range(NCONV):
                conv_at.setdefault(k * NJOB // NCONV, []).append(k)

            with nc.Block() as blk:
                @blk.sync
                def _(sync):
                    for t, (m, oc) in enumerate(jobs):
                        if t >= 2:
                            sync.wait_ge(S("p2_cast"), t - 1)
                        sync.dma_start(out=wf[t % 2][:, :], in_=d_w4[m][oc]).then_inc(S("p2_qw"), 16)

                @blk.gpsimd
                def _(gpsimd):
                    def _cstage(k):
                        gpsimd.wait_ge(S("e_conv"), k + 1)
                        _, w, off, bw, coff = conv_tiles[k]
                        blkview = d_win[off:off + 128 * bw].rearrange("(p c) -> p c", p=128)
                        gpsimd.dma_start(out=blkview[:, coff:coff + w],
                                         in_=cvo[k % 2][:, 0:w]).then_inc(S("q_cvout"), 16)

                    for t, (m, oc) in enumerate(jobs):
                        for k in conv_at.get(t, []):
                            if k >= 2:
                                gpsimd.wait_ge(S("e_conv"), k - 1)
                            w = conv_tiles[k][1]
                            gpsimd.dma_start(out=cvi[k % 2][:, 0:w],
                                             in_=conv_tiles[k][0]).then_inc(S("q_cvin"), 16)
                            if k >= 2:
                                _cstage(k - 2)
                        gpsimd.wait_ge(S("p2_ev"), t + 1)
                        if m == "k":
                            dst, T = d_kv[0, oc], SEQ
                        elif m == "v":
                            dst, T = d_kv[1, oc], SEQ
                        else:
                            dst, T = d_q[oc], TOWN
                        gpsimd.dma_start(out=dst, in_=stg[t % 2][:, 0:T]).then_inc(S("p2_st"), 16)
                    _cstage(NCONV - 2)
                    _cstage(NCONV - 1)
                    gpsimd.wait_ge(S("p2_st"), 16 * len(jobs))
                    gpsimd.wait_ge(S("q_cvout"), 16 * NCONV)

                def _p2_evac(scalar, t):
                    m, oc = jobs[t]
                    scalar.wait_ge(S("p2_pe"), t + 1)
                    if t >= 2:
                        scalar.wait_ge(S("p2_st"), 16 * (t - 1))
                    T = TOWN if m == "q" else SEQ
                    sc_in = SCALE if m == "q" else 1.0
                    scalar.activation(out=stg[t % 2][:, 0:T], in_=pqk[t % 2][:, 0:T],
                                      func=AF.Identity, scale=sc_in,
                                      bias=bias_t[m][:, oc:oc + 1]).then_inc(S("p2_ev"), 1)

                @blk.scalar
                def _(scalar):
                    for t, (m, oc) in enumerate(jobs):
                        scalar.wait_ge(S("p2_qw"), 16 * (t + 1))
                        if t >= 2:
                            scalar.wait_ge(S("p2_pe"), t - 1)
                        scalar.copy(out=wb[t % 2][:, :],
                                    in_=wf[t % 2][:, :]).then_inc(S("p2_cast"), 1)
                        if t >= 2:
                            _p2_evac(scalar, t - 2)
                    _p2_evac(scalar, len(jobs) - 2)
                    _p2_evac(scalar, len(jobs) - 1)

                @blk.vector
                def _(vector):
                    vector.wait_ge(S("q_pre"), 48)
                    for t, (m, oc) in enumerate(jobs):
                        for k in conv_at.get(t, []):
                            w = conv_tiles[k][1]
                            vector.wait_ge(S("q_cvin"), 16 * (k + 1))
                            if k >= 2:
                                vector.wait_ge(S("q_cvout"), 16 * (k - 1))
                            o = cvo[k % 2][:, 0:w]
                            i = cvi[k % 2][:, 0:w]
                            vector._custom_dve(LUT_INIT2, out=o, in0=i,
                                               s0=lt[:, 0:1], s1=lt[:, 1:2], imm2=0.0)
                            ins = None
                            for kk in range(1, 8):
                                ins = vector._custom_dve(
                                    LUT_ACC2, out=o, in0=i, in1=o,
                                    s0=lt[:, 2 * kk:2 * kk + 1],
                                    s1=lt[:, 2 * kk + 1:2 * kk + 2],
                                    imm2=float(2 * kk))
                            ins.then_inc(S("e_conv"), 1)

                @blk.tensor
                def _(tensor):
                    for t, (m, oc) in enumerate(jobs):
                        tensor.wait_ge(S("p2_cast"), t + 1)
                        if t >= 2:
                            tensor.wait_ge(S("p2_ev"), t - 1)
                        if m == "q":
                            hsrc, hs, T = h1own, TOWN, TOWN
                        else:
                            hsrc, hs, T = h_kv, SEQ, SEQ
                        ins = None
                        for ic in range(NIC):
                            for j in range(T // 512):
                                ins = tensor.matmul(
                                    pqk[t % 2][:, j * 512:(j + 1) * 512],
                                    wb[t % 2][:, ic * 128:(ic + 1) * 128],
                                    hsrc[:, ic * hs + j * 512:ic * hs + (j + 1) * 512],
                                    start=(ic == 0), stop=(ic == NIC - 1))
                        ins.then_inc(S("p2_pe"), 1)

        P12.close()

        steps = []
        for h in range(NH):
            for k, s in enumerate(SLOT_ORDER):
                steps.append((h, s))
        NTl = [NT[s] for (_, s) in steps]
        NCH = [2 * n for n in NTl]
        NGR = [(c + 7) // 8 for c in NCH]

        LAG = 3
        ptp_order = []
        for h in range(NH):
            ptp_order.append(("v", h, 0))
            ptp_order.append(("v", h, 1))
            for k in range(NSLOT):
                t = h * NSLOT + k
                if t >= LAG:
                    for g in range(NGR[t - LAG]):
                        ptp_order.append(("t", t - LAG, g))
        for t in range(NSTEP - LAG, NSTEP):
            for g in range(NGR[t]):
                ptp_order.append(("t", t, g))
        ptp_pos = {key: i for i, key in enumerate(ptp_order)}
        trav_done = {}
        vtr_done = {}
        for i, key in enumerate(ptp_order):
            if key[0] == "t":
                trav_done[key[1]] = i + 1
            else:
                vtr_done[key[1]] = i + 1

        P3 = contextlib.ExitStack()
        with P3:
            oT = P3.enter_context(nc.sbuf_tensor("oT", [128, NIC * TOWN], BF16))
            o_sb = P3.enter_context(nc.sbuf_tensor("o_sb", [128, NSLOT * D], BF16))
            AT = contextlib.ExitStack()
            with AT:
                jg = AT.enter_context(nc.sbuf_tensor("jg", [128, SEQ], F32))
                kh = [AT.enter_context(nc.sbuf_tensor(f"kh{i}", [128, SEQ], BF16)) for i in range(2)]
                vh = [AT.enter_context(nc.sbuf_tensor(f"vh{i}", [128, SEQ], BF16)) for i in range(2)]
                vt = [AT.enter_context(nc.sbuf_tensor(f"vt{i}", [128, SEQ], BF16)) for i in range(2)]
                qh = [AT.enter_context(nc.sbuf_tensor(f"qh{i}", [128, TOWN], BF16)) for i in range(2)]
                ex = [AT.enter_context(nc.sbuf_tensor(f"ex{i}", [128, 2048], BF16)) for i in range(3)]
                pbf = [AT.enter_context(nc.sbuf_tensor(f"pbf{i}", [128, 2048], BF16)) for i in range(3)]
                pts = [AT.enter_context(nc.sbuf_tensor(f"pts{i}", [128, 2048], BF16)) for i in range(3)]
                mx = AT.enter_context(nc.sbuf_tensor("mx", [128, 4], F32))
                ssum = AT.enter_context(nc.sbuf_tensor("ssum", [128, 4], F32))
                rec = AT.enter_context(nc.sbuf_tensor("rec", [128, 8], F32))
                obuf = AT.enter_context(nc.sbuf_tensor("obuf", [128, 512], F32))
                SCL = AT.enter_context(nc.psum_tensor("SCL", [128, 2048], F32))
                SCS = AT.enter_context(nc.psum_tensor("SCS", [128, 1024], F32))
                ptp = AT.enter_context(nc.psum_tensor("ptp", [128, 1024], BF16))
                ov = AT.enter_context(nc.psum_tensor("ov", [128, 128], F32))

                _fl = {"d": 0, "a": 0}

                def flush_tick(eng, key, sername, ins, sem_name, k=1):
                    _fl[key] += 1
                    ins.then_inc(S(sername), 1)
                    eng.wait_ge(S(sername), _fl[key])
                    eng.nop(nofuse=True).then_inc(S(sem_name), k)

                def lane(t):
                    return SCL if (t % 2 == 0) else SCS

                def emit_trav(tensor, t):
                    h, s = steps[t]
                    tensor.wait_ge(S("p3_st"), t + 1)
                    for g in range(NGR[t]):
                        pos = ptp_pos[("t", t, g)]
                        tensor.wait_ge(S("p3_pte"), pos)
                        ins = None
                        for kk in range(min(8, NCH[t] - g * 8)):
                            kc = g * 8 + kk
                            ins = tensor.transpose(ptp[:, kk * 128:(kk + 1) * 128],
                                                   pbf[t % 3][:, kc * 128:(kc + 1) * 128],
                                                   identb[:, :])
                        ins.then_inc(S("p3_ptw"), 1)
                    tensor.wait_ge(S("p3_pte"), trav_done[t])
                    if t >= 1:
                        tensor.wait_ge(S("a_ovc"), t)
                    ovl = ov[:, :]
                    ins = None
                    for kc in range(NCH[t]):
                        ins = tensor.matmul(ovl,
                                            pts[t % 3][:, kc * 128:(kc + 1) * 128],
                                            vt[h % 2][:, kc * 128:(kc + 1) * 128],
                                            start=(kc == 0), stop=(kc == NCH[t] - 1))
                    ins.then_inc(S("p3_av"), 1)

                def emit_stt(vector, t):
                    h, s = steps[t]
                    W = NT[s] * 256
                    if t == 0:
                        vector.wait_ge(S("e_jg"), 1)
                    vector.wait_ge(S("p3_ex"), t + 1)
                    if t >= LAG:
                        vector.wait_ge(S("p3_ptw"), trav_done[t - LAG])
                    vector.scalar_tensor_tensor(
                        out=pbf[t % 3][:, 0:W], in0=jg[:, 0:W],
                        scalar=scb[:, s:s + 1], in1=ex[t % 3][:, 0:W],
                        op0=ALU.is_le, op1=ALU.mult,
                        accum_out=ssum[:, t % 4:t % 4 + 1]).then_inc(S("p3_st"), 1)

                def emit_rec(vector, t):
                    vector.wait_ge(S("p3_st"), t + 1)
                    vector.reciprocal(out=rec[:, t % 8:t % 8 + 1],
                                      in_=ssum[:, t % 4:t % 4 + 1]).then_inc(S("p3_rc"), 1)

                def emit_oev(vector, t):
                    h, s = steps[t]
                    vector.wait_ge(S("a_ovc"), t + 1)
                    vector.wait_ge(S("p3_rc"), t + 1)
                    vector.tensor_scalar(
                        out=o_sb[:, s * D + h * 128:s * D + (h + 1) * 128],
                        in0=obuf[:, (t % 4) * 128:(t % 4) * 128 + 128],
                        scalar1=rec[:, t % 8:t % 8 + 1],
                        scalar2=None, op0=ALU.mult).then_inc(S("p3_oe"), 1)

                def emit_pte(scalar, t):
                    for g in range(NGR[t]):
                        pos = ptp_pos[("t", t, g)]
                        scalar.wait_ge(S("p3_ptw"), pos + 1)
                        if t >= LAG:
                            scalar.wait_ge(S("p3_av"), t - LAG + 1)
                        w = min(1024, NCH[t] * 128 - g * 1024)
                        scalar.copy(out=pts[t % 3][:, g * 1024:g * 1024 + w],
                                    in_=ptp[:, 0:w]).then_inc(S("p3_pte"), 1)

                with nc.Block() as blk:
                    @blk.gpsimd
                    def _(gpsimd):
                        gpsimd.iota(jg[:, :], pattern=[[1, SEQ]], base=0, channel_multiplier=-1,
                                    allow_small_or_imprecise_dtypes=True).then_inc(S("e_jg"), 1)
                        for h in range(NH):
                            if h >= 2:
                                gpsimd.wait_ge(S("p3_av"), (h - 1) * NSLOT)
                            gpsimd.dma_start(out=kh[h % 2][:, :], in_=d_kv[0, h]).then_inc(S("p3_kv"), 16)
                            gpsimd.dma_start(out=vh[h % 2][:, :], in_=d_kv[1, h]).then_inc(S("p3_kv"), 16)
                            gpsimd.dma_start(out=qh[h % 2][:, :], in_=d_q[h]).then_inc(S("p3_kv"), 16)
                            if DEBUG_DUMP and h == 1:
                                gpsimd.wait_ge(S("p3_rc"), 2)
                                gpsimd.dma_start(out=d_dbg_at[:, 0:4], in_=ssum[:, :]).then_inc(S("q_dba"), 16)
                                gpsimd.dma_start(out=d_dbg_at[:, 4:12], in_=rec[:, :]).then_inc(S("q_dba"), 16)
                                gpsimd.dma_start(out=d_dbg_at[:, 12:16], in_=mx[:, :]).then_inc(S("q_dba"), 16)
                                gpsimd.dma_start(out=d_dbg_ab[:, 0:256],
                                                 in_=ex[1][:, 0:256]).then_inc(S("q_dba"), 16)
                                gpsimd.dma_start(out=d_dbg_ab[:, 256:512],
                                                 in_=pbf[1][:, 0:256]).then_inc(S("q_dba"), 16)
                                gpsimd.dma_start(out=d_dbg_ab[:, 512:1536],
                                                 in_=qh[0][:, :]).then_inc(S("q_dba"), 16)
                                gpsimd.wait_ge(S("p3_oe"), 2)
                                gpsimd.dma_start(out=d_dbg_at[:, 16:144],
                                                 in_=obuf[:, 128:256]).then_inc(S("q_dba"), 16)
                                gpsimd.dma_start(out=d_dbg_ab[:, 1536:1664],
                                                 in_=o_sb[:, 0:128]).then_inc(S("q_dba"), 16)
                                gpsimd.wait_ge(S("q_dba"), 128)
                            if h == 6:
                                gpsimd.wait_ge(S("q_cvout"), 16 * NCONV)
                                gpsimd.collective_compute(
                                    "AllGather", ALU.bypass, replica_groups=[list(range(8))],
                                    ins=[d_win.opt()], outs=[d_wout.opt()]).then_inc(S("c_w"), 1)

                    @blk.tensor
                    def _(tensor):
                        for h in range(NH):
                            tensor.wait_ge(S("p3_kv"), 48 * (h + 1))
                            for g in range(2):
                                pos = ptp_pos[("v", h, g)]
                                if pos >= 1:
                                    tensor.wait_ge(S("p3_pte"), pos)
                                ins = None
                                for kk in range(8):
                                    kc = g * 8 + kk
                                    ins = tensor.transpose(ptp[:, kk * 128:(kk + 1) * 128],
                                                           vh[h % 2][:, kc * 128:(kc + 1) * 128],
                                                           identb[:, :])
                                ins.then_inc(S("p3_ptw"), 1)
                            for k in range(NSLOT):
                                t = h * NSLOT + k
                                s = steps[t][1]
                                SC = lane(t)
                                if t >= 2:
                                    tensor.wait_ge(S("p3_ex"), t - 1)
                                ins = None
                                for kt in range(NT[s]):
                                    ins = tensor.matmul(SC[:, kt * 256:(kt + 1) * 256],
                                                        qh[h % 2][:, s * 128:(s + 1) * 128],
                                                        kh[h % 2][:, kt * 256:(kt + 1) * 256],
                                                        start=True, stop=True)
                                ins.then_inc(S("p3_sc"), 1)
                                if t >= LAG:
                                    emit_trav(tensor, t - LAG)
                        for tl in range(NSTEP - LAG, NSTEP):
                            emit_trav(tensor, tl)

                    @blk.vector
                    def _(vector):
                        for t in range(NSTEP):
                            h, s = steps[t]
                            SC = lane(t)
                            W = NT[s] * 256
                            vector.wait_ge(S("p3_sc"), t + 1)
                            if t >= 4:
                                vector.wait_ge(S("p3_ex"), t - 3)
                            vector.reduce_max(mx[:, t % 4:t % 4 + 1], SC[:, 0:W],
                                              axis=mybir.AxisListType.X,
                                              negate=True).then_inc(S("p3_mb"), 1)
                            if t >= 3:
                                emit_oev(vector, t - 3)
                            if t >= 1:
                                emit_stt(vector, t - 1)
                                emit_rec(vector, t - 1)
                        emit_stt(vector, NSTEP - 1)
                        emit_rec(vector, NSTEP - 1)
                        for tl in range(NSTEP - 3, NSTEP):
                            emit_oev(vector, tl)

                    @blk.scalar
                    def _(scalar):
                        for h in range(NH):
                            if h >= 2:
                                scalar.wait_ge(S("p3_av"), (h - 1) * NSLOT)
                            for g in range(2):
                                pos = ptp_pos[("v", h, g)]
                                scalar.wait_ge(S("p3_ptw"), pos + 1)
                                scalar.copy(out=vt[h % 2][:, g * 1024:(g + 1) * 1024],
                                            in_=ptp[:, :]).then_inc(S("p3_pte"), 1)
                            for k in range(NSLOT):
                                t = h * NSLOT + k
                                s = steps[t][1]
                                W = NT[s] * 256
                                SC = lane(t)
                                if t >= LAG:
                                    emit_pte(scalar, t - LAG)
                                if t >= 3:
                                    tov = t - 3
                                    scalar.wait_ge(S("p3_av"), tov + 1)
                                    if tov >= 4:
                                        scalar.wait_ge(S("p3_oe"), tov - 3)
                                    scalar.copy(out=obuf[:, (tov % 4) * 128:(tov % 4) * 128 + 128],
                                                in_=ov[:, :]).then_inc(S("a_ovc"), 1)
                                scalar.wait_ge(S("p3_mb"), t + 1)
                                if t >= 3:
                                    scalar.wait_ge(S("p3_st"), t - 2)
                                scalar.activation(out=ex[t % 3][:, 0:W], in_=SC[:, 0:W],
                                                  func=AF.Exp, bias=mx[:, t % 4:t % 4 + 1],
                                                  scale=1.0).then_inc(S("p3_ex"), 1)
                        for tl in range(NSTEP - LAG, NSTEP):
                            emit_pte(scalar, tl)
                            tov = tl
                            scalar.wait_ge(S("p3_av"), tov + 1)
                            scalar.copy(out=obuf[:, (tov % 4) * 128:(tov % 4) * 128 + 128],
                                        in_=ov[:, :]).then_inc(S("a_ovc"), 1)

            OP = contextlib.ExitStack()
            with OP:
                wof = OP.enter_context(nc.sbuf_tensor("wof", [128, 2048], F32))
                wob = [OP.enter_context(nc.sbuf_tensor(f"wob{i}", [128, 2048], BF16)) for i in range(2)]
                atmp = [OP.enter_context(nc.sbuf_tensor(f"atmp{i}", [128, TOWN], F32)) for i in range(2)]
                pot = OP.enter_context(nc.psum_tensor("pot", [128, 1024], BF16))
                po = [OP.enter_context(nc.psum_tensor(f"po{i}", [128, TOWN], F32)) for i in range(2)]

                with nc.Block() as blk:
                    @blk.sync
                    def _(sync):
                        for oc in range(NIC):
                            if oc >= 1:
                                sync.wait_ge(S("p3o_cast"), oc)
                            sync.dma_start(out=wof[:, :], in_=d_w4["o"][oc]).then_inc(S("p3o_qw"), 16)

                    @blk.gpsimd
                    def _(gpsimd):
                        for oc in range(NIC):
                            gpsimd.wait_ge(S("p3o_qw"), 16 * (oc + 1))
                            if oc >= 2:
                                gpsimd.wait_ge(S("p3o_pe"), oc - 1)
                            gpsimd.tensor_copy(out=wob[oc % 2][:, :],
                                               in_=wof[:, :]).then_inc(S("p3o_cast"), 1)

                    @blk.tensor
                    def _(tensor):
                        for ic in range(NIC):
                            if ic >= 1:
                                tensor.wait_ge(S("p3o_te"), ic)
                            ins = None
                            for s in range(NSLOT):
                                ins = tensor.transpose(pot[:, s * 128:(s + 1) * 128],
                                                       o_sb[:, s * D + ic * 128:s * D + (ic + 1) * 128],
                                                       identb[:, :])
                            ins.then_inc(S("p3o_tr"), 1)

                    @blk.scalar
                    def _(scalar):
                        for ic in range(NIC):
                            scalar.wait_ge(S("p3o_tr"), ic + 1)
                            scalar.copy(out=oT[:, ic * TOWN:(ic + 1) * TOWN],
                                        in_=pot[:, :]).then_inc(S("p3o_te"), 1)

                    @blk.tensor
                    def _(tensor):
                        tensor.wait_ge(S("p3o_te"), NIC)
                        for oc in range(NIC):
                            tensor.wait_ge(S("p3o_cast"), oc + 1)
                            if oc >= 2:
                                tensor.wait_ge(S("p3o_ev"), oc - 1)
                            ins = None
                            for ic in range(NIC):
                                for j in range(TOWN // 512):
                                    ins = tensor.matmul(
                                        po[oc % 2][:, j * 512:(j + 1) * 512],
                                        wob[oc % 2][:, ic * 128:(ic + 1) * 128],
                                        oT[:, ic * TOWN + j * 512:ic * TOWN + (j + 1) * 512],
                                        start=(ic == 0), stop=(ic == NIC - 1))
                            ins.then_inc(S("p3o_pe"), 1)

                    @blk.scalar
                    def _(scalar):
                        for oc in range(NIC):
                            scalar.wait_ge(S("p3o_pe"), oc + 1)
                            if oc >= 2:
                                scalar.wait_ge(S("p3o_res"), oc - 1)
                            scalar.activation(out=atmp[oc % 2][:, :], in_=po[oc % 2][:, :],
                                              func=AF.Identity,
                                              bias=bias_t["o"][:, oc:oc + 1]).then_inc(S("p3o_ev"), 1)

                    @blk.vector
                    def _(vector):
                        for oc in range(NIC):
                            vector.wait_ge(S("p3o_ev"), oc + 1)
                            vector.scalar_tensor_tensor(
                                out=x_res[:, oc * TOWN:(oc + 1) * TOWN],
                                in0=atmp[oc % 2][:, :], scalar=gac[:, oc:oc + 1],
                                in1=x_res[:, oc * TOWN:(oc + 1) * TOWN],
                                op0=ALU.mult, op1=ALU.add).then_inc(S("p3o_res"), 1)

        FG = contextlib.ExitStack()
        with FG:
            g_sb = FG.enter_context(nc.sbuf_tensor("g_sb", [128, 43 * TOWN], BF16))
            FH = contextlib.ExitStack()
            with FH:
                h2 = FH.enter_context(nc.sbuf_tensor("h2", [128, NIC * TOWN], BF16))
                LN2 = contextlib.ExitStack()
                with LN2:
                    with nc.Block() as blk2:
                        stg_l2 = emit_ln_stats("l2", TOWN, ("sbuf", x_res, TOWN), LN2, blk2)
                        emit_ln_apply("l2", TOWN, stg_l2, h2, TOWN, g2c,
                                      modv[:, 48:64], LN2, blk2).close()

                GU = contextlib.ExitStack()
                with GU:
                    wgb = [GU.enter_context(nc.sbuf_tensor(f"wgb{i}", [128, 2048], BF16)) for i in range(2)]
                    wub = [GU.enter_context(nc.sbuf_tensor(f"wub{i}", [128, 2048], BF16)) for i in range(2)]
                    sgf = GU.enter_context(nc.sbuf_tensor("sgf", [128, TOWN], F32))
                    pg = [GU.enter_context(nc.psum_tensor(f"pg{i}", [128, TOWN], F32)) for i in range(2)]
                    pu = [GU.enter_context(nc.psum_tensor(f"pu{i}", [128, TOWN], F32)) for i in range(2)]

                    with nc.Block() as blk:
                        @blk.sync
                        def _(sync):
                            sync.wait_ge(S("c_w"), 1)
                            for oc in range(43):
                                if oc >= 2:
                                    sync.wait_ge(S("p5_pe"), oc - 1)
                                if DEBUG_DUMP and oc == 2:
                                    sync.wait_ge(S("q_dbw"), 64)
                                sync.dma_start(out=wgb[oc % 2][:, :],
                                               in_=wout_gate(oc)).then_inc(S("p5_qw"), 16)
                                sync.dma_start(out=wub[oc % 2][:, :],
                                               in_=wout_up(oc)).then_inc(S("p5_qw"), 16)

                        if DEBUG_DUMP:
                            @blk.gpsimd
                            def _(gpsimd):
                                gpsimd.wait_ge(S("p5_qw"), 32)
                                gpsimd.dma_start(out=d_dbg_w[0][:, 0:2048],
                                                 in_=wgb[0][:, :]).then_inc(S("q_dbw"), 16)
                                gpsimd.dma_start(out=d_dbg_w[0][:, 2048:4096],
                                                 in_=wub[0][:, :]).then_inc(S("q_dbw"), 16)
                                gpsimd.dma_start(out=d_dbg_w[1][:, 0:2048],
                                                 in_=wout_gate(0)).then_inc(S("q_dbw"), 16)
                                gpsimd.dma_start(out=d_dbg_w[1][:, 2048:4096],
                                                 in_=wout_up(0)).then_inc(S("q_dbw"), 16)
                                gpsimd.dma_start(out=d_dbg_h2a[:, :],
                                                 in_=h2[:, :]).then_inc(S("q_dbh"), 16)
                                gpsimd.wait_ge(S("p5_si"), 1)
                                gpsimd.dma_start(out=d_dbg_sgf[:, :],
                                                 in_=sgf[:, :]).then_inc(S("q_dbs"), 16)
                                gpsimd.wait_ge(S("p5_pe"), 6)
                                gpsimd.dma_start(out=d_dbg_h2b[:, :],
                                                 in_=h2[:, :]).then_inc(S("q_dbh"), 16)
                                gpsimd.wait_ge(S("q_dbw"), 64)

                        @blk.tensor
                        def _(tensor):
                            tensor.wait_ge(S("p5_qw"), 32)
                            for ic in range(NIC):
                                for j in range(TOWN // 512):
                                    sl = slice(j * 512, (j + 1) * 512)
                                    hsl = h2[:, ic * TOWN + j * 512:ic * TOWN + (j + 1) * 512]
                                    tensor.matmul(pg[0][:, sl],
                                                  wgb[0][:, ic * 128:(ic + 1) * 128], hsl,
                                                  start=(ic == 0), stop=(ic == NIC - 1))
                                    tensor.matmul(pu[0][:, sl],
                                                  wub[0][:, ic * 128:(ic + 1) * 128], hsl,
                                                  start=(ic == 0), stop=(ic == NIC - 1))
                            for oc in range(43):
                                tensor.wait_ge(S("p5_qw"), 32 * (oc + 1))
                                if oc >= 2:
                                    tensor.wait_ge(S("p5_gm"), oc - 1)
                                ins = None
                                for ic in range(NIC):
                                    for j in range(TOWN // 512):
                                        sl = slice(j * 512, (j + 1) * 512)
                                        hsl = h2[:, ic * TOWN + j * 512:ic * TOWN + (j + 1) * 512]
                                        tensor.matmul(pg[oc % 2][:, sl],
                                                      wgb[oc % 2][:, ic * 128:(ic + 1) * 128], hsl,
                                                      start=(ic == 0), stop=(ic == NIC - 1))
                                        ins = tensor.matmul(pu[oc % 2][:, sl],
                                                            wub[oc % 2][:, ic * 128:(ic + 1) * 128], hsl,
                                                            start=(ic == 0), stop=(ic == NIC - 1))
                                ins.then_inc(S("p5_pe"), 1)

                        @blk.scalar
                        def _(scalar):
                            for oc in range(43):
                                scalar.wait_ge(S("p5_pe"), oc + 1)
                                if DEBUG_DUMP and oc == 1:
                                    scalar.wait_ge(S("q_dbs"), 16)
                                if oc >= 2:
                                    scalar.wait_ge(S("p5_gm"), oc - 1)
                                if oc >= 1:
                                    scalar.wait_ge(S("p5_gm"), oc)
                                scalar.activation(out=sgf[:, :], in_=pg[oc % 2][:, :],
                                                  func=AF.Silu).then_inc(S("p5_si"), 1)

                        @blk.vector
                        def _(vector):
                            for oc in range(43):
                                vector.wait_ge(S("p5_si"), oc + 1)
                                vector.tensor_mul(out=g_sb[:, oc * TOWN:(oc + 1) * TOWN],
                                                  in0=sgf[:, :],
                                                  in1=pu[oc % 2][:, :]).then_inc(S("p5_gm"), 1)

            DN = contextlib.ExitStack()
            with DN:
                wdb = [DN.enter_context(nc.sbuf_tensor(f"wdb{i}", [128, 5504], BF16)) for i in range(2)]
                ysb = [DN.enter_context(nc.sbuf_tensor(f"ysb{i}", [128, TOWN], F32)) for i in range(2)]
                pd = [DN.enter_context(nc.psum_tensor(f"pd{i}", [128, TOWN], F32)) for i in range(2)]

                with nc.Block() as blk:
                    @blk.sync
                    def _(sync):
                        for oc in range(NIC):
                            if oc >= 2:
                                sync.wait_ge(S("p6_pe"), oc - 1)
                            sync.dma_start(out=wdb[oc % 2][:, :],
                                           in_=wout_down(oc)).then_inc(S("p6_qw"), 16)

                    @blk.tensor
                    def _(tensor):
                        for oc in range(NIC):
                            tensor.wait_ge(S("p6_qw"), 16 * (oc + 1))
                            if oc >= 2:
                                tensor.wait_ge(S("p6_ye"), oc - 1)
                            ins = None
                            for ic in range(43):
                                for j in range(TOWN // 512):
                                    ins = tensor.matmul(
                                        pd[oc % 2][:, j * 512:(j + 1) * 512],
                                        wdb[oc % 2][:, ic * 128:(ic + 1) * 128],
                                        g_sb[:, ic * TOWN + j * 512:ic * TOWN + (j + 1) * 512],
                                        start=(ic == 0), stop=(ic == 42))
                            ins.then_inc(S("p6_pe"), 1)

                    @blk.vector
                    def _(vector):
                        for oc in range(NIC):
                            vector.wait_ge(S("p6_pe"), oc + 1)
                            if oc >= 2:
                                vector.wait_ge(S("p6_qy"), 16 * (oc - 1))
                            vector.scalar_tensor_tensor(
                                out=ysb[oc % 2][:, :], in0=pd[oc % 2][:, :],
                                scalar=gbc[:, oc:oc + 1],
                                in1=x_res[:, oc * TOWN:(oc + 1) * TOWN],
                                op0=ALU.mult, op1=ALU.add).then_inc(S("p6_ye"), 1)

                    @blk.gpsimd
                    def _(gpsimd):
                        for oc in range(NIC):
                            gpsimd.wait_ge(S("p6_ye"), oc + 1)
                            gpsimd.dma_start(out=d_y[oc],
                                             in_=ysb[oc % 2][:, :]).then_inc(S("p6_qy"), 16)
                        gpsimd.wait_ge(S("p6_qy"), 16 * NIC)
                        if DEBUG_DUMP:
                            gpsimd.dma_start(out=d_dbg_modv[:, :], in_=modv[:, :]).then_inc(S("q_dbg"), 16)
                            gpsimd.dma_start(out=d_dbg_vec[:, 0:16], in_=g1c[:, :]).then_inc(S("q_dbg"), 16)
                            gpsimd.dma_start(out=d_dbg_vec[:, 16:32], in_=g2c[:, :]).then_inc(S("q_dbg"), 16)
                            gpsimd.dma_start(out=d_dbg_vec[:, 32:48], in_=gac[:, :]).then_inc(S("q_dbg"), 16)
                            gpsimd.dma_start(out=d_dbg_vec[:, 48:64], in_=gbc[:, :]).then_inc(S("q_dbg"), 16)
                            for ic in range(NIC):
                                gpsimd.dma_start(out=d_dbg_xres[ic],
                                                 in_=x_res[:, ic * TOWN:(ic + 1) * TOWN]).then_inc(S("q_dbg"), 16)
                            for ocg in range(43):
                                gpsimd.dma_start(out=d_dbg_g[ocg],
                                                 in_=g_sb[:, ocg * TOWN:(ocg + 1) * TOWN]).then_inc(S("q_dbg"), 16)
                            gpsimd.wait_ge(S("q_dbg"), 16 * (5 + NIC + 43))

    for i in nc.all_instructions():
        if isinstance(i, mybir.InstCustomDveAnt) and len(i.instr) == 0:
            mybir.codegen_inst_isa_one(i, nc, nc.isa)
    return nc


def _prep_shared(inputs):
    f32 = np.float32
    sh = {}
    for m, wkey in (("q", "L0_attn_Wq"), ("k", "L0_attn_Wk"),
                    ("v", "L0_attn_Wv"), ("o", "L0_attn_Wo")):
        W = np.asarray(inputs[wkey], dtype=f32)
        sh[f"w4{m}"] = np.ascontiguousarray(
            W.reshape(16, 128, 16, 128).transpose(0, 3, 2, 1).reshape(16, 128, 2048))
    for m, bkey in (("q", "L0_attn_bq"), ("k", "L0_attn_bk"),
                    ("v", "L0_attn_bv"), ("o", "L0_attn_bo")):
        bvec = np.asarray(inputs[bkey], dtype=f32)
        sh[f"b2{m}"] = np.ascontiguousarray(bvec.reshape(16, 128).T)
    sh["ctr"] = np.ascontiguousarray(np.asarray(inputs["L0_c"], dtype=f32).T)
    sh["lutv"] = np.ascontiguousarray(np.asarray(inputs["L0_ffn_lut"], dtype=f32))
    gidx = np.asarray(inputs["L0_ffn_gate_idx"], dtype=np.int32)
    uidx = np.asarray(inputs["L0_ffn_up_idx"], dtype=np.int32)
    didx = np.asarray(inputs["L0_ffn_down_idx"], dtype=np.int32)
    gpad = np.concatenate([gidx, gidx[-128:]], axis=0)
    upad = np.concatenate([uidx, uidx[-128:]], axis=0)
    sh["_gA"] = np.ascontiguousarray(
        gpad.reshape(44, 128, 16, 128).transpose(0, 3, 2, 1).reshape(44, 128, 2048))
    sh["_uA"] = np.ascontiguousarray(
        upad.reshape(44, 128, 16, 128).transpose(0, 3, 2, 1).reshape(44, 128, 2048))
    sh["_dA"] = np.ascontiguousarray(
        didx.reshape(16, 128, 43, 128).transpose(0, 3, 2, 1).reshape(16, 128, 5504))
    sh["wmod_f"] = np.asarray(inputs["L0_wmod"], dtype=f32)
    sh["x_f"] = np.asarray(inputs["L0_attn_h"], dtype=f32)
    return sh


def _in_map_for_core(c, sh):
    b, a = divmod(c, 2)
    blocks = A_BLOCKS if a == 0 else B_BLOCKS
    xT = np.ascontiguousarray(sh["x_f"][b].T)
    xown = np.concatenate([xT[:, blk * 128:(blk + 1) * 128] for blk in blocks], axis=1)
    gsel = [GATE_PAD[c][j] for j in range(6)]
    m = {
        "xown": np.ascontiguousarray(xown.reshape(16, 128, TOWN)),
        "xfull": np.ascontiguousarray(xT.reshape(16, 128, SEQ)),
        "ctr": sh["ctr"],
        "wmods": np.ascontiguousarray(sh["wmod_f"][:, c * 1536:(c + 1) * 1536]),
        "lutv": sh["lutv"],
        "gidx": np.ascontiguousarray(sh["_gA"][gsel]),
        "uidx": np.ascontiguousarray(sh["_uA"][gsel]),
        "didx": np.ascontiguousarray(sh["_dA"][[2 * c, 2 * c + 1]]),
        "scv": np.asarray([blk * 128 for blk in blocks], dtype=np.float32),
        "ebv": np.eye(NB, dtype=np.float32)[b],
    }
    for mm in "qkvo":
        m[f"w4{mm}"] = sh[f"w4{mm}"]
        m[f"b2{mm}"] = sh[f"b2{mm}"]
    return m


_NC_CACHE = {}


def kernel(**inputs):
    sh = _prep_shared(inputs)
    in_maps = [_in_map_for_core(c, sh) for c in range(8)]
    if "nc" not in _NC_CACHE:
        _NC_CACHE["nc"] = build_program()
    nc = _NC_CACHE["nc"]
    res = run_bass_kernel_spmd(nc, in_maps, core_ids=list(range(8)))
    out = np.empty((NB, SEQ, D), dtype=np.float32)
    for c in range(8):
        b, a = divmod(c, 2)
        blocks = A_BLOCKS if a == 0 else B_BLOCKS
        yc = np.asarray(res.results[c]["y"]).reshape(D, TOWN)
        for s, blk in enumerate(blocks):
            out[b, blk * 128:(blk + 1) * 128, :] = yc[:, s * 128:(s + 1) * 128].T
    return out


# version 22
# speedup vs baseline: 1.0433x; 1.0074x over previous
import contextlib
import math

import numpy as np

import concourse.bass as bass
import concourse.mybir as mybir
from concourse.bass_utils import run_bass_kernel_spmd
import concourse.dve_ops as dve_ops
from concourse.dve_op import DveOp
from concourse.dve_spec import Spec, Src0, Src1, C0, C1, C2, One, eq, lower as dve_lower

F32 = mybir.dt.float32
BF16 = mybir.dt.bfloat16
I32 = mybir.dt.int32
AF = mybir.ActivationFunctionType
ALU = mybir.AluOpType

D = 2048
NH = 16
HD = 128
SEQ = 2048
NB = 4
INTER = 5504
NIC = 16
TOWN = 1024
NSLOT = 8
SCALE = 1.0 / math.sqrt(HD)
EPS = 1.0e-5

A_BLOCKS = [0, 3, 4, 7, 8, 11, 12, 15]
B_BLOCKS = [1, 2, 5, 6, 9, 10, 13, 14]
NT = [1, 2, 3, 4, 5, 6, 7, 8]
SLOT_ORDER = [7, 0, 6, 1, 5, 2, 4, 3]
NSTEP = NH * NSLOT
DEBUG_DUMP = False


def _gate_assign():
    real, g = [], 0
    for r in range(8):
        n = 6 if r < 3 else 5
        real.append(list(range(g, g + n)))
        g += n
    assert g == 43
    padded = [bl + [bl[-1]] * (6 - len(bl)) for bl in real]
    inv = {oc: (r, j) for r, bl in enumerate(real) for j, oc in enumerate(bl)}
    return padded, inv


GATE_PAD, GATE_INV = _gate_assign()
DOWN_INV = {oc: (oc // 2, oc % 2) for oc in range(16)}

GBLK = 128 * 2048
DBLK = 128 * 5504
W_IN_ELEMS = 12 * GBLK + 2 * DBLK
OFF_UP = 6 * GBLK
OFF_DN = 12 * GBLK


def _mk_op(name, body, reference):
    for op in dve_ops.OPS:
        if op.name == name:
            return op
    shas = {}
    for ver in ("v3", "v4"):
        shas[ver] = dve_lower(Spec(body=body), ver=ver, perf_en=False, name=name).sha(ver)
    op = DveOp(name, Spec(body=body), subdim=False, uops_sha=shas, reference=reference)
    dve_ops.OPS.append(op)
    dve_ops._SUB_OPCODE_FOR_NAME[name] = dve_ops._CUSTOM_DVE_ROW_BASE + len(dve_ops.OPS) - 1
    dve_ops.CUSTOM_DVE_SPECS[name] = op.spec
    dve_ops.CUSTOM_DVE_REFS[name] = op.ref
    return op


LUT_INIT2 = _mk_op(
    "LUT_INIT2",
    C0 * eq(Src0, C2) + C1 * eq(Src0, C2 + One),
    lambda in0, in1, s0, s1, imm2: s0 * (in0 == imm2) + s1 * (in0 == imm2 + 1.0),
)
LUT_ACC2 = _mk_op(
    "LUT_ACC2",
    Src1 + C0 * eq(Src0, C2) + C1 * eq(Src0, C2 + One),
    lambda in0, in1, s0, s1, imm2: in1 + s0 * (in0 == imm2) + s1 * (in0 == imm2 + 1.0),
)


_SER_STATE = [0]


def build_program():
    nc = bass.Bass("TRN2", target_bir_lowering=False, detect_race_conditions=False)

    d_xown = nc.dram_tensor("xown", [NIC, 128, TOWN], F32, kind="ExternalInput").ap()
    d_xfull = nc.dram_tensor("xfull", [NIC, 128, SEQ], F32, kind="ExternalInput").ap()
    d_ct = nc.dram_tensor("ctr", [D, NB], F32, kind="ExternalInput").ap()
    d_wmod = nc.dram_tensor("wmods", [D, 1536], F32, kind="ExternalInput").ap()
    d_lut = nc.dram_tensor("lutv", [16], F32, kind="ExternalInput").ap()
    d_w4 = {m: nc.dram_tensor(f"w4{m}", [16, 128, D], F32, kind="ExternalInput").ap()
            for m in "qkvo"}
    d_b2 = {m: nc.dram_tensor(f"b2{m}", [128, 16], F32, kind="ExternalInput").ap()
            for m in "qkvo"}
    d_gidx = nc.dram_tensor("gidx", [6, 128, 2048], I32, kind="ExternalInput").ap()
    d_uidx = nc.dram_tensor("uidx", [6, 128, 2048], I32, kind="ExternalInput").ap()
    d_didx = nc.dram_tensor("didx", [2, 128, 5504], I32, kind="ExternalInput").ap()
    d_sc = nc.dram_tensor("scv", [8], F32, kind="ExternalInput").ap()
    d_eb = nc.dram_tensor("ebv", [NB], F32, kind="ExternalInput").ap()
    d_y = nc.dram_tensor("y", [NIC, 128, TOWN], F32, kind="ExternalOutput").ap()

    d_modin = nc.dram_tensor("modin", [128, 48], F32).ap()
    d_modout = nc.dram_tensor("modout", [8, 128, 48], F32).ap()
    d_win = nc.dram_tensor("win", [W_IN_ELEMS], BF16).ap()
    d_wout = nc.dram_tensor("wout", [8 * W_IN_ELEMS], BF16, addr_space="Shared").ap()
    d_kv = nc.dram_tensor("kvs", [2, NH, 128, SEQ], BF16).ap()
    d_wobf = nc.dram_tensor("wobf", [NIC, 128, 2048], BF16).ap()
    d_q = nc.dram_tensor("qsc", [NH, 128, TOWN], BF16).ap()
    if DEBUG_DUMP:
        d_dbg_modv = nc.dram_tensor("dbg_modv", [128, 96], F32, kind="ExternalOutput").ap()
        d_dbg_vec = nc.dram_tensor("dbg_vec", [128, 64], F32, kind="ExternalOutput").ap()
        d_dbg_xres = nc.dram_tensor("dbg_xres", [NIC, 128, TOWN], F32, kind="ExternalOutput").ap()
        d_dbg_g = nc.dram_tensor("dbg_g", [43, 128, TOWN], BF16, kind="ExternalOutput").ap()
        d_dbg_w = nc.dram_tensor("dbg_w", [2, 128, 4096], BF16, kind="ExternalOutput").ap()
        d_dbg_h2a = nc.dram_tensor("dbg_h2a", [128, 16384], BF16, kind="ExternalOutput").ap()
        d_dbg_h2b = nc.dram_tensor("dbg_h2b", [128, 16384], BF16, kind="ExternalOutput").ap()
        d_dbg_sgf = nc.dram_tensor("dbg_sgf", [128, TOWN], F32, kind="ExternalOutput").ap()
        d_dbg_at = nc.dram_tensor("dbg_at", [128, 144], F32, kind="ExternalOutput").ap()
        d_dbg_ab = nc.dram_tensor("dbg_ab", [128, 1664], BF16, kind="ExternalOutput").ap()

    def wout_slice(base_off, blk_elems):
        return d_wout[base_off:base_off + blk_elems].rearrange("(p c) -> p c", p=128)

    def wout_gate(oc):
        r, j = GATE_INV[oc]
        return wout_slice(r * W_IN_ELEMS + j * GBLK, GBLK)

    def wout_up(oc):
        r, j = GATE_INV[oc]
        return wout_slice(r * W_IN_ELEMS + OFF_UP + j * GBLK, GBLK)

    def wout_down(oc):
        r, j = DOWN_INV[oc]
        return wout_slice(r * W_IN_ELEMS + OFF_DN + j * DBLK, DBLK)

    ES = contextlib.ExitStack()
    with ES:
        sem = {}

        def S(name):
            if name not in sem:
                sem[name] = ES.enter_context(nc.semaphore(name))
            return sem[name]

        x_res = ES.enter_context(nc.sbuf_tensor("x_res", [128, NIC * TOWN], F32))
        identb = ES.enter_context(nc.sbuf_tensor("identb", [128, 128], BF16))
        identf = ES.enter_context(nc.sbuf_tensor("identf", [128, 128], F32))
        lt = ES.enter_context(nc.sbuf_tensor("lt", [128, 16], F32))
        scb = ES.enter_context(nc.sbuf_tensor("scb", [128, 8], F32))
        ebb = ES.enter_context(nc.sbuf_tensor("ebb", [128, NB], F32))
        bias_t = {m: ES.enter_context(nc.sbuf_tensor(f"bias_{m}", [128, 16], F32))
                  for m in "qkvo"}
        modv = ES.enter_context(nc.sbuf_tensor("modv", [128, 96], F32))
        g1c = ES.enter_context(nc.sbuf_tensor("g1c", [128, 16], F32))
        g2c = ES.enter_context(nc.sbuf_tensor("g2c", [128, 16], F32))
        gac = ES.enter_context(nc.sbuf_tensor("gac", [128, 16], F32))
        gbc = ES.enter_context(nc.sbuf_tensor("gbc", [128, 16], F32))
        onesb = ES.enter_context(nc.sbuf_tensor("onesb", [128, 1], BF16))
        onesf1 = ES.enter_context(nc.sbuf_tensor("onesf1", [1, 128], F32))
        onesf1b = ES.enter_context(nc.sbuf_tensor("onesf1b", [1, 128], BF16))
        epsr = ES.enter_context(nc.sbuf_tensor("epsr", [1, 1], F32))
        scr = ES.enter_context(nc.sbuf_tensor("scr", [128, 16], F32))

        SA = modv[:, 0:16]
        CA = modv[:, 16:32]
        GA = modv[:, 32:48]
        SB_ = modv[:, 48:64]
        CB = modv[:, 64:80]
        GB = modv[:, 80:96]

        conv_tiles = []
        for k in range(6):
            for p2_ in range(2):
                conv_tiles.append((d_gidx[k][:, 1024 * p2_:1024 * (p2_ + 1)], 1024,
                                   k * GBLK, 2048, 1024 * p2_))
        for k in range(6):
            for p2_ in range(2):
                conv_tiles.append((d_uidx[k][:, 1024 * p2_:1024 * (p2_ + 1)], 1024,
                                   OFF_UP + k * GBLK, 2048, 1024 * p2_))
        for k in range(2):
            off = 0
            while off < 5504:
                w = min(1024, 5504 - off)
                conv_tiles.append((d_didx[k][:, off:off + w], w,
                                   OFF_DN + k * DBLK, 5504, off))
                off += w
        NCONV = len(conv_tiles)

        def emit_ln_stats(tag, T, src_spec, LN, blk, stats_waits=(), cast_waits=(),
                          dma_engine=None):
            kind = src_spec[0]
            NJ = T // 512
            PSTAT = contextlib.ExitStack()
            ps_sx = PSTAT.enter_context(nc.psum_tensor(f"{tag}sx", [1, T], F32))
            ps_sxx = PSTAT.enter_context(nc.psum_tensor(f"{tag}sxx", [1, T], F32))
            sxxs = LN.enter_context(nc.sbuf_tensor(f"{tag}sxxs", [1, T], F32))
            mir = LN.enter_context(nc.sbuf_tensor(f"{tag}mir", [1, T], BF16))
            xf = None
            if kind == "dram":
                xf = [LN.enter_context(nc.sbuf_tensor(f"{tag}xf{i}", [128, T], F32)) for i in range(2)]
            xb = [PSTAT.enter_context(nc.sbuf_tensor(f"{tag}xb{i}", [128, T], BF16)) for i in range(2)]
            xq = PSTAT.enter_context(nc.sbuf_tensor(f"{tag}xq", [128, T], BF16))

            def lsrc(ic):
                if kind == "dram":
                    return xf[ic % 2][:, :]
                _, t_sb, stride = src_spec
                return t_sb[:, ic * stride:ic * stride + T]

            if kind == "dram":
                def _dma_pass1(eng):
                    dap = src_spec[1]
                    for t in range(NIC):
                        if t >= 2:
                            eng.wait_ge(S(f"{tag}_vc"), t - 1)
                            eng.wait_ge(S(f"{tag}_ac"), t - 1)
                        eng.dma_start(out=xf[t % 2][:, :],
                                      in_=dap[t]).then_inc(S(f"{tag}_qx"), 16)
                def _dma_pass2(eng):
                    for t in range(NIC):
                        tg = NIC + t
                        if t < 2:
                            eng.wait_ge(S(f"{tag}_vc"), NIC)
                            eng.wait_ge(S(f"{tag}_ac"), NIC)
                        else:
                            eng.wait_ge(S(f"{tag}_ap"), t - 1)
                        eng.dma_start(out=xf[tg % 2][:, :],
                                      in_=src_spec[1][t]).then_inc(S(f"{tag}_qx"), 16)
            else:
                _dma_pass1 = _dma_pass2 = None

            @blk.vector
            def _(vector):
                for sname, v in cast_waits:
                    vector.wait_ge(S(sname), v)
                for ic in range(NIC):
                    if kind == "dram":
                        vector.wait_ge(S(f"{tag}_qx"), 16 * (ic + 1))
                    if ic >= 2:
                        vector.wait_ge(S(f"{tag}_pe"), ic - 1)
                    vector.tensor_copy(out=xb[ic % 2][:, :],
                                       in_=lsrc(ic)).then_inc(S(f"{tag}_vc"), 1)

            @blk.scalar
            def _(scalar):
                for sname, v in cast_waits:
                    scalar.wait_ge(S(sname), v)
                for ic in range(NIC):
                    if kind == "dram":
                        scalar.wait_ge(S(f"{tag}_qx"), 16 * (ic + 1))
                    if ic >= 1:
                        scalar.wait_ge(S(f"{tag}_pe"), ic)
                    scalar.activation(out=xq[:, :], in_=lsrc(ic),
                                      func=AF.Square).then_inc(S(f"{tag}_ac"), 1)

            @blk.tensor
            def _(tensor):
                for sname, v in stats_waits:
                    tensor.wait_ge(S(sname), v)
                for ic in range(NIC):
                    tensor.wait_ge(S(f"{tag}_vc"), ic + 1)
                    tensor.wait_ge(S(f"{tag}_ac"), ic + 1)
                    ins = None
                    for j in range(NJ):
                        tensor.matmul(ps_sx[:, j * 512:(j + 1) * 512], onesb[:, :],
                                      xb[ic % 2][:, j * 512:(j + 1) * 512],
                                      start=(ic == 0), stop=(ic == NIC - 1))
                        ins = tensor.matmul(ps_sxx[:, j * 512:(j + 1) * 512], onesb[:, :],
                                            xq[:, j * 512:(j + 1) * 512],
                                            start=(ic == 0), stop=(ic == NIC - 1))
                    ins.then_inc(S(f"{tag}_pe"), 1)

            @blk.vector
            def _(vector):
                vector.wait_ge(S(f"{tag}_pe"), NIC)
                vector.tensor_scalar(out=ps_sx[:, :], in0=ps_sx[:, :], scalar1=1.0 / D,
                                     scalar2=None, op0=ALU.mult)
                vector.tensor_scalar(out=ps_sxx[:, :], in0=ps_sxx[:, :], scalar1=1.0 / D,
                                     scalar2=None, op0=ALU.mult).then_inc(S(f"{tag}_st"), 1)

            @blk.scalar
            def _(scalar):
                scalar.wait_ge(S(f"{tag}_st"), 1)
                scalar.activation(out=mir[:, :], in_=ps_sx[:, :],
                                  func=AF.Square).then_inc(S(f"{tag}_st"), 1)

            @blk.vector
            def _(vector):
                vector.wait_ge(S(f"{tag}_st"), 2)
                vector.tensor_sub(out=ps_sxx[:, :], in0=ps_sxx[:, :],
                                  in1=mir[:, :]).then_inc(S(f"{tag}_st"), 1)

            @blk.scalar
            def _(scalar):
                scalar.wait_ge(S(f"{tag}_st"), 3)
                scalar.activation(out=sxxs[:, :], in_=ps_sxx[:, :], func=AF.Sqrt,
                                  bias=epsr[:, 0:1]).then_inc(S(f"{tag}_st"), 1)

            @blk.vector
            def _(vector):
                vector.wait_ge(S(f"{tag}_st"), 4)
                vector.reciprocal(out=sxxs[:, :], in_=sxxs[:, :])
                vector.tensor_mul(out=mir[:, :], in0=ps_sx[:, :],
                                  in1=sxxs[:, :]).then_inc(S(f"{tag}_st"), 1)

            PSTAT.close()
            return {"sxxs": sxxs, "mir": mir, "lsrc": lsrc, "kind": kind,
                    "dma1": _dma_pass1, "dma2": _dma_pass2}

        def emit_ln_apply(tag, T, st, h_out, h_stride, gcol, scol, LN, blk, apply_waits=()):
            NJ = T // 512
            APL = contextlib.ExitStack()
            pb_istd = APL.enter_context(nc.psum_tensor(f"{tag}bi", [128, T], F32))
            pb_mi = APL.enter_context(nc.psum_tensor(f"{tag}bm", [128, T], F32))
            tt = APL.enter_context(nc.sbuf_tensor(f"{tag}tt", [128, T], F32))
            sxxs, mir, lsrc, kind = st["sxxs"], st["mir"], st["lsrc"], st["kind"]

            @blk.tensor
            def _(tensor):
                tensor.wait_ge(S(f"{tag}_st"), 5)
                ins = None
                for j in range(NJ):
                    tensor.matmul(pb_istd[:, j * 512:(j + 1) * 512], onesf1[:, :],
                                  sxxs[:, j * 512:(j + 1) * 512], start=True, stop=True)
                    ins = tensor.matmul(pb_mi[:, j * 512:(j + 1) * 512], onesf1b[:, :],
                                        mir[:, j * 512:(j + 1) * 512], start=True, stop=True)
                ins.then_inc(S(f"{tag}_bc"), 1)

            @blk.vector
            def _(vector):
                for sname, v in apply_waits:
                    vector.wait_ge(S(sname), v)
                vector.wait_ge(S(f"{tag}_bc"), 1)
                for ic in range(NIC):
                    if kind == "dram":
                        vector.wait_ge(S(f"{tag}_qx"), 16 * (NIC + ic + 1))
                    vector.tensor_mul(out=tt[:, :], in0=lsrc(ic), in1=pb_istd[:, :])
                    vector.tensor_sub(out=tt[:, :], in0=tt[:, :], in1=pb_mi[:, :])
                    vector.tensor_scalar(
                        out=h_out[:, ic * h_stride:ic * h_stride + T],
                        in0=tt[:, :], scalar1=gcol[:, ic:ic + 1],
                        scalar2=scol[:, ic:ic + 1],
                        op0=ALU.mult, op1=ALU.add).then_inc(S(f"{tag}_ap"), 1)
            return APL

        P12 = contextlib.ExitStack()
        h_kv = P12.enter_context(nc.sbuf_tensor("h_kv", [128, NIC * SEQ], BF16))
        h1own = P12.enter_context(nc.sbuf_tensor("h1own", [128, NIC * TOWN], BF16))

        P0 = contextlib.ExitStack()
        with P0:
            Mall = P0.enter_context(nc.sbuf_tensor("Mall", [128, 384], F32))
            P0M = contextlib.ExitStack()
            ct = P0M.enter_context(nc.sbuf_tensor("ct", [128, 64], F32))
            cs = P0M.enter_context(nc.sbuf_tensor("cs", [128, 64], F32))
            wmf = [P0M.enter_context(nc.sbuf_tensor(f"wmf{i}", [128, 1536], F32)) for i in range(2)]
            modp = P0M.enter_context(nc.sbuf_tensor("modp", [4, 1536], F32))
            PMOD = contextlib.ExitStack()
            pmod = PMOD.enter_context(nc.psum_tensor("pmod", [4, 1536], F32))
            pmt = PMOD.enter_context(nc.psum_tensor("pmt", [128, 48], F32))

            with nc.Block() as blk:
                @blk.sync
                def _(sync):
                    for ic in range(NIC):
                        sync.dma_start(out=x_res[:, ic * TOWN:(ic + 1) * TOWN],
                                       in_=d_xown[ic]).then_inc(S("q_xres"), 16)
                    for ic in range(NIC):
                        sync.dma_start(out=ct[:, ic * 4:(ic + 1) * 4],
                                       in_=d_ct[ic * 128:(ic + 1) * 128, :]).then_inc(S("q_ct"), 16)
                    for m in "qkvo":
                        sync.dma_start(out=bias_t[m][:, :], in_=d_b2[m][:, :]).then_inc(S("q_bias"), 16)
                    for ic in range(NIC):
                        if ic >= 2:
                            sync.wait_ge(S("e_modmm"), ic - 1)
                        sync.dma_start(out=wmf[ic % 2][:, :],
                                       in_=d_wmod[ic * 128:(ic + 1) * 128, :]).then_inc(S("q_wmf"), 16)
                    sync.wait_ge(S("ch0"), 4)
                    sync.dma_start(out=d_modin[:, :], in_=Mall[:, 0:48]).then_inc(S("q_modin"), 16)

                @blk.vector
                def _(vector):
                    _sern = [0]

                    def ser(ins):
                        _sern[0] += 1
                        ins.then_inc(S("e_ser"), 1)
                        vector.wait_ge(S("e_ser"), _sern[0])

                    vector.wait_ge(S("e_iota"), 1)
                    ser(vector.tensor_scalar(out=identf[:, :], in0=identf[:, :], scalar1=0.0,
                                             scalar2=None, op0=ALU.is_equal))
                    vector.tensor_copy(out=identb[:, :], in_=identf[:, :])
                    vector.memset(onesb[:, :], 1.0)
                    vector.memset(epsr[:, :], EPS)
                    vector.memset(onesf1b[:, :], 1.0)
                    vector.memset(onesf1[:, :], 1.0).then_inc(S("e_ident"), 1)
                    vector.wait_ge(S("q_bias"), 64)
                    ser(vector.tensor_scalar(out=bias_t["q"][:, :], in0=bias_t["q"][:, :],
                                             scalar1=SCALE, scalar2=None, op0=ALU.mult))
                    vector.engine_nop().then_inc(S("e_serdone"), 1)
                    _SER_STATE[0] = _sern[0]

                @blk.scalar
                def _(scalar):
                    scalar.wait_ge(S("q_ct"), 256)
                    scalar.activation(out=cs[:, :], in_=ct[:, :], func=AF.Silu).then_inc(S("a_cs"), 1)
                    scalar.wait_ge(S("e_modmm"), NIC)
                    scalar.copy(out=modp[:, :], in_=pmod[:, :]).then_inc(S("ch0"), 1)
                    scalar.wait_ge(S("ch0"), 3)
                    scalar.copy(out=Mall[:, 0:48], in_=pmt[:, :]).then_inc(S("ch0"), 1)

                @blk.tensor
                def _(tensor):
                    tensor.wait_ge(S("a_cs"), 1)
                    for ic in range(NIC):
                        tensor.wait_ge(S("q_wmf"), 16 * (ic + 1))
                        ins = None
                        for j in range(3):
                            ins = tensor.matmul(pmod[:, j * 512:(j + 1) * 512],
                                                cs[:, ic * 4:(ic + 1) * 4],
                                                wmf[ic % 2][:, j * 512:(j + 1) * 512],
                                                start=(ic == 0), stop=(ic == NIC - 1))
                        ins.then_inc(S("e_modmm"), 1)
                    tensor.wait_ge(S("ch0"), 1)
                    tensor.wait_ge(S("e_ident"), 1)
                    ins = None
                    for g in range(12):
                        ins = tensor.transpose(pmt[:, g * 4:(g + 1) * 4],
                                               modp[:, g * 128:(g + 1) * 128],
                                               identf[0:4, 0:4])
                    ins.then_inc(S("ch0"), 2)

                PMOD.close()
                P0M.close()
                LN1 = contextlib.ExitStack()
                stg_lg = emit_ln_stats("lg", SEQ, ("dram", d_xfull), LN1, blk,
                                       stats_waits=(("ch0", 4),),
                                       cast_waits=(("ch0", 4),))

                @blk.gpsimd
                def _(gpsimd):
                    gpsimd.dma_start(out=lt[:, :],
                                     in_=d_lut[None, :].broadcast_to((128, 16))).then_inc(S("q_pre"), 16)
                    gpsimd.dma_start(out=scb[:, :],
                                     in_=d_sc[None, :].broadcast_to((128, 8))).then_inc(S("q_pre"), 16)
                    gpsimd.dma_start(out=ebb[:, :],
                                     in_=d_eb[None, :].broadcast_to((128, NB))).then_inc(S("q_pre"), 16)
                    gpsimd.iota(identf[:, :], pattern=[[1, 128]], base=0, channel_multiplier=-1,
                                allow_small_or_imprecise_dtypes=True).then_inc(S("e_iota"), 1)
                    gpsimd.wait_ge(S("ch0"), 4)
                    stg_lg["dma1"](gpsimd)
                    gpsimd.wait_ge(S("q_modin"), 16)
                    gpsimd.collective_compute(
                        "AllGather", ALU.bypass, replica_groups=[list(range(8))],
                        ins=[d_modin.opt()], outs=[d_modout.opt()]).then_inc(S("c_mod"), 1)
                    gpsimd.wait_ge(S("c_mod"), 1)
                    gpsimd.dma_start(out=Mall[:, :],
                                     in_=d_modout.transpose([1, 0, 2])).then_inc(S("q_modall"), 16)
                    stg_lg["dma2"](gpsimd)

                @blk.vector
                def _(vector):
                    _sern = [_SER_STATE[0]]

                    def ser(ins):
                        _sern[0] += 1
                        ins.then_inc(S("e_ser"), 1)
                        vector.wait_ge(S("e_ser"), _sern[0])

                    vector.wait_ge(S("q_pre"), 48)
                    vector.wait_ge(S("q_modall"), 16)
                    ser(vector.tensor_scalar(out=modv[:, :], in0=Mall[:, 0::4],
                                             scalar1=ebb[:, 0:1], scalar2=None, op0=ALU.mult))
                    for b4 in range(1, NB):
                        ser(vector.scalar_tensor_tensor(out=modv[:, :], in0=Mall[:, b4::4],
                                                        scalar=ebb[:, b4:b4 + 1], in1=modv[:, :],
                                                        op0=ALU.mult, op1=ALU.add))
                    ser(vector.tensor_scalar_add(out=g1c[:, :], in0=CA, scalar1=1.0))
                    ser(vector.tensor_scalar_add(out=g2c[:, :], in0=CB, scalar1=1.0))
                    for vsrc, dst in ((GA, gac), (GB, gbc)):
                        ser(vector.tensor_scalar(out=scr[:, :], in0=vsrc, scalar1=0.0,
                                                 scalar2=2.0, op0=ALU.is_ge, op1=ALU.mult))
                        ser(vector.tensor_scalar_add(out=scr[:, :], in0=scr[:, :], scalar1=-1.0))
                        ser(vector.tensor_mul(out=dst[:, :], in0=vsrc, in1=scr[:, :]))
                        ser(vector.tensor_scalar_max(out=dst[:, :], in0=dst[:, :], scalar1=0.25))
                        ser(vector.tensor_mul(out=dst[:, :], in0=dst[:, :], in1=scr[:, :]))
                    vector.engine_nop().then_inc(S("e_mod"), 1)

                emit_ln_apply("lg", SEQ, stg_lg, h_kv, SEQ, g1c, modv[:, 0:16], LN1, blk,
                              apply_waits=(("e_mod", 1),)).close()
                stg_lo = emit_ln_stats("lo", TOWN, ("sbuf", x_res, TOWN), LN1, blk,
                                       stats_waits=(("lg_ap", NIC),),
                                       cast_waits=(("q_xres", 256), ("lg_ap", NIC)))
                emit_ln_apply("lo", TOWN, stg_lo, h1own, TOWN, g1c, modv[:, 0:16], LN1, blk,
                              apply_waits=(("e_mod", 1),)).close()
            LN1.close()

        P2 = contextlib.ExitStack()
        with P2:
            wf = [P2.enter_context(nc.sbuf_tensor(f"wf{i}", [128, 2048], F32)) for i in range(2)]
            wb = [P2.enter_context(nc.sbuf_tensor(f"wbq{i}", [128, 2048], BF16)) for i in range(2)]
            stg = [P2.enter_context(nc.sbuf_tensor(f"stg{i}", [128, 2048], BF16)) for i in range(2)]
            cvi = [P2.enter_context(nc.sbuf_tensor(f"cvi{i}", [128, 1024], I32)) for i in range(2)]
            cvo = [P2.enter_context(nc.sbuf_tensor(f"cvo{i}", [128, 1024], BF16)) for i in range(2)]
            pqk = [P2.enter_context(nc.psum_tensor(f"pqk{i}", [128, 2048], F32)) for i in range(2)]

            jobs = [("k", oc) for oc in range(NH)] + \
                   [("v", oc) for oc in range(NH)] + \
                   [("q", oc) for oc in range(NH)]
            NJOB = len(jobs)
            conv_at = {}
            for k in range(NCONV):
                conv_at.setdefault(k * NJOB // NCONV, []).append(k)

            with nc.Block() as blk:
                @blk.sync
                def _(sync):
                    for t, (m, oc) in enumerate(jobs):
                        if t >= 2:
                            sync.wait_ge(S("p2_cast"), t - 1)
                        sync.dma_start(out=wf[t % 2][:, :], in_=d_w4[m][oc]).then_inc(S("p2_qw"), 16)

                @blk.gpsimd
                def _(gpsimd):
                    def _cstage(k):
                        gpsimd.wait_ge(S("e_conv"), k + 1)
                        _, w, off, bw, coff = conv_tiles[k]
                        blkview = d_win[off:off + 128 * bw].rearrange("(p c) -> p c", p=128)
                        gpsimd.dma_start(out=blkview[:, coff:coff + w],
                                         in_=cvo[k % 2][:, 0:w]).then_inc(S("q_cvout"), 16)

                    for t, (m, oc) in enumerate(jobs):
                        for k in conv_at.get(t, []):
                            if k >= 2:
                                gpsimd.wait_ge(S("e_conv"), k - 1)
                            w = conv_tiles[k][1]
                            gpsimd.dma_start(out=cvi[k % 2][:, 0:w],
                                             in_=conv_tiles[k][0]).then_inc(S("q_cvin"), 16)
                            if k >= 2:
                                _cstage(k - 2)
                        gpsimd.wait_ge(S("p2_ev"), t + 1)
                        if m == "k":
                            dst, T = d_kv[0, oc], SEQ
                        elif m == "v":
                            dst, T = d_kv[1, oc], SEQ
                        else:
                            dst, T = d_q[oc], TOWN
                        gpsimd.dma_start(out=dst, in_=stg[t % 2][:, 0:T]).then_inc(S("p2_st"), 16)
                    _cstage(NCONV - 2)
                    _cstage(NCONV - 1)
                    gpsimd.wait_ge(S("p2_st"), 16 * len(jobs))
                    gpsimd.wait_ge(S("q_cvout"), 16 * NCONV)

                def _p2_evac(scalar, t):
                    m, oc = jobs[t]
                    scalar.wait_ge(S("p2_pe"), t + 1)
                    if t >= 2:
                        scalar.wait_ge(S("p2_st"), 16 * (t - 1))
                    T = TOWN if m == "q" else SEQ
                    sc_in = SCALE if m == "q" else 1.0
                    scalar.activation(out=stg[t % 2][:, 0:T], in_=pqk[t % 2][:, 0:T],
                                      func=AF.Identity, scale=sc_in,
                                      bias=bias_t[m][:, oc:oc + 1]).then_inc(S("p2_ev"), 1)

                @blk.scalar
                def _(scalar):
                    for t, (m, oc) in enumerate(jobs):
                        scalar.wait_ge(S("p2_qw"), 16 * (t + 1))
                        if t >= 2:
                            scalar.wait_ge(S("p2_pe"), t - 1)
                        scalar.copy(out=wb[t % 2][:, :],
                                    in_=wf[t % 2][:, :]).then_inc(S("p2_cast"), 1)
                        if t >= 2:
                            _p2_evac(scalar, t - 2)
                    _p2_evac(scalar, len(jobs) - 2)
                    _p2_evac(scalar, len(jobs) - 1)

                @blk.vector
                def _(vector):
                    vector.wait_ge(S("q_pre"), 48)
                    for t, (m, oc) in enumerate(jobs):
                        for k in conv_at.get(t, []):
                            w = conv_tiles[k][1]
                            vector.wait_ge(S("q_cvin"), 16 * (k + 1))
                            if k >= 2:
                                vector.wait_ge(S("q_cvout"), 16 * (k - 1))
                            o = cvo[k % 2][:, 0:w]
                            i = cvi[k % 2][:, 0:w]
                            vector._custom_dve(LUT_INIT2, out=o, in0=i,
                                               s0=lt[:, 0:1], s1=lt[:, 1:2], imm2=0.0)
                            ins = None
                            for kk in range(1, 8):
                                ins = vector._custom_dve(
                                    LUT_ACC2, out=o, in0=i, in1=o,
                                    s0=lt[:, 2 * kk:2 * kk + 1],
                                    s1=lt[:, 2 * kk + 1:2 * kk + 2],
                                    imm2=float(2 * kk))
                            ins.then_inc(S("e_conv"), 1)

                @blk.tensor
                def _(tensor):
                    for t, (m, oc) in enumerate(jobs):
                        tensor.wait_ge(S("p2_cast"), t + 1)
                        if t >= 2:
                            tensor.wait_ge(S("p2_ev"), t - 1)
                        if m == "q":
                            hsrc, hs, T = h1own, TOWN, TOWN
                        else:
                            hsrc, hs, T = h_kv, SEQ, SEQ
                        ins = None
                        for ic in range(NIC):
                            for j in range(T // 512):
                                ins = tensor.matmul(
                                    pqk[t % 2][:, j * 512:(j + 1) * 512],
                                    wb[t % 2][:, ic * 128:(ic + 1) * 128],
                                    hsrc[:, ic * hs + j * 512:ic * hs + (j + 1) * 512],
                                    start=(ic == 0), stop=(ic == NIC - 1))
                        ins.then_inc(S("p2_pe"), 1)

        P12.close()

        steps = []
        for h in range(NH):
            for k, s in enumerate(SLOT_ORDER):
                steps.append((h, s))
        NTl = [NT[s] for (_, s) in steps]
        NCH = [2 * n for n in NTl]
        NGR = [(c + 7) // 8 for c in NCH]

        LAG = 3
        ptp_order = []
        for h in range(NH):
            ptp_order.append(("v", h, 0))
            ptp_order.append(("v", h, 1))
            for k in range(NSLOT):
                t = h * NSLOT + k
                if t >= LAG:
                    for g in range(NGR[t - LAG]):
                        ptp_order.append(("t", t - LAG, g))
        for t in range(NSTEP - LAG, NSTEP):
            for g in range(NGR[t]):
                ptp_order.append(("t", t, g))
        ptp_pos = {key: i for i, key in enumerate(ptp_order)}
        trav_done = {}
        vtr_done = {}
        for i, key in enumerate(ptp_order):
            if key[0] == "t":
                trav_done[key[1]] = i + 1
            else:
                vtr_done[key[1]] = i + 1

        P3 = contextlib.ExitStack()
        with P3:
            oT = P3.enter_context(nc.sbuf_tensor("oT", [128, NIC * TOWN], BF16))
            o_sb = P3.enter_context(nc.sbuf_tensor("o_sb", [128, NSLOT * D], BF16))
            AT = contextlib.ExitStack()
            with AT:
                jg = AT.enter_context(nc.sbuf_tensor("jg", [128, SEQ], F32))
                wof3 = oT.bitcast(F32)[:, 0:1024]
                wob3 = oT[:, 4096:6144]
                kh = [AT.enter_context(nc.sbuf_tensor(f"kh{i}", [128, SEQ], BF16)) for i in range(2)]
                vh = [AT.enter_context(nc.sbuf_tensor(f"vh{i}", [128, SEQ], BF16)) for i in range(2)]
                vt = [AT.enter_context(nc.sbuf_tensor(f"vt{i}", [128, SEQ], BF16)) for i in range(2)]
                qh = [AT.enter_context(nc.sbuf_tensor(f"qh{i}", [128, TOWN], BF16)) for i in range(2)]
                ex = [AT.enter_context(nc.sbuf_tensor(f"ex{i}", [128, 2048], BF16)) for i in range(3)]
                pbf = [AT.enter_context(nc.sbuf_tensor(f"pbf{i}", [128, 2048], BF16)) for i in range(3)]
                pts = [AT.enter_context(nc.sbuf_tensor(f"pts{i}", [128, 2048], BF16)) for i in range(3)]
                mx = AT.enter_context(nc.sbuf_tensor("mx", [128, 4], F32))
                ssum = AT.enter_context(nc.sbuf_tensor("ssum", [128, 4], F32))
                rec = AT.enter_context(nc.sbuf_tensor("rec", [128, 8], F32))
                obuf = AT.enter_context(nc.sbuf_tensor("obuf", [128, 512], F32))
                SCL = AT.enter_context(nc.psum_tensor("SCL", [128, 2048], F32))
                SCS = AT.enter_context(nc.psum_tensor("SCS", [128, 1024], F32))
                ptp = AT.enter_context(nc.psum_tensor("ptp", [128, 1024], BF16))
                ov = AT.enter_context(nc.psum_tensor("ov", [128, 128], F32))

                _fl = {"d": 0, "a": 0}

                def flush_tick(eng, key, sername, ins, sem_name, k=1):
                    _fl[key] += 1
                    ins.then_inc(S(sername), 1)
                    eng.wait_ge(S(sername), _fl[key])
                    eng.nop(nofuse=True).then_inc(S(sem_name), k)

                def lane(t):
                    return SCL if (t % 2 == 0) else SCS

                def emit_trav(tensor, t):
                    h, s = steps[t]
                    tensor.wait_ge(S("p3_st"), t + 1)
                    for g in range(NGR[t]):
                        pos = ptp_pos[("t", t, g)]
                        tensor.wait_ge(S("p3_pte"), pos)
                        ins = None
                        for kk in range(min(8, NCH[t] - g * 8)):
                            kc = g * 8 + kk
                            ins = tensor.transpose(ptp[:, kk * 128:(kk + 1) * 128],
                                                   pbf[t % 3][:, kc * 128:(kc + 1) * 128],
                                                   identb[:, :])
                        ins.then_inc(S("p3_ptw"), 1)
                    tensor.wait_ge(S("p3_pte"), trav_done[t])
                    if t >= 1:
                        tensor.wait_ge(S("a_ovc"), t)
                    ovl = ov[:, :]
                    ins = None
                    for kc in range(NCH[t]):
                        ins = tensor.matmul(ovl,
                                            pts[t % 3][:, kc * 128:(kc + 1) * 128],
                                            vt[h % 2][:, kc * 128:(kc + 1) * 128],
                                            start=(kc == 0), stop=(kc == NCH[t] - 1))
                    ins.then_inc(S("p3_av"), 1)

                def emit_stt(vector, t):
                    h, s = steps[t]
                    W = NT[s] * 256
                    if t == 0:
                        vector.wait_ge(S("e_jg"), 1)
                    vector.wait_ge(S("p3_ex"), t + 1)
                    if t >= LAG:
                        vector.wait_ge(S("p3_ptw"), trav_done[t - LAG])
                    vector.scalar_tensor_tensor(
                        out=pbf[t % 3][:, 0:W], in0=jg[:, 0:W],
                        scalar=scb[:, s:s + 1], in1=ex[t % 3][:, 0:W],
                        op0=ALU.is_le, op1=ALU.mult,
                        accum_out=ssum[:, t % 4:t % 4 + 1]).then_inc(S("p3_st"), 1)

                def emit_rec(vector, t):
                    vector.wait_ge(S("p3_st"), t + 1)
                    vector.reciprocal(out=rec[:, t % 8:t % 8 + 1],
                                      in_=ssum[:, t % 4:t % 4 + 1]).then_inc(S("p3_rc"), 1)

                def emit_oev(vector, t):
                    h, s = steps[t]
                    vector.wait_ge(S("a_ovc"), t + 1)
                    vector.wait_ge(S("p3_rc"), t + 1)
                    vector.tensor_scalar(
                        out=o_sb[:, s * D + h * 128:s * D + (h + 1) * 128],
                        in0=obuf[:, (t % 4) * 128:(t % 4) * 128 + 128],
                        scalar1=rec[:, t % 8:t % 8 + 1],
                        scalar2=None, op0=ALU.mult).then_inc(S("p3_oe"), 1)

                def emit_pte(scalar, t):
                    for g in range(NGR[t]):
                        pos = ptp_pos[("t", t, g)]
                        scalar.wait_ge(S("p3_ptw"), pos + 1)
                        if t >= LAG:
                            scalar.wait_ge(S("p3_av"), t - LAG + 1)
                        w = min(1024, NCH[t] * 128 - g * 1024)
                        scalar.copy(out=pts[t % 3][:, g * 1024:g * 1024 + w],
                                    in_=ptp[:, 0:w]).then_inc(S("p3_pte"), 1)

                with nc.Block() as blk:
                    @blk.sync
                    def _(sync):
                        sync.dma_start(out=wof3,
                                       in_=d_w4["o"][0][:, 0:1024]).then_inc(S("p3w_in"), 16)
                        for oc in range(NIC):
                            sync.wait_ge(S("p3w_c"), 2 * oc + 1)
                            sync.dma_start(out=wof3,
                                           in_=d_w4["o"][oc][:, 1024:2048]).then_inc(S("p3w_in"), 16)
                            sync.wait_ge(S("p3w_c"), 2 * oc + 2)
                            sync.dma_start(out=d_wobf[oc],
                                           in_=wob3).then_inc(S("p3w_out"), 16)
                            if oc + 1 < NIC:
                                sync.dma_start(out=wof3,
                                               in_=d_w4["o"][oc + 1][:, 0:1024]).then_inc(S("p3w_in"), 16)

                    @blk.gpsimd
                    def _(gpsimd):
                        gpsimd.iota(jg[:, :], pattern=[[1, SEQ]], base=0, channel_multiplier=-1,
                                    allow_small_or_imprecise_dtypes=True).then_inc(S("e_jg"), 1)
                        for h in range(NH):
                            if h >= 2:
                                gpsimd.wait_ge(S("p3_av"), (h - 1) * NSLOT)
                            gpsimd.dma_start(out=kh[h % 2][:, :], in_=d_kv[0, h]).then_inc(S("p3_kv"), 16)
                            gpsimd.dma_start(out=vh[h % 2][:, :], in_=d_kv[1, h]).then_inc(S("p3_kv"), 16)
                            gpsimd.dma_start(out=qh[h % 2][:, :], in_=d_q[h]).then_inc(S("p3_kv"), 16)
                            gpsimd.wait_ge(S("p3w_in"), 16 * (2 * h + 1))
                            if h >= 1:
                                gpsimd.wait_ge(S("p3w_out"), 16 * h)
                            gpsimd.tensor_copy(out=wob3[:, 0:1024],
                                               in_=wof3).then_inc(S("p3w_c"), 1)
                            gpsimd.wait_ge(S("p3w_in"), 16 * (2 * h + 2))
                            gpsimd.tensor_copy(out=wob3[:, 1024:2048],
                                               in_=wof3).then_inc(S("p3w_c"), 1)
                            if DEBUG_DUMP and h == 1:
                                gpsimd.wait_ge(S("p3_rc"), 2)
                                gpsimd.dma_start(out=d_dbg_at[:, 0:4], in_=ssum[:, :]).then_inc(S("q_dba"), 16)
                                gpsimd.dma_start(out=d_dbg_at[:, 4:12], in_=rec[:, :]).then_inc(S("q_dba"), 16)
                                gpsimd.dma_start(out=d_dbg_at[:, 12:16], in_=mx[:, :]).then_inc(S("q_dba"), 16)
                                gpsimd.dma_start(out=d_dbg_ab[:, 0:256],
                                                 in_=ex[1][:, 0:256]).then_inc(S("q_dba"), 16)
                                gpsimd.dma_start(out=d_dbg_ab[:, 256:512],
                                                 in_=pbf[1][:, 0:256]).then_inc(S("q_dba"), 16)
                                gpsimd.dma_start(out=d_dbg_ab[:, 512:1536],
                                                 in_=qh[0][:, :]).then_inc(S("q_dba"), 16)
                                gpsimd.wait_ge(S("p3_oe"), 2)
                                gpsimd.dma_start(out=d_dbg_at[:, 16:144],
                                                 in_=obuf[:, 128:256]).then_inc(S("q_dba"), 16)
                                gpsimd.dma_start(out=d_dbg_ab[:, 1536:1664],
                                                 in_=o_sb[:, 0:128]).then_inc(S("q_dba"), 16)
                                gpsimd.wait_ge(S("q_dba"), 128)
                            if h == 6:
                                gpsimd.wait_ge(S("q_cvout"), 16 * NCONV)
                                gpsimd.collective_compute(
                                    "AllGather", ALU.bypass, replica_groups=[list(range(8))],
                                    ins=[d_win.opt()], outs=[d_wout.opt()]).then_inc(S("c_w"), 1)

                    @blk.tensor
                    def _(tensor):
                        for h in range(NH):
                            tensor.wait_ge(S("p3_kv"), 48 * (h + 1))
                            for g in range(2):
                                pos = ptp_pos[("v", h, g)]
                                if pos >= 1:
                                    tensor.wait_ge(S("p3_pte"), pos)
                                ins = None
                                for kk in range(8):
                                    kc = g * 8 + kk
                                    ins = tensor.transpose(ptp[:, kk * 128:(kk + 1) * 128],
                                                           vh[h % 2][:, kc * 128:(kc + 1) * 128],
                                                           identb[:, :])
                                ins.then_inc(S("p3_ptw"), 1)
                            for k in range(NSLOT):
                                t = h * NSLOT + k
                                s = steps[t][1]
                                SC = lane(t)
                                if t >= 2:
                                    tensor.wait_ge(S("p3_ex"), t - 1)
                                ins = None
                                for kt in range(NT[s]):
                                    ins = tensor.matmul(SC[:, kt * 256:(kt + 1) * 256],
                                                        qh[h % 2][:, s * 128:(s + 1) * 128],
                                                        kh[h % 2][:, kt * 256:(kt + 1) * 256],
                                                        start=True, stop=True)
                                ins.then_inc(S("p3_sc"), 1)
                                if t >= LAG:
                                    emit_trav(tensor, t - LAG)
                        for tl in range(NSTEP - LAG, NSTEP):
                            emit_trav(tensor, tl)

                    @blk.vector
                    def _(vector):
                        for t in range(NSTEP):
                            h, s = steps[t]
                            SC = lane(t)
                            W = NT[s] * 256
                            vector.wait_ge(S("p3_sc"), t + 1)
                            if t >= 4:
                                vector.wait_ge(S("p3_ex"), t - 3)
                            vector.reduce_max(mx[:, t % 4:t % 4 + 1], SC[:, 0:W],
                                              axis=mybir.AxisListType.X,
                                              negate=True).then_inc(S("p3_mb"), 1)
                            if t >= 3:
                                emit_oev(vector, t - 3)
                            if t >= 1:
                                emit_stt(vector, t - 1)
                                emit_rec(vector, t - 1)
                        emit_stt(vector, NSTEP - 1)
                        emit_rec(vector, NSTEP - 1)
                        for tl in range(NSTEP - 3, NSTEP):
                            emit_oev(vector, tl)

                    @blk.scalar
                    def _(scalar):
                        for h in range(NH):
                            if h >= 2:
                                scalar.wait_ge(S("p3_av"), (h - 1) * NSLOT)
                            for g in range(2):
                                pos = ptp_pos[("v", h, g)]
                                scalar.wait_ge(S("p3_ptw"), pos + 1)
                                scalar.copy(out=vt[h % 2][:, g * 1024:(g + 1) * 1024],
                                            in_=ptp[:, :]).then_inc(S("p3_pte"), 1)
                            for k in range(NSLOT):
                                t = h * NSLOT + k
                                s = steps[t][1]
                                W = NT[s] * 256
                                SC = lane(t)
                                if t >= LAG:
                                    emit_pte(scalar, t - LAG)
                                if t >= 3:
                                    tov = t - 3
                                    scalar.wait_ge(S("p3_av"), tov + 1)
                                    if tov >= 4:
                                        scalar.wait_ge(S("p3_oe"), tov - 3)
                                    scalar.copy(out=obuf[:, (tov % 4) * 128:(tov % 4) * 128 + 128],
                                                in_=ov[:, :]).then_inc(S("a_ovc"), 1)
                                scalar.wait_ge(S("p3_mb"), t + 1)
                                if t >= 3:
                                    scalar.wait_ge(S("p3_st"), t - 2)
                                scalar.activation(out=ex[t % 3][:, 0:W], in_=SC[:, 0:W],
                                                  func=AF.Exp, bias=mx[:, t % 4:t % 4 + 1],
                                                  scale=1.0).then_inc(S("p3_ex"), 1)
                        for tl in range(NSTEP - LAG, NSTEP):
                            emit_pte(scalar, tl)
                            tov = tl
                            scalar.wait_ge(S("p3_av"), tov + 1)
                            scalar.copy(out=obuf[:, (tov % 4) * 128:(tov % 4) * 128 + 128],
                                        in_=ov[:, :]).then_inc(S("a_ovc"), 1)

            OP = contextlib.ExitStack()
            with OP:
                wob = [OP.enter_context(nc.sbuf_tensor(f"wob{i}", [128, 2048], BF16)) for i in range(2)]
                atmp = [OP.enter_context(nc.sbuf_tensor(f"atmp{i}", [128, TOWN], F32)) for i in range(2)]
                pot = OP.enter_context(nc.psum_tensor("pot", [128, 1024], BF16))
                po = [OP.enter_context(nc.psum_tensor(f"po{i}", [128, TOWN], F32)) for i in range(2)]

                with nc.Block() as blk:
                    @blk.sync
                    def _(sync):
                        for oc in range(NIC):
                            if oc >= 2:
                                sync.wait_ge(S("p3o_pe"), oc - 1)
                            sync.dma_start(out=wob[oc % 2][:, :],
                                           in_=d_wobf[oc]).then_inc(S("p3o_qw"), 16)

                    @blk.tensor
                    def _(tensor):
                        for ic in range(NIC):
                            if ic >= 1:
                                tensor.wait_ge(S("p3o_te"), ic)
                            ins = None
                            for s in range(NSLOT):
                                ins = tensor.transpose(pot[:, s * 128:(s + 1) * 128],
                                                       o_sb[:, s * D + ic * 128:s * D + (ic + 1) * 128],
                                                       identb[:, :])
                            ins.then_inc(S("p3o_tr"), 1)

                    @blk.scalar
                    def _(scalar):
                        for ic in range(NIC):
                            scalar.wait_ge(S("p3o_tr"), ic + 1)
                            scalar.copy(out=oT[:, ic * TOWN:(ic + 1) * TOWN],
                                        in_=pot[:, :]).then_inc(S("p3o_te"), 1)

                    @blk.tensor
                    def _(tensor):
                        tensor.wait_ge(S("p3o_te"), NIC)
                        for oc in range(NIC):
                            tensor.wait_ge(S("p3o_qw"), 16 * (oc + 1))
                            if oc >= 2:
                                tensor.wait_ge(S("p3o_ev"), oc - 1)
                            ins = None
                            for ic in range(NIC):
                                for j in range(TOWN // 512):
                                    ins = tensor.matmul(
                                        po[oc % 2][:, j * 512:(j + 1) * 512],
                                        wob[oc % 2][:, ic * 128:(ic + 1) * 128],
                                        oT[:, ic * TOWN + j * 512:ic * TOWN + (j + 1) * 512],
                                        start=(ic == 0), stop=(ic == NIC - 1))
                            ins.then_inc(S("p3o_pe"), 1)

                    @blk.scalar
                    def _(scalar):
                        for oc in range(NIC):
                            scalar.wait_ge(S("p3o_pe"), oc + 1)
                            if oc >= 2:
                                scalar.wait_ge(S("p3o_res"), oc - 1)
                            scalar.activation(out=atmp[oc % 2][:, :], in_=po[oc % 2][:, :],
                                              func=AF.Identity,
                                              bias=bias_t["o"][:, oc:oc + 1]).then_inc(S("p3o_ev"), 1)

                    @blk.vector
                    def _(vector):
                        for oc in range(NIC):
                            vector.wait_ge(S("p3o_ev"), oc + 1)
                            vector.scalar_tensor_tensor(
                                out=x_res[:, oc * TOWN:(oc + 1) * TOWN],
                                in0=atmp[oc % 2][:, :], scalar=gac[:, oc:oc + 1],
                                in1=x_res[:, oc * TOWN:(oc + 1) * TOWN],
                                op0=ALU.mult, op1=ALU.add).then_inc(S("p3o_res"), 1)

        FG = contextlib.ExitStack()
        with FG:
            g_sb = FG.enter_context(nc.sbuf_tensor("g_sb", [128, 43 * TOWN], BF16))
            FH = contextlib.ExitStack()
            with FH:
                h2 = FH.enter_context(nc.sbuf_tensor("h2", [128, NIC * TOWN], BF16))
                LN2 = contextlib.ExitStack()
                with LN2:
                    with nc.Block() as blk2:
                        stg_l2 = emit_ln_stats("l2", TOWN, ("sbuf", x_res, TOWN), LN2, blk2)
                        emit_ln_apply("l2", TOWN, stg_l2, h2, TOWN, g2c,
                                      modv[:, 48:64], LN2, blk2).close()

                GU = contextlib.ExitStack()
                with GU:
                    wgb = [GU.enter_context(nc.sbuf_tensor(f"wgb{i}", [128, 2048], BF16)) for i in range(2)]
                    wub = [GU.enter_context(nc.sbuf_tensor(f"wub{i}", [128, 2048], BF16)) for i in range(2)]
                    sgf = GU.enter_context(nc.sbuf_tensor("sgf", [128, TOWN], F32))
                    pg = [GU.enter_context(nc.psum_tensor(f"pg{i}", [128, TOWN], F32)) for i in range(2)]
                    pu = [GU.enter_context(nc.psum_tensor(f"pu{i}", [128, TOWN], F32)) for i in range(2)]

                    with nc.Block() as blk:
                        @blk.sync
                        def _(sync):
                            sync.wait_ge(S("c_w"), 1)
                            for oc in range(43):
                                if oc >= 2:
                                    sync.wait_ge(S("p5_pe"), oc - 1)
                                if DEBUG_DUMP and oc == 2:
                                    sync.wait_ge(S("q_dbw"), 64)
                                sync.dma_start(out=wgb[oc % 2][:, :],
                                               in_=wout_gate(oc)).then_inc(S("p5_qw"), 16)
                                sync.dma_start(out=wub[oc % 2][:, :],
                                               in_=wout_up(oc)).then_inc(S("p5_qw"), 16)

                        if DEBUG_DUMP:
                            @blk.gpsimd
                            def _(gpsimd):
                                gpsimd.wait_ge(S("p5_qw"), 32)
                                gpsimd.dma_start(out=d_dbg_w[0][:, 0:2048],
                                                 in_=wgb[0][:, :]).then_inc(S("q_dbw"), 16)
                                gpsimd.dma_start(out=d_dbg_w[0][:, 2048:4096],
                                                 in_=wub[0][:, :]).then_inc(S("q_dbw"), 16)
                                gpsimd.dma_start(out=d_dbg_w[1][:, 0:2048],
                                                 in_=wout_gate(0)).then_inc(S("q_dbw"), 16)
                                gpsimd.dma_start(out=d_dbg_w[1][:, 2048:4096],
                                                 in_=wout_up(0)).then_inc(S("q_dbw"), 16)
                                gpsimd.dma_start(out=d_dbg_h2a[:, :],
                                                 in_=h2[:, :]).then_inc(S("q_dbh"), 16)
                                gpsimd.wait_ge(S("p5_si"), 1)
                                gpsimd.dma_start(out=d_dbg_sgf[:, :],
                                                 in_=sgf[:, :]).then_inc(S("q_dbs"), 16)
                                gpsimd.wait_ge(S("p5_pe"), 6)
                                gpsimd.dma_start(out=d_dbg_h2b[:, :],
                                                 in_=h2[:, :]).then_inc(S("q_dbh"), 16)
                                gpsimd.wait_ge(S("q_dbw"), 64)

                        @blk.tensor
                        def _(tensor):
                            tensor.wait_ge(S("p5_qw"), 32)
                            for ic in range(NIC):
                                for j in range(TOWN // 512):
                                    sl = slice(j * 512, (j + 1) * 512)
                                    hsl = h2[:, ic * TOWN + j * 512:ic * TOWN + (j + 1) * 512]
                                    tensor.matmul(pg[0][:, sl],
                                                  wgb[0][:, ic * 128:(ic + 1) * 128], hsl,
                                                  start=(ic == 0), stop=(ic == NIC - 1))
                                    tensor.matmul(pu[0][:, sl],
                                                  wub[0][:, ic * 128:(ic + 1) * 128], hsl,
                                                  start=(ic == 0), stop=(ic == NIC - 1))
                            for oc in range(43):
                                tensor.wait_ge(S("p5_qw"), 32 * (oc + 1))
                                if oc >= 2:
                                    tensor.wait_ge(S("p5_gm"), oc - 1)
                                ins = None
                                for ic in range(NIC):
                                    for j in range(TOWN // 512):
                                        sl = slice(j * 512, (j + 1) * 512)
                                        hsl = h2[:, ic * TOWN + j * 512:ic * TOWN + (j + 1) * 512]
                                        tensor.matmul(pg[oc % 2][:, sl],
                                                      wgb[oc % 2][:, ic * 128:(ic + 1) * 128], hsl,
                                                      start=(ic == 0), stop=(ic == NIC - 1))
                                        ins = tensor.matmul(pu[oc % 2][:, sl],
                                                            wub[oc % 2][:, ic * 128:(ic + 1) * 128], hsl,
                                                            start=(ic == 0), stop=(ic == NIC - 1))
                                ins.then_inc(S("p5_pe"), 1)

                        @blk.scalar
                        def _(scalar):
                            for oc in range(43):
                                scalar.wait_ge(S("p5_pe"), oc + 1)
                                if DEBUG_DUMP and oc == 1:
                                    scalar.wait_ge(S("q_dbs"), 16)
                                if oc >= 2:
                                    scalar.wait_ge(S("p5_gm"), oc - 1)
                                if oc >= 1:
                                    scalar.wait_ge(S("p5_gm"), oc)
                                scalar.activation(out=sgf[:, :], in_=pg[oc % 2][:, :],
                                                  func=AF.Silu).then_inc(S("p5_si"), 1)

                        @blk.vector
                        def _(vector):
                            for oc in range(43):
                                vector.wait_ge(S("p5_si"), oc + 1)
                                vector.tensor_mul(out=g_sb[:, oc * TOWN:(oc + 1) * TOWN],
                                                  in0=sgf[:, :],
                                                  in1=pu[oc % 2][:, :]).then_inc(S("p5_gm"), 1)

            DN = contextlib.ExitStack()
            with DN:
                wdb = [DN.enter_context(nc.sbuf_tensor(f"wdb{i}", [128, 5504], BF16)) for i in range(2)]
                ysb = [DN.enter_context(nc.sbuf_tensor(f"ysb{i}", [128, TOWN], F32)) for i in range(2)]
                pd = [DN.enter_context(nc.psum_tensor(f"pd{i}", [128, TOWN], F32)) for i in range(2)]

                with nc.Block() as blk:
                    @blk.sync
                    def _(sync):
                        for oc in range(NIC):
                            if oc >= 2:
                                sync.wait_ge(S("p6_pe"), oc - 1)
                            sync.dma_start(out=wdb[oc % 2][:, :],
                                           in_=wout_down(oc)).then_inc(S("p6_qw"), 16)

                    @blk.tensor
                    def _(tensor):
                        for oc in range(NIC):
                            tensor.wait_ge(S("p6_qw"), 16 * (oc + 1))
                            if oc >= 2:
                                tensor.wait_ge(S("p6_ye"), oc - 1)
                            ins = None
                            for ic in range(43):
                                for j in range(TOWN // 512):
                                    ins = tensor.matmul(
                                        pd[oc % 2][:, j * 512:(j + 1) * 512],
                                        wdb[oc % 2][:, ic * 128:(ic + 1) * 128],
                                        g_sb[:, ic * TOWN + j * 512:ic * TOWN + (j + 1) * 512],
                                        start=(ic == 0), stop=(ic == 42))
                            ins.then_inc(S("p6_pe"), 1)

                    @blk.vector
                    def _(vector):
                        for oc in range(NIC):
                            vector.wait_ge(S("p6_pe"), oc + 1)
                            if oc >= 2:
                                vector.wait_ge(S("p6_qy"), 16 * (oc - 1))
                            vector.scalar_tensor_tensor(
                                out=ysb[oc % 2][:, :], in0=pd[oc % 2][:, :],
                                scalar=gbc[:, oc:oc + 1],
                                in1=x_res[:, oc * TOWN:(oc + 1) * TOWN],
                                op0=ALU.mult, op1=ALU.add).then_inc(S("p6_ye"), 1)

                    @blk.gpsimd
                    def _(gpsimd):
                        for oc in range(NIC):
                            gpsimd.wait_ge(S("p6_ye"), oc + 1)
                            gpsimd.dma_start(out=d_y[oc],
                                             in_=ysb[oc % 2][:, :]).then_inc(S("p6_qy"), 16)
                        gpsimd.wait_ge(S("p6_qy"), 16 * NIC)
                        if DEBUG_DUMP:
                            gpsimd.dma_start(out=d_dbg_modv[:, :], in_=modv[:, :]).then_inc(S("q_dbg"), 16)
                            gpsimd.dma_start(out=d_dbg_vec[:, 0:16], in_=g1c[:, :]).then_inc(S("q_dbg"), 16)
                            gpsimd.dma_start(out=d_dbg_vec[:, 16:32], in_=g2c[:, :]).then_inc(S("q_dbg"), 16)
                            gpsimd.dma_start(out=d_dbg_vec[:, 32:48], in_=gac[:, :]).then_inc(S("q_dbg"), 16)
                            gpsimd.dma_start(out=d_dbg_vec[:, 48:64], in_=gbc[:, :]).then_inc(S("q_dbg"), 16)
                            for ic in range(NIC):
                                gpsimd.dma_start(out=d_dbg_xres[ic],
                                                 in_=x_res[:, ic * TOWN:(ic + 1) * TOWN]).then_inc(S("q_dbg"), 16)
                            for ocg in range(43):
                                gpsimd.dma_start(out=d_dbg_g[ocg],
                                                 in_=g_sb[:, ocg * TOWN:(ocg + 1) * TOWN]).then_inc(S("q_dbg"), 16)
                            gpsimd.wait_ge(S("q_dbg"), 16 * (5 + NIC + 43))

    for i in nc.all_instructions():
        if isinstance(i, mybir.InstCustomDveAnt) and len(i.instr) == 0:
            mybir.codegen_inst_isa_one(i, nc, nc.isa)
    return nc


def _prep_shared(inputs):
    f32 = np.float32
    sh = {}
    for m, wkey in (("q", "L0_attn_Wq"), ("k", "L0_attn_Wk"),
                    ("v", "L0_attn_Wv"), ("o", "L0_attn_Wo")):
        W = np.asarray(inputs[wkey], dtype=f32)
        sh[f"w4{m}"] = np.ascontiguousarray(
            W.reshape(16, 128, 16, 128).transpose(0, 3, 2, 1).reshape(16, 128, 2048))
    for m, bkey in (("q", "L0_attn_bq"), ("k", "L0_attn_bk"),
                    ("v", "L0_attn_bv"), ("o", "L0_attn_bo")):
        bvec = np.asarray(inputs[bkey], dtype=f32)
        sh[f"b2{m}"] = np.ascontiguousarray(bvec.reshape(16, 128).T)
    sh["ctr"] = np.ascontiguousarray(np.asarray(inputs["L0_c"], dtype=f32).T)
    sh["lutv"] = np.ascontiguousarray(np.asarray(inputs["L0_ffn_lut"], dtype=f32))
    gidx = np.asarray(inputs["L0_ffn_gate_idx"], dtype=np.int32)
    uidx = np.asarray(inputs["L0_ffn_up_idx"], dtype=np.int32)
    didx = np.asarray(inputs["L0_ffn_down_idx"], dtype=np.int32)
    gpad = np.concatenate([gidx, gidx[-128:]], axis=0)
    upad = np.concatenate([uidx, uidx[-128:]], axis=0)
    sh["_gA"] = np.ascontiguousarray(
        gpad.reshape(44, 128, 16, 128).transpose(0, 3, 2, 1).reshape(44, 128, 2048))
    sh["_uA"] = np.ascontiguousarray(
        upad.reshape(44, 128, 16, 128).transpose(0, 3, 2, 1).reshape(44, 128, 2048))
    sh["_dA"] = np.ascontiguousarray(
        didx.reshape(16, 128, 43, 128).transpose(0, 3, 2, 1).reshape(16, 128, 5504))
    sh["wmod_f"] = np.asarray(inputs["L0_wmod"], dtype=f32)
    sh["x_f"] = np.asarray(inputs["L0_attn_h"], dtype=f32)
    return sh


def _in_map_for_core(c, sh):
    b, a = divmod(c, 2)
    blocks = A_BLOCKS if a == 0 else B_BLOCKS
    xT = np.ascontiguousarray(sh["x_f"][b].T)
    xown = np.concatenate([xT[:, blk * 128:(blk + 1) * 128] for blk in blocks], axis=1)
    gsel = [GATE_PAD[c][j] for j in range(6)]
    m = {
        "xown": np.ascontiguousarray(xown.reshape(16, 128, TOWN)),
        "xfull": np.ascontiguousarray(xT.reshape(16, 128, SEQ)),
        "ctr": sh["ctr"],
        "wmods": np.ascontiguousarray(sh["wmod_f"][:, c * 1536:(c + 1) * 1536]),
        "lutv": sh["lutv"],
        "gidx": np.ascontiguousarray(sh["_gA"][gsel]),
        "uidx": np.ascontiguousarray(sh["_uA"][gsel]),
        "didx": np.ascontiguousarray(sh["_dA"][[2 * c, 2 * c + 1]]),
        "scv": np.asarray([blk * 128 for blk in blocks], dtype=np.float32),
        "ebv": np.eye(NB, dtype=np.float32)[b],
    }
    for mm in "qkvo":
        m[f"w4{mm}"] = sh[f"w4{mm}"]
        m[f"b2{mm}"] = sh[f"b2{mm}"]
    return m


_NC_CACHE = {}


def kernel(**inputs):
    sh = _prep_shared(inputs)
    in_maps = [_in_map_for_core(c, sh) for c in range(8)]
    if "nc" not in _NC_CACHE:
        _NC_CACHE["nc"] = build_program()
    nc = _NC_CACHE["nc"]
    res = run_bass_kernel_spmd(nc, in_maps, core_ids=list(range(8)))
    out = np.empty((NB, SEQ, D), dtype=np.float32)
    for c in range(8):
        b, a = divmod(c, 2)
        blocks = A_BLOCKS if a == 0 else B_BLOCKS
        yc = np.asarray(res.results[c]["y"]).reshape(D, TOWN)
        for s, blk in enumerate(blocks):
            out[b, blk * 128:(blk + 1) * 128, :] = yc[:, s * 128:(s + 1) * 128].T
    return out
